# version 15
# baseline (speedup 1.0000x reference)
import numpy as np
from contextlib import ExitStack

import concourse.bass as bass
import concourse.mybir as mybir
from concourse.bass_utils import run_bass_kernel_spmd

F32 = mybir.dt.float32
BF16 = mybir.dt.bfloat16
AF = mybir.ActivationFunctionType
ALU = mybir.AluOpType

REAL_CFG = dict(B=4, S=4096, DM=2048, H=16, KVH=4, DH=8192)


class Sem:
    __slots__ = ("h", "v", "name")

    def __init__(self, h, name):
        self.h = h
        self.v = 0
        self.name = name


class Prog:
    ENGS = ("tensor", "vector", "scalar", "gpsimd", "sync")

    def __init__(self):
        self.q = {e: [] for e in self.ENGS}

    def emit(self, eng, fn):
        self.q[eng].append(fn)

    def run(self, nc):
        with nc.Block() as blk:
            for e in self.ENGS:
                fns = self.q[e]
                if fns:
                    getattr(blk, e)(lambda engine, fns=fns: [f(engine) for f in fns])


class Builder:
    def __init__(self, cfg):
        self.cfg = dict(cfg)
        c = self.cfg
        self.B = c["B"]; self.S = c["S"]; self.DM = c["DM"]
        self.H = c["H"]; self.KVH = c["KVH"]; self.DH = c["DH"]
        assert self.H * 128 == self.DM
        self.DC = self.DM // 128
        self.HC = self.DH // 128
        self.G = self.H // self.KVH
        self.KD = self.KVH * 128
        assert self.KD <= 512
        self.T = self.S // 2
        assert self.T % 512 == 0 and self.S % 512 == 0
        self.NTC = self.T // 512
        self.NSUB = self.S // 128
        self.NSUB_OWN = self.T // 128
        self.NTB = self.DM // 256
        self.NG = self.DM // 512
        assert self.NG >= 1 and self.DM % 512 == 0
        self.NHREG = (self.DM // 2) // 512
        assert self.NHREG == 2
        assert self.DC % 2 == 0
        self.nc = bass.Bass("TRN2", target_bir_lowering=False)
        self.prog = None

    def W(self, eng, sem, v=None):
        if v is None:
            v = sem.v
        if v <= 0:
            return
        self.prog.emit(eng, lambda e, h=sem.h, v=v: e.wait_ge(h, v))

    def OP(self, eng, fn, *incs):
        pairs = []
        for s, n in incs:
            s.v += n
            pairs.append((s.h, n))

        def cl(e, fn=fn, pairs=pairs):
            inst = fn(e)
            for h, n in pairs:
                inst = inst.then_inc(h, n)

        self.prog.emit(eng, cl)

    def declare(self):
        nc = self.nc
        DM, KD, H, HC, DH = self.DM, self.KD, self.H, self.HC, self.DH
        T, NTC, NTB, DC = self.T, self.NTC, self.NTB, self.DC
        D = {}
        D["x_own"] = nc.dram_tensor("x_own", [T, DM], F32, kind="ExternalInput")
        D["x_oth"] = nc.dram_tensor("x_oth", [T, DM], F32, kind="ExternalInput")
        D["wq_pre"] = nc.dram_tensor("wq_pre", [H, 128, DM], F32, kind="ExternalInput")
        D["wo_pre"] = nc.dram_tensor("wo_pre", [NTB, 128, H * 256], F32,
                                     kind="ExternalInput")
        D["up_pre"] = nc.dram_tensor("up_pre", [HC, 128, DM], F32,
                                     kind="ExternalInput")
        D["dn_pre"] = nc.dram_tensor("dn_pre", [HC, 128, DM], F32,
                                     kind="ExternalInput")
        D["wk_cm"] = nc.dram_tensor("wk_cm", [128, DC * KD], F32, kind="ExternalInput")
        D["wv_cm"] = nc.dram_tensor("wv_cm", [128, DC * KD], F32, kind="ExternalInput")
        D["bq_t"] = nc.dram_tensor("bq_t", [128, H], F32, kind="ExternalInput")
        D["bk_r"] = nc.dram_tensor("bk_r", [1, KD], F32, kind="ExternalInput")
        D["bv_r"] = nc.dram_tensor("bv_r", [1, KD], F32, kind="ExternalInput")
        D["us_t"] = nc.dram_tensor("us_t", [128, HC], F32, kind="ExternalInput")
        D["ds_r"] = nc.dram_tensor("ds_r", [1, DM], F32, kind="ExternalInput")
        D["os_r"] = nc.dram_tensor("os_r", [1, 1], F32, kind="ExternalInput")
        D["y"] = nc.dram_tensor("y", [T, DM], F32, kind="ExternalOutput")
        D["upS"] = nc.dram_tensor("upS", [HC, 128, DM], BF16, kind="Internal")
        D["dnS"] = nc.dram_tensor("dnS", [HC, 128, DM], BF16, kind="Internal")
        D["wqS"] = nc.dram_tensor("wqS", [H, 128, DM], BF16, kind="Internal")
        D["woS"] = nc.dram_tensor("woS", [NTB, 128, H * 256], BF16, kind="Internal")
        D["xnS"] = nc.dram_tensor("xnS", [NTC, 4, 128, DM], BF16, kind="Internal")
        self.D = D

    def build(self):
        self.declare()
        nc = self.nc
        with ExitStack() as persist:
            E = persist.enter_context
            P = {}
            P["kvb"] = E(nc.sbuf_tensor("kvb", [128, self.KD], BF16))
            P["ksb"] = E(nc.sbuf_tensor("ksb", [128, self.KVH], BF16))
            P["ident"] = E(nc.sbuf_tensor("ident", [128, 128], BF16))
            P["ones_c"] = E(nc.sbuf_tensor("ones_c", [128, 1], BF16))
            P["ones_r"] = E(nc.sbuf_tensor("ones_r", [1, 128], F32))
            P["ceps5"] = E(nc.sbuf_tensor("ceps5", [128, 1], F32))
            P["ceps6"] = E(nc.sbuf_tensor("ceps6", [1, 1], F32))
            P["bq_t"] = E(nc.sbuf_tensor("bq_tb", [128, self.H], F32))
            P["usc"] = E(nc.sbuf_tensor("uscb", [128, self.HC], F32))
            P["dsb"] = E(nc.sbuf_tensor("dsb", [128, self.DM], F32))
            P["os_bc"] = E(nc.sbuf_tensor("os_bc", [128, 1], F32))
            P["xn_t"] = E(nc.sbuf_tensor("xn_t", [128, 4 * self.DM], BF16))
            P["wq_sl"] = [E(nc.sbuf_tensor(f"wq_sl{i}", [128, self.DM], BF16))
                          for i in range(2)]
            self.P = P
            S = {}
            for name in ("PE", "VE", "SE", "PC", "LDI",
                         "XA0", "XA1", "XA2", "XA3",
                         "PRE0", "PRE1", "SW0", "SW1",
                         "XN0", "WQ0", "WQ1", "WO0", "WO1",
                         "UP0", "UP1", "UP2", "UP3",
                         "DN0", "DN1", "DN2", "DN3",
                         "XR0", "XR1", "XR2", "YW0", "YW1",
                         "XW0", "XW1", "XW2", "XW3"):
                S[name] = Sem(E(nc.semaphore(name)), name)
            self.Sm = S

            with ExitStack() as a_scope:
                self.prog = Prog()
                self.phase_a(a_scope)
                self.prog.run(nc)
            self.a_end = dict(PE=S["PE"].v, VE=S["VE"].v, SE=S["SE"].v,
                              SW0=S["SW0"].v, SW1=S["SW1"].v,
                              XW=[S[f"XW{i}"].v for i in range(4)])

            with ExitStack() as c_scope:
                self.prog = Prog()
                self.phase_c(c_scope)
                self.prog.run(nc)
        return nc

    def phase_a(self, scope):
        nc, D, P, S = self.nc, self.D, self.P, self.Sm
        W, OP = self.W, self.OP
        DC, KD, DM, H, HC, NG = self.DC, self.KD, self.DM, self.H, self.HC, self.NG
        KVH, NSUB, NSUB_OWN, NTB = self.KVH, self.NSUB, self.NSUB_OWN, self.NTB
        Ea = scope.enter_context
        NS = 4
        x_t = [Ea(nc.sbuf_tensor(f"x_t{i}", [128, DM], F32)) for i in range(NS)]
        xn_tm = [Ea(nc.sbuf_tensor(f"xn_tm{i}", [128, DM], BF16)) for i in range(NS)]
        xn_fm = [Ea(nc.sbuf_tensor(f"xn_fm{i}", [128, DM], BF16)) for i in range(NS)]
        wkb = Ea(nc.sbuf_tensor("wkb", [128, DC * KD], BF16))
        wvb = Ea(nc.sbuf_tensor("wvb", [128, DC * KD], BF16))
        bn24 = [Ea(nc.sbuf_tensor(f"bn24_{i}", [128, 6 * NG], F32)) for i in range(NS)]
        mv = [Ea(nc.sbuf_tensor(f"mv{i}", [128, 2], F32)) for i in range(NS)]
        lnv = [Ea(nc.sbuf_tensor(f"lnv{i}", [128, 1], F32)) for i in range(NS)]
        rstd = [Ea(nc.sbuf_tensor(f"rstd{i}", [128, 1], F32)) for i in range(NS)]
        negmr = [Ea(nc.sbuf_tensor(f"negmr{i}", [128, 1], F32)) for i in range(NS)]
        kb = [Ea(nc.sbuf_tensor(f"kb{i}", [128, KD], F32)) for i in range(2)]
        neg = [Ea(nc.sbuf_tensor(f"negb{i}", [128, KD], F32)) for i in range(2)]
        e_t = [Ea(nc.sbuf_tensor(f"e_t{i}", [128, KD], F32)) for i in range(2)]
        sk_t = [Ea(nc.sbuf_tensor(f"sk_t{i}", [128, KD], BF16)) for i in range(2)]
        v_t = [Ea(nc.sbuf_tensor(f"v_t{i}", [128, KD], BF16)) for i in range(2)]
        dsrow = Ea(nc.sbuf_tensor("dsrow", [1, DM], F32))
        bkrow = Ea(nc.sbuf_tensor("bkrow", [1, KD], F32))
        bvrow = Ea(nc.sbuf_tensor("bvrow", [1, KD], F32))
        osrow = Ea(nc.sbuf_tensor("osrow", [1, 1], F32))
        prep_b = [Ea(nc.sbuf_tensor(f"prep_b{i}", [128, max(DM, H * 256)], BF16))
                  for i in range(2)]
        P["bk_bc"] = Ea(nc.sbuf_tensor("bk_bc", [128, KD], F32))
        P["bv_bc"] = Ea(nc.sbuf_tensor("bv_bc", [128, KD], F32))
        ps_kv = Ea(nc.psum_tensor("ps_kv", [128, KD], F32))
        ps_ks = Ea(nc.psum_tensor("ps_ks", [128, KVH], F32))
        ps_tr = [Ea(nc.psum_tensor(f"ps_tr{i}", [128, DM], BF16)) for i in range(2)]
        ps_k = Ea(nc.psum_tensor("ps_k", [128, 512], F32))
        ps_v = Ea(nc.psum_tensor("ps_v", [128, 512], F32))

        n_ldi = 0

        def gload(dst, src):
            nonlocal n_ldi
            OP("gpsimd", lambda e, d=dst, s=src: e.dma_start(d, s), (S["LDI"], 16))
            n_ldi += 1

        gload(P["bq_t"][:], D["bq_t"][:, :])
        gload(P["usc"][:], D["us_t"][:, :])
        gload(dsrow[:], D["ds_r"][:, :])
        gload(bkrow[:], D["bk_r"][:, :])
        gload(bvrow[:], D["bv_r"][:, :])
        gload(osrow[:], D["os_r"][:, :])
        gload(wkb[:], D["wk_cm"][:, :])
        gload(wvb[:], D["wv_cm"][:, :])
        ldi_total = 16 * n_ldi

        OP("gpsimd", lambda e: e.memset(P["ident"][:], 0.0), (S["PC"], 1))
        W("gpsimd", S["PC"])
        OP("gpsimd", lambda e: e.affine_select(
            out=P["ident"][:], in_=P["ident"][:], compare_op=ALU.not_equal,
            fill=1.0, base=0, pattern=[[-1, 128]], channel_multiplier=1),
            (S["PC"], 1))
        pc_ident = S["PC"].v

        OP("vector", lambda e: e.memset(P["ones_c"][:], 1.0), (S["VE"], 1))
        OP("vector", lambda e: e.memset(P["ones_r"][:], 1.0), (S["VE"], 1))
        OP("vector", lambda e: e.memset(P["ceps5"][:], 1e-5), (S["VE"], 1))
        OP("vector", lambda e: e.memset(P["ceps6"][:], 1e-6), (S["VE"], 1))
        ve_init = S["VE"].v

        W("tensor", S["LDI"], ldi_total)
        W("tensor", S["VE"], ve_init)
        OP("tensor", lambda e: e.matmul(ps_k[:, 0:1], P["ones_r"][:], osrow[:],
                                        start=True, stop=True), (S["PE"], 1))
        W("scalar", S["PE"], S["PE"].v)
        OP("scalar", lambda e: e.copy(P["os_bc"][:], ps_k[:, 0:1]), (S["SE"], 1))
        se_os = S["SE"].v
        W("tensor", S["SE"], se_os)
        OP("tensor", lambda e: e.matmul(ps_k[:, 0:KD], P["ones_r"][:], bkrow[:],
                                        start=True, stop=True), (S["PE"], 1))
        W("scalar", S["PE"], S["PE"].v)
        OP("scalar", lambda e: e.copy(P["bk_bc"][:], ps_k[:, 0:KD]), (S["SE"], 1))
        OP("tensor", lambda e: e.matmul(ps_v[:, 0:KD], P["ones_r"][:], bvrow[:],
                                        start=True, stop=True), (S["PE"], 1))
        W("scalar", S["PE"], S["PE"].v)
        OP("scalar", lambda e: e.copy(P["bv_bc"][:], ps_v[:, 0:KD]), (S["SE"], 1))
        for g in range(NG):
            W("tensor", S["SE"], S["SE"].v)
            OP("tensor", lambda e, g=g: e.matmul(
                ps_v[:, :], P["ones_r"][:], dsrow[:, g * 512:(g + 1) * 512],
                start=True, stop=True), (S["PE"], 1))
            W("scalar", S["PE"], S["PE"].v)
            W("scalar", S["SE"], se_os)
            OP("scalar", lambda e, g=g: e.activation(
                P["dsb"][:, g * 512:(g + 1) * 512], ps_v[:, :],
                AF.Identity, scale=P["os_bc"][:]), (S["SE"], 1))
        W("tensor", S["SE"], S["SE"].v)
        se_consts = S["SE"].v

        prep_items = [("wq", f) for f in range(H)] + [("wo", n) for n in range(NTB)]
        pstate = dict(idx=0, out_free=[0, 0])
        pre_cnt = [0, 0]
        self.prep_done = {}

        def do_prep(n):
            for _ in range(n):
                i = pstate["idx"]
                if i >= len(prep_items):
                    return
                pstate["idx"] = i + 1
                kind, j = prep_items[i]
                sl = i % 2
                pre = S[f"PRE{sl}"]
                sw = S[f"SW{sl}"]
                if pstate["out_free"][sl]:
                    W("gpsimd", sw, pstate["out_free"][sl])
                if kind == "wq":
                    src = D["wq_pre"][j, :, :]
                    dst_s = D["wqS"][j, :, :]
                    wid = self.DM
                else:
                    src = D["wo_pre"][j, :, :]
                    dst_s = D["woS"][j, :, :]
                    wid = H * 256
                OP("gpsimd", lambda e, d=prep_b[sl], s=src, w=wid:
                   e.dma_start(d[:, 0:w], s), (pre, 16))
                pre_cnt[sl] += 16
                W("scalar", pre, pre_cnt[sl])
                OP("scalar", lambda e, d=dst_s, s=prep_b[sl], w=wid:
                   e.dma_start(d, s[:, 0:w]), (sw, 16))
                pstate["out_free"][sl] = sw.v
                self.prep_done[i] = sw.v

        rec = dict(se_xn=[0] * NSUB, ve_bn=[0] * NSUB, ve_tre=[0] * NSUB,
                   pe_tr=[0] * NSUB, pe_k=[0] * NSUB, pe_v=[0] * NSUB,
                   ve_kb=[0] * NSUB, ve_sk=[0] * NSUB, ve_v=[0] * NSUB,
                   se_ln=[0] * NSUB, se_rstd=[0] * NSUB, ve_negmr=[0] * NSUB,
                   ve_aggr=[0] * NSUB, pe_kv=[0] * NSUB)
        xa_cnt = [0] * NS
        xw_done = [0] * NSUB
        nblk = DC // 2

        def emit_ln(st):
            own = st < NSUB_OWN
            sl = st % NS
            src = D["x_own"] if own else D["x_oth"]
            row0 = (st if own else st - NSUB_OWN) * 128
            xa = S[f"XA{sl}"]
            if st >= NS:
                W("sync", S["SE"], rec["se_xn"][st - NS])
                W("sync", S["VE"], rec["ve_bn"][st - NS])
            OP("sync", lambda e, d=x_t[sl], s=src, r=row0:
               e.dma_start(d[:], s[r:r + 128, :]), (xa, 16))
            xa_cnt[sl] += 16
            W("vector", xa, xa_cnt[sl])
            if st >= NS:
                W("vector", S["VE"], rec["ve_negmr"][st - NS])
                W("vector", S["SE"], rec["se_ln"][st - NS])
            for g in range(NG):
                OP("vector", lambda e, g=g, x=x_t[sl], b=bn24[sl]: e.bn_stats(
                    b[:, g * 6:(g + 1) * 6], x[:, g * 512:(g + 1) * 512]),
                    (S["VE"], 1))
            W("vector", S["VE"], S["VE"].v)
            OP("vector", lambda e, b=bn24[sl], m=mv[sl]: e.bn_aggr(m[:], b[:]),
               (S["VE"], 1))
            rec["ve_aggr"][st] = S["VE"].v
            rec["ve_bn"][st] = S["VE"].v
            W("scalar", S["VE"], rec["ve_aggr"][st])
            if st >= NS:
                W("scalar", S["VE"], rec["ve_negmr"][st - NS])
            OP("scalar", lambda e, m=mv[sl], l=lnv[sl]: e.activation(
                l[:], m[:, 1:2], AF.Ln, bias=P["ceps5"][:]), (S["SE"], 1))
            rec["se_ln"][st] = S["SE"].v
            W("scalar", S["SE"], S["SE"].v)
            OP("scalar", lambda e, l=lnv[sl], r=rstd[sl]: e.activation(
                r[:], l[:], AF.Exp, scale=-0.5), (S["SE"], 1))
            rec["se_rstd"][st] = S["SE"].v
            W("vector", S["SE"], rec["se_rstd"][st])
            if st >= NS:
                W("vector", S["SE"], rec["se_xn"][st - NS])
            OP("vector", lambda e, m=mv[sl], r=rstd[sl], n=negmr[sl]:
               e.scalar_tensor_tensor(n[:], m[:, 0:1], -1.0, r[:],
                                      ALU.mult, ALU.mult), (S["VE"], 1))
            rec["ve_negmr"][st] = S["VE"].v
            W("scalar", S["VE"], rec["ve_negmr"][st])
            if st >= NS:
                W("scalar", S["PE"], rec["pe_tr"][st - NS])
            OP("scalar", lambda e, x=x_t[sl], o=xn_tm[sl], r=rstd[sl], n=negmr[sl]:
               e.activation(o[:], x[:], AF.Identity, scale=r[:], bias=n[:]),
               (S["SE"], 1))
            rec["se_xn"][st] = S["SE"].v

        def body_tr(st):
            sl = st % NS
            W("tensor", S["SE"], rec["se_xn"][st])
            if st == 0:
                W("tensor", S["PC"], pc_ident)
                W("tensor", S["SE"], se_consts)
            if st >= 2:
                W("tensor", S["VE"], rec["ve_tre"][st - 2])
            for dc in range(DC):
                OP("tensor", lambda e, dc=dc, x=xn_tm[sl]:
                   e.transpose(ps_tr[st % 2][:, dc * 128:(dc + 1) * 128],
                               x[:, dc * 128:(dc + 1) * 128], P["ident"][:]),
                   *(((S["PE"], 1),) if dc == DC - 1 else ()))
            rec["pe_tr"][st] = S["PE"].v

        def body_kv(st):
            own = st < NSUB_OWN
            tau, sub = st // 4, st % 4
            sl = st % NS
            psl = st % 2
            W("vector", S["PE"], rec["pe_tr"][st])
            if st >= NS:
                W("vector", S["PE"], rec["pe_v"][st - NS])
                if st - NS < NSUB_OWN:
                    W("vector", S[f"XW{st % 4}"], xw_done[st - NS])
            OP("vector", lambda e, o=xn_fm[sl]:
               e.tensor_copy(o[:], ps_tr[st % 2][:, :]), (S["VE"], 1))
            rec["ve_tre"][st] = S["VE"].v
            if own:
                W("sync", S["VE"], rec["ve_tre"][st])
                OP("sync", lambda e, t=tau, s2=sub, s=xn_fm[sl]:
                   e.dma_start(D["xnS"][t, s2, :, :], s[:]),
                   (S[f"XW{st % 4}"], 16))
                xw_done[st] = S[f"XW{st % 4}"].v
            W("tensor", S["VE"], rec["ve_tre"][st])
            if st >= 1:
                W("tensor", S["VE"], rec["ve_kb"][st - 1])
            for dc in range(DC):
                OP("tensor", lambda e, dc=dc, x=xn_fm[sl]: e.matmul(
                    ps_k[:, 0:KD], x[:, dc * 128:(dc + 1) * 128],
                    wkb[:, dc * KD:(dc + 1) * KD],
                    start=(dc == 0), stop=(dc == DC - 1)),
                    *(((S["PE"], 1),) if dc == DC - 1 else ()))
            rec["pe_k"][st] = S["PE"].v
            if st >= 1:
                W("tensor", S["VE"], rec["ve_v"][st - 1])
            for dc in range(DC):
                OP("tensor", lambda e, dc=dc, x=xn_fm[sl]: e.matmul(
                    ps_v[:, 0:KD], x[:, dc * 128:(dc + 1) * 128],
                    wvb[:, dc * KD:(dc + 1) * KD],
                    start=(dc == 0), stop=(dc == DC - 1)),
                    *(((S["PE"], 1),) if dc == DC - 1 else ()))
            rec["pe_v"][st] = S["PE"].v
            W("vector", S["PE"], rec["pe_k"][st])
            if st >= 2:
                W("vector", S["VE"], rec["ve_sk"][st - 2])
            OP("vector", lambda e, o=kb[psl]: e.tensor_tensor(
                o[:], ps_k[:, 0:KD], P["bk_bc"][:], ALU.add), (S["VE"], 1))
            rec["ve_kb"][st] = S["VE"].v
            W("vector", S["VE"], S["VE"].v)
            OP("vector", lambda e, i=kb[psl], o=neg[psl]:
               e.tensor_scalar_min(o[:], i[:], 0.0), (S["VE"], 1))
            W("scalar", S["VE"], S["VE"].v)
            if st >= 2:
                W("scalar", S["VE"], rec["ve_sk"][st - 2])
            OP("scalar", lambda e, i=neg[psl], o=e_t[psl]:
               e.activation(o[:], i[:], AF.Exp), (S["SE"], 1))
            W("vector", S["SE"], S["SE"].v)
            if st >= 2:
                W("vector", S["PE"], rec["pe_kv"][st - 1])
            OP("vector", lambda e, o=sk_t[psl], i=kb[psl], x=e_t[psl]:
               e.scalar_tensor_tensor(o[:], i[:], 0.0, x[:], ALU.max, ALU.add),
               (S["VE"], 1))
            rec["ve_sk"][st] = S["VE"].v
            W("vector", S["PE"], rec["pe_v"][st])
            OP("vector", lambda e, o=v_t[psl]: e.tensor_tensor(
                o[:], ps_v[:, 0:KD], P["bv_bc"][:], ALU.add), (S["VE"], 1))
            rec["ve_v"][st] = S["VE"].v
            W("tensor", S["VE"], rec["ve_v"][st])
            for kvh in range(KVH):
                kv_start = (st == 0 and kvh == 0)
                kv_stop = (st == NSUB - 1 and kvh == KVH - 1)
                OP("tensor", lambda e, kvh=kvh, a=sk_t[psl], b=v_t[psl],
                   st_=kv_start, sp_=kv_stop: e.matmul(
                    ps_kv[:, kvh * 128:(kvh + 1) * 128],
                    a[:, kvh * 128:(kvh + 1) * 128],
                    b[:, kvh * 128:(kvh + 1) * 128],
                    start=st_, stop=sp_, skip_group_check=True))
                OP("tensor", lambda e, kvh=kvh, a=sk_t[psl],
                   st_=kv_start, sp_=kv_stop: e.matmul(
                    ps_ks[:, kvh:kvh + 1],
                    a[:, kvh * 128:(kvh + 1) * 128], P["ones_c"][:],
                    start=st_, stop=sp_, skip_group_check=True),
                    *(((S["PE"], 1),) if kvh == KVH - 1 else ()))
            rec["pe_kv"][st] = S["PE"].v

        emit_ln(0)
        emit_ln(1)
        emit_ln(2)
        body_tr(0)
        for st in range(NSUB):
            if st + 1 < NSUB:
                body_tr(st + 1)
            body_kv(st)
            if st + 3 < NSUB:
                emit_ln(st + 3)
            do_prep(1)
        do_prep(len(prep_items))
        W("vector", S["PE"], rec["pe_kv"][NSUB - 1])
        OP("vector", lambda e: e.tensor_copy(P["kvb"][:], ps_kv[:, :]), (S["VE"], 1))
        OP("vector", lambda e: e.tensor_copy(P["ksb"][:], ps_ks[:, :]), (S["VE"], 1))
        for i in range(4):
            W("sync", S[f"XW{i}"], S[f"XW{i}"].v)
        OP("sync", lambda e: e.dma_start(
            P["xn_t"][:].rearrange("p (s d) -> p s d", s=4),
            D["xnS"][0, :, :, :].rearrange("s p d -> p s d")), (S["XN0"], 16))
        W("sync", S["SW0"], S["SW0"].v)
        W("sync", S["SW1"], S["SW1"].v)
        OP("sync", lambda e: e.dma_start(P["wq_sl"][0][:], D["wqS"][0, :, :]),
           (S["WQ0"], 16))
        OP("sync", lambda e: e.dma_start(P["wq_sl"][1][:], D["wqS"][1, :, :]),
           (S["WQ1"], 16))
        self.c_pre = dict(xn0=S["XN0"].v, wq0=S["WQ0"].v, wq1=S["WQ1"].v)

    def phase_c(self, scope):
        nc, D, P, S = self.nc, self.D, self.P, self.Sm
        W, OP = self.W, self.OP
        DC, KD, DM, H, HC = self.DC, self.KD, self.DM, self.H, self.HC
        NTB, KVH, G, NTC, NHREG = self.NTB, self.KVH, self.G, self.NTC, self.NHREG
        Ec = scope.enter_context
        attn_t = Ec(nc.sbuf_tensor("attn_t", [128, H * 512], BF16))
        hidden = Ec(nc.sbuf_tensor("hidden", [128, HC * 512], BF16))
        xn_t = P["xn_t"]
        wq_sl = P["wq_sl"]
        wo_sl = [Ec(nc.sbuf_tensor(f"wo_sl{i}", [128, H * 256], BF16)) for i in range(2)]
        up_sl = [Ec(nc.sbuf_tensor(f"up_sl{i}", [128, DM], BF16)) for i in range(2)]
        dn_sl = [Ec(nc.sbuf_tensor(f"dn_sl{i}", [128, DM], BF16)) for i in range(2)]
        t1_all = Ec(nc.sbuf_tensor("t1_all", [128, 8 * 512], F32))
        pi_view = t1_all[:].bitcast(BF16)
        pi_b = [pi_view[:, i * DM:(i + 1) * DM] for i in range(4)]
        up_pi = pi_b
        dn_pi = pi_b
        sq_sl = [Ec(nc.sbuf_tensor(f"sq_sl{i}", [128, 512], BF16)) for i in range(2)]
        s_tile = Ec(nc.sbuf_tensor("s_tile", [128, 4 * DM], BF16))
        qb = Ec(nc.sbuf_tensor("qb", [128, 512], F32))
        negq = Ec(nc.sbuf_tensor("negq", [128, 512], F32))
        e_q = Ec(nc.sbuf_tensor("e_q", [128, 512], F32))
        r_tmp = Ec(nc.sbuf_tensor("r_tmp", [1, 512], F32))
        r_row = Ec(nc.sbuf_tensor("r_row", [1, 512], F32))
        rbc = Ec(nc.sbuf_tensor("rbc", [128, 512], F32))
        sg = [Ec(nc.sbuf_tensor(f"sg_{i}", [128, 512], F32)) for i in range(2)]
        xr = [Ec(nc.sbuf_tensor(f"xr_{i}", [128, 512], F32)) for i in range(3)]
        yt = [Ec(nc.sbuf_tensor(f"yt_{i}", [128, 512], F32)) for i in range(2)]
        PS = [Ec(nc.psum_tensor(f"psc{i}", [128, 512], F32)) for i in range(8)]
        self.C = dict(PS=PS, t1_all=t1_all, sg=sg, xr=xr, yt=yt, s_tile=s_tile)

        a = self.a_end
        for eng in Prog.ENGS:
            W(eng, S["PE"], a["PE"])
            W(eng, S["VE"], a["VE"])
            W(eng, S["SE"], a["SE"])
        W("sync", S["SW0"], a["SW0"])
        W("sync", S["SW1"], a["SW1"])
        for i in range(4):
            W("sync", S[f"XW{i}"], a["XW"][i])
        W("gpsimd", S["SW0"], a["SW0"])
        W("gpsimd", S["SW1"], a["SW1"])
        W("scalar", S["SW0"], a["SW0"])
        W("scalar", S["SW1"], a["SW1"])

        xn_re = xn_t[:].rearrange("p (s dc t) -> p s dc t", s=4, dc=DC)

        xn_cnt = [0] * NTC
        wq_cnt = [0, 0]
        wo_cnt = [0, 0]
        up_cnt = [0, 0]
        dn_cnt = [0, 0]
        self.q_pe = {}
        mix_pe = {}
        attn_ve = {}
        sq_ve = {}
        qevac_se = {}
        up_pe = {}
        up_sgn = {}
        dn_sgn = {}
        upw_done = {}
        dnw_done = {}
        s_pe = {}
        s_evac_se = {}
        up_evac_se = {}
        dn_jj_pe = {}
        self.dn_evac_ve = [0] * 8
        self.t1_free_se = [0] * 8
        self.sg_free = [0, 0]
        self.xr_free = [0] * 3
        self.yw_done = [0] * 2
        self.last_ph2_ve = 0
        self.last_stile_ve = 0
        self.pending_ph2 = None
        last_dn_pe = 0
        prev_lnr_se = [0]
        prev_rbc_se = [0]
        prev_bc_pe = [0]
        up_store_total = None
        dn_store_total = None
        q_pe = self.q_pe

        def emit_xn_load(tau):
            if tau >= 1:
                W("sync", S["PE"], up_pe[(tau - 1, HC - 1)])
            OP("sync", lambda e, t=tau: e.dma_start(
                xn_t[:].rearrange("p (s d) -> p s d", s=4),
                D["xnS"][t, :, :, :].rearrange("s p d -> p s d")), (S["XN0"], 16))
            xn_cnt[tau] = S["XN0"].v

        def emit_wq_load(tau, f):
            fsl = f % 2
            wq = S[f"WQ{fsl}"]
            prev = q_pe.get((tau, f - 2)) if f >= 2 else (
                q_pe.get((tau - 1, H - 2 + f)) if tau >= 1 else None)
            if prev:
                W("sync", S["PE"], prev)
            OP("sync", lambda e, d=wq_sl[fsl], f=f: e.dma_start(
                d[:], D["wqS"][f, :, :]), (wq, 16))
            wq_cnt[fsl] = wq.v

        for tau in range(NTC):
            if tau == 0:
                xn_cnt[0] = self.c_pre["xn0"]
                wq_cnt[0] = self.c_pre["wq0"]
                wq_cnt[1] = self.c_pre["wq1"]

            def emit_q(f):
                fsl = f % 2
                pq = PS[f % 2]
                W("tensor", S[f"WQ{fsl}"], wq_cnt[fsl])
                if f == 0:
                    W("tensor", S["XN0"], xn_cnt[tau])
                    if tau >= 1:
                        W("tensor", S["VE"], self.dn_evac_ve[0])
                if f == 1 and tau >= 1:
                    W("tensor", S["VE"], self.dn_evac_ve[1])
                pe_war = qevac_se.get((tau, f - 2)) if f >= 2 else (
                    qevac_se.get((tau - 1, H - 2 + f)) if tau >= 1 else None)
                if pe_war:
                    W("tensor", S["SE"], pe_war)
                for dc in range(DC):
                    OP("tensor", lambda e, dc=dc, w=wq_sl[fsl], p=pq: e.matmul(
                        p[:, :], w[:, dc * 128:(dc + 1) * 128],
                        xn_re[:, :, dc, :], start=(dc == 0), stop=(dc == DC - 1)),
                        *(((S["PE"], 1),) if dc == DC - 1 else ()))
                q_pe[(tau, f)] = S["PE"].v
                W("scalar", S["PE"], q_pe[(tau, f)])
                prevsq = sq_ve.get((tau, f - 1)) if f >= 1 else (
                    sq_ve.get((tau - 1, H - 1)) if tau >= 1 else None)
                if prevsq:
                    W("scalar", S["VE"], prevsq)
                OP("scalar", lambda e, p=pq, f=f: e.activation(
                    qb[:], p[:, :], AF.Identity, bias=P["bq_t"][:, f:f + 1]),
                    (S["SE"], 1))
                qevac_se[(tau, f)] = S["SE"].v
                W("vector", S["SE"], S["SE"].v)
                OP("vector", lambda e: e.tensor_scalar_min(negq[:], qb[:], 0.0),
                   (S["VE"], 1))
                W("scalar", S["VE"], S["VE"].v)
                OP("scalar", lambda e: e.activation(e_q[:], negq[:], AF.Exp),
                   (S["SE"], 1))
                W("vector", S["SE"], S["SE"].v)
                prevmix = mix_pe.get((tau, f - 2)) if f >= 2 else (
                    mix_pe.get((tau - 1, H - 2 + f)) if tau >= 1 else None)
                if prevmix:
                    W("vector", S["PE"], prevmix)
                OP("vector", lambda e, o=sq_sl[fsl]: e.scalar_tensor_tensor(
                    o[:], qb[:], 0.0, e_q[:], ALU.max, ALU.add), (S["VE"], 1))
                sq_ve[(tau, f)] = S["VE"].v

            def emit_attn_tail(f):
                fsl = f % 2
                kvh = f // G
                pm = PS[2 + (f % 2)]
                W("tensor", S["VE"], sq_ve[(tau, f)])
                if f < 2 and tau >= 1:
                    W("tensor", S["VE"], self.dn_evac_ve[2 + (f % 2)])
                prevattn = attn_ve.get((tau, f - 2)) if f >= 2 else (
                    attn_ve.get((tau - 1, H - 2 + f)) if tau >= 1 else None)
                if prevattn:
                    W("tensor", S["VE"], prevattn)
                OP("tensor", lambda e, kvh=kvh, q=sq_sl[fsl], p=pm: e.matmul(
                    p[:, :], P["kvb"][:, kvh * 128:(kvh + 1) * 128], q[:],
                    start=True, stop=True))
                if prev_lnr_se[0]:
                    W("tensor", S["SE"], prev_lnr_se[0])
                if f == 0 and tau >= 1:
                    W("tensor", S["VE"], self.dn_evac_ve[4])
                OP("tensor", lambda e, kvh=kvh, q=sq_sl[fsl]: e.matmul(
                    PS[4][0:1, :], P["ksb"][:, kvh:kvh + 1], q[:],
                    start=True, stop=True), (S["PE"], 1))
                mix_pe[(tau, f)] = S["PE"].v
                W("scalar", S["PE"], mix_pe[(tau, f)])
                if prev_bc_pe[0]:
                    W("scalar", S["PE"], prev_bc_pe[0])
                OP("scalar", lambda e: e.activation(r_tmp[:], PS[4][0:1, :],
                                                    AF.Ln, bias=P["ceps6"][:]),
                   (S["SE"], 1))
                W("scalar", S["SE"], S["SE"].v)
                OP("scalar", lambda e: e.activation(r_row[:], r_tmp[:], AF.Exp,
                                                    scale=-1.0), (S["SE"], 1))
                prev_lnr_se[0] = S["SE"].v
                W("tensor", S["SE"], prev_lnr_se[0])
                if prev_rbc_se[0]:
                    W("tensor", S["SE"], prev_rbc_se[0])
                if f == 0 and tau >= 1:
                    W("tensor", S["VE"], self.dn_evac_ve[5])
                OP("tensor", lambda e: e.matmul(PS[5][:, :], P["ones_r"][:],
                                                r_row[:], start=True, stop=True),
                   (S["PE"], 1))
                prev_bc_pe[0] = S["PE"].v
                W("scalar", S["PE"], prev_bc_pe[0])
                prevav = attn_ve.get((tau, f - 1)) if f >= 1 else (
                    attn_ve.get((tau - 1, H - 1)) if tau >= 1 else None)
                if prevav:
                    W("scalar", S["VE"], prevav)
                OP("scalar", lambda e: e.copy(rbc[:], PS[5][:, :]), (S["SE"], 1))
                prev_rbc_se[0] = S["SE"].v
                W("vector", S["SE"], prev_rbc_se[0])
                if f == 0 and tau >= 1:
                    W("vector", S["PE"], s_pe[(tau - 1, "end")])
                OP("vector", lambda e, f=f, p=pm: e.tensor_tensor(
                    attn_t[:, f * 512:(f + 1) * 512], p[:, :], rbc[:], ALU.mult),
                    (S["VE"], 1))
                attn_ve[(tau, f)] = S["VE"].v

            for f in range(H):
                if f == 2 and self.pending_ph2 is not None:
                    self.pending_ph2[0]()
                    self.pending_ph2b = self.pending_ph2[1]
                    self.pending_ph2 = None
                emit_q(f)
                if f + 2 < H:
                    emit_wq_load(tau, f + 2)
                if f >= 1:
                    emit_attn_tail(f - 1)
            emit_attn_tail(H - 1)
            attn_done_ve = S["VE"].v

            wo_done = {}

            def emit_wo_load(ntb):
                osl = ntb % 2
                wo = S[f"WO{osl}"]
                prevs = s_pe.get((tau, ntb - 2, 3)) if ntb >= 2 else (
                    s_pe.get((tau - 1, NTB - 2 + ntb, 3)) if tau >= 1 else None)
                if prevs:
                    W("gpsimd", S["PE"], prevs)
                OP("gpsimd", lambda e, d=wo_sl[osl], n=ntb: e.dma_start(
                    d[:], D["woS"][n, :, :]), (wo, 16))
                wo_done[ntb] = wo.v

            emit_wo_load(0)
            emit_wo_load(1)
            if getattr(self, "pending_ph2b", None) is not None:
                self.pending_ph2b()
                self.pending_ph2b = None
            for ntb in range(NTB):
                osl = ntb % 2
                W("tensor", S[f"WO{osl}"], wo_done[ntb])
                if ntb == 0:
                    W("tensor", S["VE"], attn_done_ve)
                    if tau >= 1:
                        W("tensor", S["VE"], self.dn_evac_ve[6])
                        W("tensor", S["VE"], self.dn_evac_ve[7])
                for sub in range(4):
                    gidx = ntb * 4 + sub
                    psl = PS[6 + (gidx % 2)]
                    war = s_evac_se.get((tau, gidx - 2)) if gidx >= 2 else (
                        s_evac_se.get((tau - 1, NTB * 4 - 2 + gidx)) if tau >= 1
                        else None)
                    if war:
                        W("tensor", S["SE"], war)
                    for h in range(H):
                        OP("tensor", lambda e, h=h, sub=sub, p=psl, w=wo_sl[osl]:
                           e.matmul(p[:, 0:256],
                                    attn_t[:, h * 512 + sub * 128:
                                           h * 512 + (sub + 1) * 128],
                                    w[:, h * 256:(h + 1) * 256],
                                    start=(h == 0), stop=(h == H - 1)),
                            *(((S["PE"], 1),) if h == H - 1 else ()))
                    s_pe[(tau, ntb, sub)] = S["PE"].v
                    W("scalar", S["PE"], S["PE"].v)
                    if gidx == 0 and tau >= 1:
                        W("scalar", S["VE"], self.last_stile_ve)
                    OP("scalar", lambda e, sub=sub, n=ntb, p=psl: e.copy(
                        s_tile[:, sub * DM + n * 256: sub * DM + (n + 1) * 256],
                        p[:, 0:256]), (S["SE"], 1))
                    s_evac_se[(tau, gidx)] = S["SE"].v
                if ntb + 2 < NTB:
                    emit_wo_load(ntb + 2)
            s_pe[(tau, "end")] = S["PE"].v

            if tau == 1:
                W("gpsimd", S["SW0"], up_store_total[0])
                W("gpsimd", S["SW1"], up_store_total[1])
            for j in range(HC):
                usl = j % 2
                up = S[f"UP{usl}"]
                pu = PS[j % 2]
                prev = up_pe.get((tau, j - 2)) if j >= 2 else (
                    up_pe.get((tau - 1, HC - 2 + j)) if tau >= 1 else None)
                if tau == 0:
                    p4 = j % 4
                    up = S[f"UP{p4}"]
                    if j >= 4:
                        sem, v = up_sgn[j - 4]
                        W("gpsimd", sem, v)
                    OP("gpsimd", lambda e, d=up_pi[p4], j=j: e.dma_start(
                        d[:], D["up_pre"][j, :, :]), (up, 16))
                    ld_done = up.v
                    eng = "scalar" if (j % 2 == 0) else "vector"
                    W(eng, up, ld_done)
                    if prev:
                        W(eng, S["PE"], prev)
                    if j >= 2:
                        W(eng, S[f"SW{usl}"], upw_done[j - 2])
                    if eng == "scalar":
                        OP("scalar", lambda e, ap=up_sl[usl], b=up_pi[p4]:
                           e.sign(ap[:], b[:]), (S["SE"], 1))
                        up_sgn[j] = (S["SE"], S["SE"].v)
                    else:
                        OP("vector", lambda e, ap=up_sl[usl], b=up_pi[p4]:
                           e.tensor_scalar(ap[:], b[:], 0.0, None, ALU.is_gt),
                           (S["VE"], 1))
                        W("vector", S["VE"], S["VE"].v)
                        OP("vector", lambda e, ap=up_sl[usl]:
                           e.tensor_scalar(ap[:], ap[:], 2.0, -1.0, ALU.mult,
                                           ALU.add), (S["VE"], 1))
                        up_sgn[j] = (S["VE"], S["VE"].v)
                    sem, v = up_sgn[j]
                    W("scalar", sem, v)
                    OP("scalar", lambda e, j=j, s=up_sl[usl]: e.dma_start(
                        D["upS"][j, :, :], s[:]), (S[f"SW{usl}"], 16))
                    upw_done[j] = S[f"SW{usl}"].v
                    W("tensor", sem, v)
                else:
                    if prev:
                        W("gpsimd", S["PE"], prev)
                    OP("gpsimd", lambda e, d=up_sl[usl], j=j: e.dma_start(
                        d[:], D["upS"][j, :, :]), (up, 16))
                    up_cnt[usl] = up.v
                    W("tensor", up, up_cnt[usl])
                if j < 2:
                    W("tensor", S["SE"], qevac_se[(tau, H - 2 + j)])
                    if tau >= 1:
                        W("tensor", S["PE"], last_dn_pe)
                else:
                    W("tensor", S["SE"], up_evac_se[(tau, j - 2)])
                for dc in range(DC):
                    OP("tensor", lambda e, dc=dc, w=up_sl[usl], p=pu: e.matmul(
                        p[:, :], w[:, dc * 128:(dc + 1) * 128],
                        xn_re[:, :, dc, :], start=(dc == 0), stop=(dc == DC - 1)),
                        *(((S["PE"], 1),) if dc == DC - 1 else ()))
                up_pe[(tau, j)] = S["PE"].v
                W("scalar", S["PE"], S["PE"].v)
                OP("scalar", lambda e, j=j, p=pu: e.activation(
                    hidden[:, j * 512:(j + 1) * 512], p[:, :], AF.Relu,
                    scale=P["usc"][:, j:j + 1]), (S["SE"], 1))
                up_evac_se[(tau, j)] = S["SE"].v
            up_evac_last = S["SE"].v
            if tau == 0:
                up_store_total = (S["SW0"].v, S["SW1"].v)

            if tau == 0:
                for j in range(HC):
                    dsl = j % 2
                    p4 = j % 4
                    dnm = S[f"DN{p4}"]
                    if j >= 4:
                        sem, v = dn_sgn[j - 4]
                        W("gpsimd", sem, v)
                    else:
                        sem, v = up_sgn[HC - 4 + j]
                        W("gpsimd", sem, v)
                    OP("gpsimd", lambda e, d=dn_pi[p4], j=j: e.dma_start(
                        d[:], D["dn_pre"][j, :, :]), (dnm, 16))
                    ld_done = dnm.v
                    eng = "scalar" if (j % 2 == 0) else "vector"
                    W(eng, dnm, ld_done)
                    if j >= 2:
                        W(eng, S["PE"], dn_jj_pe[(0, 0, j - 2)])
                        W(eng, S[f"SW{dsl}"], dnw_done[j - 2])
                    if eng == "scalar":
                        OP("scalar", lambda e, ap=dn_sl[dsl], b=dn_pi[p4]:
                           e.sign(ap[:], b[:]), (S["SE"], 1))
                        dn_sgn[j] = (S["SE"], S["SE"].v)
                    else:
                        OP("vector", lambda e, ap=dn_sl[dsl], b=dn_pi[p4]:
                           e.tensor_scalar(ap[:], b[:], 0.0, None, ALU.is_gt),
                           (S["VE"], 1))
                        W("vector", S["VE"], S["VE"].v)
                        OP("vector", lambda e, ap=dn_sl[dsl]:
                           e.tensor_scalar(ap[:], ap[:], 2.0, -1.0, ALU.mult,
                                           ALU.add), (S["VE"], 1))
                        dn_sgn[j] = (S["VE"], S["VE"].v)
                    sem, v = dn_sgn[j]
                    W("scalar", sem, v)
                    OP("scalar", lambda e, j=j, s=dn_sl[dsl]: e.dma_start(
                        D["dnS"][j, :, :], s[:]), (S[f"SW{dsl}"], 16))
                    dnw_done[j] = S[f"SW{dsl}"].v
                    W("tensor", sem, v)
                    W("tensor", S["SE"], up_evac_se[(0, j)])
                    for sub in range(4):
                        for ntd in range(NHREG):
                            bank = sub * NHREG + ntd
                            islast = (sub == 3 and ntd == NHREG - 1)
                            OP("tensor", lambda e, j=j, sub=sub, ntd=ntd, b=bank,
                               d=dn_sl[dsl]: e.matmul(
                                PS[b][:, :],
                                hidden[:, j * 512 + sub * 128:
                                       j * 512 + (sub + 1) * 128],
                                d[:, ntd * 512:(ntd + 1) * 512],
                                start=(j == 0), stop=(j == HC - 1)),
                                *(((S["PE"], 1),) if islast else ()))
                    dn_jj_pe[(0, 0, j)] = S["PE"].v
                dn_done = S["PE"].v
                last_dn_pe = dn_done
                dn_store_total = (S["SW0"].v, S["SW1"].v)
                emit_xn_load(1)
                emit_wq_load(1, 0)
                emit_wq_load(1, 1)
                pha, phb = self._dn_evac(0, 0, dn_done)
                pha()
                phb()
                for jj in range(HC // 2):
                    dsl = jj % 2
                    dnm = S[f"DN{dsl}"]
                    if jj == 0:
                        W("sync", S["SW0"], dn_store_total[0])
                        W("sync", S["SW1"], dn_store_total[1])
                    war = (dn_jj_pe.get((0, 1, jj - 2)) if jj >= 2
                           else dn_jj_pe[(0, 0, HC - 2 + jj)])
                    if war:
                        W("sync", S["PE"], war)
                    OP("sync", lambda e, d=dn_sl[dsl], jj=jj: e.dma_start(
                        d[:].rearrange("p (j c) -> p j c", j=2),
                        D["dnS"][2 * jj:2 * jj + 2, :, DM // 2:DM].rearrange(
                            "j p c -> p j c")), (dnm, 16))
                    dn_cnt[dsl] = dnm.v
                    W("tensor", dnm, dn_cnt[dsl])
                    if jj == 0:
                        for b in range(8):
                            W("tensor", S["VE"], self.dn_evac_ve[b])
                    for j2 in range(2):
                        j = 2 * jj + j2
                        for sub in range(4):
                            for ntd in range(NHREG):
                                bank = sub * NHREG + ntd
                                islast = (j2 == 1 and sub == 3
                                          and ntd == NHREG - 1)
                                OP("tensor", lambda e, j=j, j2=j2, sub=sub,
                                   ntd=ntd, b=bank, d=dn_sl[dsl]: e.matmul(
                                    PS[b][:, :],
                                    hidden[:, j * 512 + sub * 128:
                                           j * 512 + (sub + 1) * 128],
                                    d[:, j2 * 1024 + ntd * 512:
                                      j2 * 1024 + (ntd + 1) * 512],
                                    start=(j == 0), stop=(j == HC - 1)),
                                    *(((S["PE"], 1),) if islast else ()))
                    dn_jj_pe[(0, 1, jj)] = S["PE"].v
                dn_done = S["PE"].v
                last_dn_pe = dn_done
                self.pending_ph2 = self._dn_evac(0, 1, dn_done)
            else:
                for dhalf in range(2):
                    for jj in range(HC // 2):
                        dsl = jj % 2
                        dnm = S[f"DN{dsl}"]
                        war = dn_jj_pe.get((tau, dhalf, jj - 2)) if jj >= 2 else (
                            dn_jj_pe.get((tau, 0, HC // 2 - 2 + jj))
                            if dhalf == 1 else
                            (dn_jj_pe.get((tau - 1, 1, HC // 2 - 2 + jj))
                             if tau >= 2 else
                             dn_jj_pe.get((0, 1, HC // 2 - 2 + jj))))
                        if war:
                            W("sync", S["PE"], war)
                        OP("sync", lambda e, d=dn_sl[dsl], dh=dhalf, jj=jj:
                           e.dma_start(
                            d[:].rearrange("p (j c) -> p j c", j=2),
                            D["dnS"][2 * jj:2 * jj + 2, :,
                                     dh * (DM // 2):(dh + 1) * (DM // 2)]
                            .rearrange("j p c -> p j c")), (dnm, 16))
                        dn_cnt[dsl] = dnm.v
                        W("tensor", dnm, dn_cnt[dsl])
                        if jj == 0:
                            W("tensor", S["SE"], up_evac_last)
                            for b in range(8):
                                if self.dn_evac_ve[b]:
                                    W("tensor", S["VE"], self.dn_evac_ve[b])
                        for j2 in range(2):
                            j = 2 * jj + j2
                            for sub in range(4):
                                for ntd in range(NHREG):
                                    bank = sub * NHREG + ntd
                                    islast = (j2 == 1 and sub == 3
                                              and ntd == NHREG - 1)
                                    OP("tensor", lambda e, j=j, j2=j2, sub=sub,
                                       ntd=ntd, b=bank, d=dn_sl[dsl]: e.matmul(
                                        PS[b][:, :],
                                        hidden[:, j * 512 + sub * 128:
                                               j * 512 + (sub + 1) * 128],
                                        d[:, j2 * 1024 + ntd * 512:
                                          j2 * 1024 + (ntd + 1) * 512],
                                        start=(j == 0), stop=(j == HC - 1)),
                                        *(((S["PE"], 1),) if islast else ()))
                        dn_jj_pe[(tau, dhalf, jj)] = S["PE"].v
                    dn_done = S["PE"].v
                    last_dn_pe = dn_done
                    if dhalf == 0 and tau + 1 < NTC:
                        emit_xn_load(tau + 1)
                        emit_wq_load(tau + 1, 0)
                        emit_wq_load(tau + 1, 1)
                    pha, phb = self._dn_evac(tau, dhalf, dn_done)
                    if dhalf == 0:
                        pha()
                        phb()
                    else:
                        self.pending_ph2 = (pha, phb)
        if self.pending_ph2 is not None:
            self.pending_ph2[0]()
            self.pending_ph2[1]()
            self.pending_ph2 = None
        W("sync", S["YW0"], S["YW0"].v)
        W("sync", S["YW1"], S["YW1"].v)
        W("gpsimd", S["YW0"], S["YW0"].v)
        W("gpsimd", S["YW1"], S["YW1"].v)

    def _dn_evac(self, tau, dhalf, dn_done):
        S, P, D = self.Sm, self.P, self.D
        W, OP = self.W, self.OP
        DM, NHREG = self.DM, self.NHREG
        C = self.C
        PS, t1_all, sg, xr, yt, s_tile = (C["PS"], C["t1_all"], C["sg"],
                                          C["xr"], C["yt"], C["s_tile"])
        ev_order = [0, 2, 4, 5, 1, 3, 6, 7]
        NXR = len(xr)
        inst = []
        for i, bank in enumerate(ev_order):
            sub, ntd = bank // NHREG, bank % NHREG
            inst.append(dict(bank=bank, sub=sub, slot=i % NXR,
                             dcol=dhalf * (DM // 2) + ntd * 512,
                             row0=tau * 512 + sub * 128, xr_done=0))

        def emit_xr_load(i):
            r = inst[i]
            if self.xr_free[r["slot"]]:
                W("scalar", S["VE"], self.xr_free[r["slot"]])
            OP("scalar", lambda e, r0=r["row0"], c=r["dcol"], sl=r["slot"]:
               e.dma_start(xr[sl][:], D["x_own"][r0:r0 + 128, c:c + 512]),
               (S[f"XR{r['slot']}"], 16))
            r["xr_done"] = S[f"XR{r['slot']}"].v

        for i in range(NXR):
            emit_xr_load(i)
        W("vector", S["PE"], dn_done)
        if self.last_ph2_ve:
            W("vector", S["VE"], self.last_ph2_ve)
        for r in inst:
            b = r["bank"]
            if self.t1_free_se[b]:
                W("vector", S["SE"], self.t1_free_se[b])
            OP("vector", lambda e, b=b, dcol=r["dcol"]:
               e.tensor_tensor(t1_all[:, b * 512:(b + 1) * 512], PS[b][:, :],
                               P["dsb"][:, dcol:dcol + 512], ALU.mult),
               (S["VE"], 1))
            self.dn_evac_ve[b] = S["VE"].v

        def phase2a():
            for i, r in enumerate(inst):
                b, par = r["bank"], i % 2
                W("scalar", S["VE"], self.dn_evac_ve[b])
                if self.sg_free[par]:
                    W("scalar", S["VE"], self.sg_free[par])
                OP("scalar", lambda e, b=b, par=par: e.activation(
                    sg[par][:], t1_all[:, b * 512:(b + 1) * 512], AF.Silu),
                    (S["SE"], 1))
                self.t1_free_se[b] = S["SE"].v
                W("vector", S["SE"], S["SE"].v)
                OP("vector", lambda e, b=b, par=par, sub=r["sub"], dcol=r["dcol"]:
                   e.tensor_tensor(t1_all[:, b * 512:(b + 1) * 512], sg[par][:],
                                   s_tile[:, sub * DM + dcol:
                                          sub * DM + dcol + 512], ALU.mult),
                   (S["VE"], 1))
                self.sg_free[par] = S["VE"].v
                self.last_stile_ve = S["VE"].v

        def phase2b():
            for i, r in enumerate(inst):
                b, p4 = r["bank"], i % 2
                W("vector", S[f"XR{r['slot']}"], r["xr_done"])
                if self.yw_done[p4]:
                    W("vector", S[f"YW{p4}"], self.yw_done[p4])
                OP("vector", lambda e, b=b, p4=p4, sl=r["slot"]: e.tensor_tensor(
                    yt[p4][:], t1_all[:, b * 512:(b + 1) * 512], xr[sl][:],
                    ALU.add), (S["VE"], 1))
                self.xr_free[r["slot"]] = S["VE"].v
                self.last_ph2_ve = S["VE"].v
                W("gpsimd", S["VE"], S["VE"].v)
                OP("gpsimd", lambda e, r0=r["row0"], c=r["dcol"], p4=p4:
                   e.dma_start(D["y"][r0:r0 + 128, c:c + 512], yt[p4][:]),
                   (S[f"YW{p4}"], 16))
                self.yw_done[p4] = S[f"YW{p4}"].v
                if i + NXR < 8:
                    emit_xr_load(i + NXR)
        return phase2a, phase2b


def _host_layout(inputs, cfg):
    B, Sq, DM, H, KVH, DH = (cfg["B"], cfg["S"], cfg["DM"], cfg["H"],
                             cfg["KVH"], cfg["DH"])
    KD = KVH * 128
    HC = DH // 128
    DC = DM // 128
    NTB = DM // 256
    x = np.asarray(inputs["L0_attn_hidden_states"])
    Wq = np.asarray(inputs["L0_attn_Wq"])
    Wo = np.asarray(inputs["L0_attn_Wo"])
    up = np.asarray(inputs["L0_ffn_up_weight"])
    dn = np.asarray(inputs["L0_ffn_down_weight"])
    shared = dict(
        wq_pre=np.ascontiguousarray(
            Wq.reshape(H, 128, DC, 128).transpose(0, 3, 2, 1)
            .reshape(H, 128, DM)),
        wo_pre=np.ascontiguousarray(
            Wo.reshape(NTB, 256, H, 128).transpose(0, 3, 2, 1)
            .reshape(NTB, 128, H * 256)),
        up_pre=np.ascontiguousarray(
            up.reshape(HC, 128, DC, 128).transpose(0, 3, 2, 1)
            .reshape(HC, 128, DM)),
        dn_pre=np.ascontiguousarray(dn.T).reshape(HC, 128, DM),
        wk_cm=np.ascontiguousarray(
            np.asarray(inputs["L0_attn_Wk"]).T.reshape(DC, 128, KD)
            .transpose(1, 0, 2).reshape(128, DC * KD)),
        wv_cm=np.ascontiguousarray(
            np.asarray(inputs["L0_attn_Wv"]).T.reshape(DC, 128, KD)
            .transpose(1, 0, 2).reshape(128, DC * KD)),
        bq_t=np.ascontiguousarray(np.asarray(inputs["L0_attn_bq"]).reshape(H, 128).T),
        bk_r=np.ascontiguousarray(np.asarray(inputs["L0_attn_bk"]).reshape(1, KD)),
        bv_r=np.ascontiguousarray(np.asarray(inputs["L0_attn_bv"]).reshape(1, KD)),
        us_t=np.ascontiguousarray(
            np.asarray(inputs["L0_ffn_up_scale"]).reshape(HC, 128).T),
        ds_r=np.ascontiguousarray(
            np.asarray(inputs["L0_ffn_down_scale"]).reshape(1, DM)),
        os_r=np.ascontiguousarray(
            np.asarray(inputs["L0_ffn_output_scale"]).reshape(1, 1)),
    )
    T = Sq // 2
    in_maps = []
    for c in range(2 * B):
        b, h = c // 2, c % 2
        m = dict(shared)
        m["x_own"] = np.ascontiguousarray(x[b, h * T:(h + 1) * T])
        m["x_oth"] = np.ascontiguousarray(x[b, (1 - h) * T:(2 - h) * T])
        in_maps.append(m)
    return in_maps


def kernel(**inputs):
    cfg = REAL_CFG
    in_maps = _host_layout(inputs, cfg)
    bld = Builder(cfg)
    nc = bld.build()
    res = run_bass_kernel_spmd(nc, in_maps, core_ids=list(range(2 * cfg["B"])))
    B, Sq, DM = cfg["B"], cfg["S"], cfg["DM"]
    T = Sq // 2
    y = np.empty((B, Sq, DM), dtype=np.float32)
    for c in range(2 * B):
        b, h = c // 2, c % 2
        y[b, h * T:(h + 1) * T] = res.results[c]["y"]
    return y


# version 16
# speedup vs baseline: 1.0137x; 1.0137x over previous
import numpy as np
from contextlib import ExitStack

import concourse.bass as bass
import concourse.mybir as mybir
from concourse.bass_utils import run_bass_kernel_spmd

F32 = mybir.dt.float32
BF16 = mybir.dt.bfloat16
AF = mybir.ActivationFunctionType
ALU = mybir.AluOpType

REAL_CFG = dict(B=4, S=4096, DM=2048, H=16, KVH=4, DH=8192)


class Sem:
    __slots__ = ("h", "v", "name")

    def __init__(self, h, name):
        self.h = h
        self.v = 0
        self.name = name


class Prog:
    ENGS = ("tensor", "vector", "scalar", "gpsimd", "sync")

    def __init__(self):
        self.q = {e: [] for e in self.ENGS}

    def emit(self, eng, fn):
        self.q[eng].append(fn)

    def run(self, nc):
        with nc.Block() as blk:
            for e in self.ENGS:
                fns = self.q[e]
                if fns:
                    getattr(blk, e)(lambda engine, fns=fns: [f(engine) for f in fns])


class Builder:
    def __init__(self, cfg):
        self.cfg = dict(cfg)
        c = self.cfg
        self.B = c["B"]; self.S = c["S"]; self.DM = c["DM"]
        self.H = c["H"]; self.KVH = c["KVH"]; self.DH = c["DH"]
        assert self.H * 128 == self.DM
        self.DC = self.DM // 128
        self.HC = self.DH // 128
        self.G = self.H // self.KVH
        self.KD = self.KVH * 128
        assert self.KD <= 512
        self.T = self.S // 2
        assert self.T % 512 == 0 and self.S % 512 == 0
        self.NTC = self.T // 512
        self.NSUB = self.S // 128
        self.NSUB_OWN = self.T // 128
        self.NTB = self.DM // 256
        self.NG = self.DM // 512
        assert self.NG >= 1 and self.DM % 512 == 0
        self.NHREG = (self.DM // 2) // 512
        assert self.NHREG == 2
        assert self.DC % 2 == 0
        self.nc = bass.Bass("TRN2", target_bir_lowering=False)
        self.prog = None

    def W(self, eng, sem, v=None):
        if v is None:
            v = sem.v
        if v <= 0:
            return
        self.prog.emit(eng, lambda e, h=sem.h, v=v: e.wait_ge(h, v))

    def OP(self, eng, fn, *incs):
        pairs = []
        for s, n in incs:
            s.v += n
            pairs.append((s.h, n))

        def cl(e, fn=fn, pairs=pairs):
            inst = fn(e)
            for h, n in pairs:
                inst = inst.then_inc(h, n)

        self.prog.emit(eng, cl)

    def declare(self):
        nc = self.nc
        DM, KD, H, HC, DH = self.DM, self.KD, self.H, self.HC, self.DH
        T, NTC, NTB, DC = self.T, self.NTC, self.NTB, self.DC
        D = {}
        D["x_own"] = nc.dram_tensor("x_own", [T, DM], F32, kind="ExternalInput")
        D["x_oth"] = nc.dram_tensor("x_oth", [T, DM], F32, kind="ExternalInput")
        D["wq_pre"] = nc.dram_tensor("wq_pre", [H, 128, DM], F32, kind="ExternalInput")
        D["wo_pre"] = nc.dram_tensor("wo_pre", [NTB, 128, H * 256], F32,
                                     kind="ExternalInput")
        D["up_pre"] = nc.dram_tensor("up_pre", [HC, 128, DM], F32,
                                     kind="ExternalInput")
        D["dn_pre"] = nc.dram_tensor("dn_pre", [HC, 128, DM], F32,
                                     kind="ExternalInput")
        D["wk_cm"] = nc.dram_tensor("wk_cm", [128, DC * KD], F32, kind="ExternalInput")
        D["wv_cm"] = nc.dram_tensor("wv_cm", [128, DC * KD], F32, kind="ExternalInput")
        D["bq_t"] = nc.dram_tensor("bq_t", [128, H], F32, kind="ExternalInput")
        D["bk_r"] = nc.dram_tensor("bk_r", [1, KD], F32, kind="ExternalInput")
        D["bv_r"] = nc.dram_tensor("bv_r", [1, KD], F32, kind="ExternalInput")
        D["us_t"] = nc.dram_tensor("us_t", [128, HC], F32, kind="ExternalInput")
        D["ds_r"] = nc.dram_tensor("ds_r", [1, DM], F32, kind="ExternalInput")
        D["os_r"] = nc.dram_tensor("os_r", [1, 1], F32, kind="ExternalInput")
        D["y"] = nc.dram_tensor("y", [T, DM], F32, kind="ExternalOutput")
        D["upS"] = nc.dram_tensor("upS", [HC, 128, DM], BF16, kind="Internal")
        D["dnS"] = nc.dram_tensor("dnS", [HC, 128, DM], BF16, kind="Internal")
        D["wqS"] = nc.dram_tensor("wqS", [H, 128, DM], BF16, kind="Internal")
        D["woS"] = nc.dram_tensor("woS", [NTB, 128, H * 256], BF16, kind="Internal")
        D["xnS"] = nc.dram_tensor("xnS", [NTC, 4, 128, DM], BF16, kind="Internal")
        self.D = D

    def build(self):
        self.declare()
        nc = self.nc
        with ExitStack() as persist:
            E = persist.enter_context
            P = {}
            P["kvb"] = E(nc.sbuf_tensor("kvb", [128, self.KD], BF16))
            P["ksb"] = E(nc.sbuf_tensor("ksb", [128, self.KVH], BF16))
            P["ident"] = E(nc.sbuf_tensor("ident", [128, 128], BF16))
            P["ones_c"] = E(nc.sbuf_tensor("ones_c", [128, 1], BF16))
            P["ones_r"] = E(nc.sbuf_tensor("ones_r", [1, 128], F32))
            P["ceps5"] = E(nc.sbuf_tensor("ceps5", [128, 1], F32))
            P["ceps6"] = E(nc.sbuf_tensor("ceps6", [1, 1], F32))
            P["bq_t"] = E(nc.sbuf_tensor("bq_tb", [128, self.H], F32))
            P["usc"] = E(nc.sbuf_tensor("uscb", [128, self.HC], F32))
            P["dsb"] = E(nc.sbuf_tensor("dsb", [128, self.DM], F32))
            P["os_bc"] = E(nc.sbuf_tensor("os_bc", [128, 1], F32))
            P["xn_t"] = E(nc.sbuf_tensor("xn_t", [128, 4 * self.DM], BF16))
            P["wq_sl"] = [E(nc.sbuf_tensor(f"wq_sl{i}", [128, self.DM], BF16))
                          for i in range(2)]
            self.P = P
            S = {}
            for name in ("PE", "VE", "SE", "PC", "LDI",
                         "XA0", "XA1", "XA2", "XA3",
                         "PRE0", "PRE1", "SW0", "SW1",
                         "XN0", "WQ0", "WQ1", "WO0", "WO1",
                         "UP0", "UP1", "UP2", "UP3",
                         "DN0", "DN1", "DN2", "DN3", "DNP0", "DNP1",
                         "XR0", "XR1", "XR2", "YW0", "YW1",
                         "XW0", "XW1", "XW2", "XW3"):
                S[name] = Sem(E(nc.semaphore(name)), name)
            self.Sm = S

            with ExitStack() as a_scope:
                self.prog = Prog()
                self.phase_a(a_scope)
                self.prog.run(nc)
            self.a_end = dict(PE=S["PE"].v, VE=S["VE"].v, SE=S["SE"].v,
                              SW0=S["SW0"].v, SW1=S["SW1"].v,
                              XW=[S[f"XW{i}"].v for i in range(4)])

            with ExitStack() as c_scope:
                self.prog = Prog()
                self.phase_c(c_scope)
                self.prog.run(nc)
        return nc

    def phase_a(self, scope):
        nc, D, P, S = self.nc, self.D, self.P, self.Sm
        W, OP = self.W, self.OP
        DC, KD, DM, H, HC, NG = self.DC, self.KD, self.DM, self.H, self.HC, self.NG
        KVH, NSUB, NSUB_OWN, NTB = self.KVH, self.NSUB, self.NSUB_OWN, self.NTB
        Ea = scope.enter_context
        NS = 4
        x_t = [Ea(nc.sbuf_tensor(f"x_t{i}", [128, DM], F32)) for i in range(NS)]
        xn_tm = [Ea(nc.sbuf_tensor(f"xn_tm{i}", [128, DM], BF16)) for i in range(NS)]
        xn_fm = [Ea(nc.sbuf_tensor(f"xn_fm{i}", [128, DM], BF16)) for i in range(NS)]
        wkb = Ea(nc.sbuf_tensor("wkb", [128, DC * KD], BF16))
        wvb = Ea(nc.sbuf_tensor("wvb", [128, DC * KD], BF16))
        bn24 = [Ea(nc.sbuf_tensor(f"bn24_{i}", [128, 6 * NG], F32)) for i in range(NS)]
        mv = [Ea(nc.sbuf_tensor(f"mv{i}", [128, 2], F32)) for i in range(NS)]
        lnv = [Ea(nc.sbuf_tensor(f"lnv{i}", [128, 1], F32)) for i in range(NS)]
        rstd = [Ea(nc.sbuf_tensor(f"rstd{i}", [128, 1], F32)) for i in range(NS)]
        negmr = [Ea(nc.sbuf_tensor(f"negmr{i}", [128, 1], F32)) for i in range(NS)]
        kb = [Ea(nc.sbuf_tensor(f"kb{i}", [128, KD], F32)) for i in range(2)]
        neg = [Ea(nc.sbuf_tensor(f"negb{i}", [128, KD], F32)) for i in range(2)]
        e_t = [Ea(nc.sbuf_tensor(f"e_t{i}", [128, KD], F32)) for i in range(2)]
        sk_t = [Ea(nc.sbuf_tensor(f"sk_t{i}", [128, KD], BF16)) for i in range(2)]
        v_t = [Ea(nc.sbuf_tensor(f"v_t{i}", [128, KD], BF16)) for i in range(2)]
        dsrow = Ea(nc.sbuf_tensor("dsrow", [1, DM], F32))
        bkrow = Ea(nc.sbuf_tensor("bkrow", [1, KD], F32))
        bvrow = Ea(nc.sbuf_tensor("bvrow", [1, KD], F32))
        osrow = Ea(nc.sbuf_tensor("osrow", [1, 1], F32))
        prep_b = [Ea(nc.sbuf_tensor(f"prep_b{i}", [128, max(DM, H * 256)], BF16))
                  for i in range(2)]
        P["bk_bc"] = Ea(nc.sbuf_tensor("bk_bc", [128, KD], F32))
        P["bv_bc"] = Ea(nc.sbuf_tensor("bv_bc", [128, KD], F32))
        ps_kv = Ea(nc.psum_tensor("ps_kv", [128, KD], F32))
        ps_ks = Ea(nc.psum_tensor("ps_ks", [128, KVH], F32))
        ps_tr = [Ea(nc.psum_tensor(f"ps_tr{i}", [128, DM], BF16)) for i in range(2)]
        ps_k = Ea(nc.psum_tensor("ps_k", [128, 512], F32))
        ps_v = Ea(nc.psum_tensor("ps_v", [128, 512], F32))

        n_ldi = 0

        def gload(dst, src):
            nonlocal n_ldi
            OP("gpsimd", lambda e, d=dst, s=src: e.dma_start(d, s), (S["LDI"], 16))
            n_ldi += 1

        gload(P["bq_t"][:], D["bq_t"][:, :])
        gload(P["usc"][:], D["us_t"][:, :])
        gload(dsrow[:], D["ds_r"][:, :])
        gload(bkrow[:], D["bk_r"][:, :])
        gload(bvrow[:], D["bv_r"][:, :])
        gload(osrow[:], D["os_r"][:, :])
        gload(wkb[:], D["wk_cm"][:, :])
        gload(wvb[:], D["wv_cm"][:, :])
        ldi_total = 16 * n_ldi

        OP("gpsimd", lambda e: e.memset(P["ident"][:], 0.0), (S["PC"], 1))
        W("gpsimd", S["PC"])
        OP("gpsimd", lambda e: e.affine_select(
            out=P["ident"][:], in_=P["ident"][:], compare_op=ALU.not_equal,
            fill=1.0, base=0, pattern=[[-1, 128]], channel_multiplier=1),
            (S["PC"], 1))
        pc_ident = S["PC"].v

        OP("vector", lambda e: e.memset(P["ones_c"][:], 1.0), (S["VE"], 1))
        OP("vector", lambda e: e.memset(P["ones_r"][:], 1.0), (S["VE"], 1))
        OP("vector", lambda e: e.memset(P["ceps5"][:], 1e-5), (S["VE"], 1))
        OP("vector", lambda e: e.memset(P["ceps6"][:], 1e-6), (S["VE"], 1))
        ve_init = S["VE"].v

        W("tensor", S["LDI"], ldi_total)
        W("tensor", S["VE"], ve_init)
        OP("tensor", lambda e: e.matmul(ps_k[:, 0:1], P["ones_r"][:], osrow[:],
                                        start=True, stop=True), (S["PE"], 1))
        W("scalar", S["PE"], S["PE"].v)
        OP("scalar", lambda e: e.copy(P["os_bc"][:], ps_k[:, 0:1]), (S["SE"], 1))
        se_os = S["SE"].v
        W("tensor", S["SE"], se_os)
        OP("tensor", lambda e: e.matmul(ps_k[:, 0:KD], P["ones_r"][:], bkrow[:],
                                        start=True, stop=True), (S["PE"], 1))
        W("scalar", S["PE"], S["PE"].v)
        OP("scalar", lambda e: e.copy(P["bk_bc"][:], ps_k[:, 0:KD]), (S["SE"], 1))
        OP("tensor", lambda e: e.matmul(ps_v[:, 0:KD], P["ones_r"][:], bvrow[:],
                                        start=True, stop=True), (S["PE"], 1))
        W("scalar", S["PE"], S["PE"].v)
        OP("scalar", lambda e: e.copy(P["bv_bc"][:], ps_v[:, 0:KD]), (S["SE"], 1))
        for g in range(NG):
            W("tensor", S["SE"], S["SE"].v)
            OP("tensor", lambda e, g=g: e.matmul(
                ps_v[:, :], P["ones_r"][:], dsrow[:, g * 512:(g + 1) * 512],
                start=True, stop=True), (S["PE"], 1))
            W("scalar", S["PE"], S["PE"].v)
            W("scalar", S["SE"], se_os)
            OP("scalar", lambda e, g=g: e.activation(
                P["dsb"][:, g * 512:(g + 1) * 512], ps_v[:, :],
                AF.Identity, scale=P["os_bc"][:]), (S["SE"], 1))
        W("tensor", S["SE"], S["SE"].v)
        se_consts = S["SE"].v

        prep_items = [("wq", f) for f in range(H)] + [("wo", n) for n in range(NTB)]
        pstate = dict(idx=0, out_free=[0, 0])
        pre_cnt = [0, 0]
        self.prep_done = {}

        def do_prep(n):
            for _ in range(n):
                i = pstate["idx"]
                if i >= len(prep_items):
                    return
                pstate["idx"] = i + 1
                kind, j = prep_items[i]
                sl = i % 2
                pre = S[f"PRE{sl}"]
                sw = S[f"SW{sl}"]
                if pstate["out_free"][sl]:
                    W("gpsimd", sw, pstate["out_free"][sl])
                if kind == "wq":
                    src = D["wq_pre"][j, :, :]
                    dst_s = D["wqS"][j, :, :]
                    wid = self.DM
                else:
                    src = D["wo_pre"][j, :, :]
                    dst_s = D["woS"][j, :, :]
                    wid = H * 256
                OP("gpsimd", lambda e, d=prep_b[sl], s=src, w=wid:
                   e.dma_start(d[:, 0:w], s), (pre, 16))
                pre_cnt[sl] += 16
                W("scalar", pre, pre_cnt[sl])
                OP("scalar", lambda e, d=dst_s, s=prep_b[sl], w=wid:
                   e.dma_start(d, s[:, 0:w]), (sw, 16))
                pstate["out_free"][sl] = sw.v
                self.prep_done[i] = sw.v

        rec = dict(se_xn=[0] * NSUB, ve_bn=[0] * NSUB, ve_tre=[0] * NSUB,
                   pe_tr=[0] * NSUB, pe_k=[0] * NSUB, pe_v=[0] * NSUB,
                   ve_kb=[0] * NSUB, ve_sk=[0] * NSUB, ve_v=[0] * NSUB,
                   se_ln=[0] * NSUB, se_rstd=[0] * NSUB, ve_negmr=[0] * NSUB,
                   ve_aggr=[0] * NSUB, pe_kv=[0] * NSUB)
        xa_cnt = [0] * NS
        xw_done = [0] * NSUB
        nblk = DC // 2

        def emit_ln(st):
            own = st < NSUB_OWN
            sl = st % NS
            src = D["x_own"] if own else D["x_oth"]
            row0 = (st if own else st - NSUB_OWN) * 128
            xa = S[f"XA{sl}"]
            if st >= NS:
                W("sync", S["SE"], rec["se_xn"][st - NS])
                W("sync", S["VE"], rec["ve_bn"][st - NS])
            OP("sync", lambda e, d=x_t[sl], s=src, r=row0:
               e.dma_start(d[:], s[r:r + 128, :]), (xa, 16))
            xa_cnt[sl] += 16
            W("vector", xa, xa_cnt[sl])
            if st >= NS:
                W("vector", S["VE"], rec["ve_negmr"][st - NS])
                W("vector", S["SE"], rec["se_ln"][st - NS])
            for g in range(NG):
                OP("vector", lambda e, g=g, x=x_t[sl], b=bn24[sl]: e.bn_stats(
                    b[:, g * 6:(g + 1) * 6], x[:, g * 512:(g + 1) * 512]),
                    (S["VE"], 1))
            W("vector", S["VE"], S["VE"].v)
            OP("vector", lambda e, b=bn24[sl], m=mv[sl]: e.bn_aggr(m[:], b[:]),
               (S["VE"], 1))
            rec["ve_aggr"][st] = S["VE"].v
            rec["ve_bn"][st] = S["VE"].v
            W("scalar", S["VE"], rec["ve_aggr"][st])
            if st >= NS:
                W("scalar", S["VE"], rec["ve_negmr"][st - NS])
            OP("scalar", lambda e, m=mv[sl], l=lnv[sl]: e.activation(
                l[:], m[:, 1:2], AF.Ln, bias=P["ceps5"][:]), (S["SE"], 1))
            rec["se_ln"][st] = S["SE"].v
            W("scalar", S["SE"], S["SE"].v)
            OP("scalar", lambda e, l=lnv[sl], r=rstd[sl]: e.activation(
                r[:], l[:], AF.Exp, scale=-0.5), (S["SE"], 1))
            rec["se_rstd"][st] = S["SE"].v
            W("vector", S["SE"], rec["se_rstd"][st])
            if st >= NS:
                W("vector", S["SE"], rec["se_xn"][st - NS])
            OP("vector", lambda e, m=mv[sl], r=rstd[sl], n=negmr[sl]:
               e.scalar_tensor_tensor(n[:], m[:, 0:1], -1.0, r[:],
                                      ALU.mult, ALU.mult), (S["VE"], 1))
            rec["ve_negmr"][st] = S["VE"].v
            W("scalar", S["VE"], rec["ve_negmr"][st])
            if st >= NS:
                W("scalar", S["PE"], rec["pe_tr"][st - NS])
            OP("scalar", lambda e, x=x_t[sl], o=xn_tm[sl], r=rstd[sl], n=negmr[sl]:
               e.activation(o[:], x[:], AF.Identity, scale=r[:], bias=n[:]),
               (S["SE"], 1))
            rec["se_xn"][st] = S["SE"].v

        def body_tr(st):
            sl = st % NS
            W("tensor", S["SE"], rec["se_xn"][st])
            if st == 0:
                W("tensor", S["PC"], pc_ident)
                W("tensor", S["SE"], se_consts)
            if st >= 2:
                W("tensor", S["VE"], rec["ve_tre"][st - 2])
            for dc in range(DC):
                OP("tensor", lambda e, dc=dc, x=xn_tm[sl]:
                   e.transpose(ps_tr[st % 2][:, dc * 128:(dc + 1) * 128],
                               x[:, dc * 128:(dc + 1) * 128], P["ident"][:]),
                   *(((S["PE"], 1),) if dc == DC - 1 else ()))
            rec["pe_tr"][st] = S["PE"].v

        def body_kv(st):
            own = st < NSUB_OWN
            tau, sub = st // 4, st % 4
            sl = st % NS
            psl = st % 2
            W("vector", S["PE"], rec["pe_tr"][st])
            if st >= NS:
                W("vector", S["PE"], rec["pe_v"][st - NS])
                if st - NS < NSUB_OWN:
                    W("vector", S[f"XW{st % 4}"], xw_done[st - NS])
            OP("vector", lambda e, o=xn_fm[sl]:
               e.tensor_copy(o[:], ps_tr[st % 2][:, :]), (S["VE"], 1))
            rec["ve_tre"][st] = S["VE"].v
            if own:
                W("sync", S["VE"], rec["ve_tre"][st])
                OP("sync", lambda e, t=tau, s2=sub, s=xn_fm[sl]:
                   e.dma_start(D["xnS"][t, s2, :, :], s[:]),
                   (S[f"XW{st % 4}"], 16))
                xw_done[st] = S[f"XW{st % 4}"].v
            W("tensor", S["VE"], rec["ve_tre"][st])
            if st >= 1:
                W("tensor", S["VE"], rec["ve_kb"][st - 1])
            for dc in range(DC):
                OP("tensor", lambda e, dc=dc, x=xn_fm[sl]: e.matmul(
                    ps_k[:, 0:KD], x[:, dc * 128:(dc + 1) * 128],
                    wkb[:, dc * KD:(dc + 1) * KD],
                    start=(dc == 0), stop=(dc == DC - 1)),
                    *(((S["PE"], 1),) if dc == DC - 1 else ()))
            rec["pe_k"][st] = S["PE"].v
            if st >= 1:
                W("tensor", S["VE"], rec["ve_v"][st - 1])
            for dc in range(DC):
                OP("tensor", lambda e, dc=dc, x=xn_fm[sl]: e.matmul(
                    ps_v[:, 0:KD], x[:, dc * 128:(dc + 1) * 128],
                    wvb[:, dc * KD:(dc + 1) * KD],
                    start=(dc == 0), stop=(dc == DC - 1)),
                    *(((S["PE"], 1),) if dc == DC - 1 else ()))
            rec["pe_v"][st] = S["PE"].v
            W("vector", S["PE"], rec["pe_k"][st])
            if st >= 2:
                W("vector", S["VE"], rec["ve_sk"][st - 2])
            OP("vector", lambda e, o=kb[psl]: e.tensor_tensor(
                o[:], ps_k[:, 0:KD], P["bk_bc"][:], ALU.add), (S["VE"], 1))
            rec["ve_kb"][st] = S["VE"].v
            W("vector", S["VE"], S["VE"].v)
            OP("vector", lambda e, i=kb[psl], o=neg[psl]:
               e.tensor_scalar_min(o[:], i[:], 0.0), (S["VE"], 1))
            W("scalar", S["VE"], S["VE"].v)
            if st >= 2:
                W("scalar", S["VE"], rec["ve_sk"][st - 2])
            OP("scalar", lambda e, i=neg[psl], o=e_t[psl]:
               e.activation(o[:], i[:], AF.Exp), (S["SE"], 1))
            W("vector", S["SE"], S["SE"].v)
            if st >= 2:
                W("vector", S["PE"], rec["pe_kv"][st - 1])
            OP("vector", lambda e, o=sk_t[psl], i=kb[psl], x=e_t[psl]:
               e.scalar_tensor_tensor(o[:], i[:], 0.0, x[:], ALU.max, ALU.add),
               (S["VE"], 1))
            rec["ve_sk"][st] = S["VE"].v
            W("vector", S["PE"], rec["pe_v"][st])
            OP("vector", lambda e, o=v_t[psl]: e.tensor_tensor(
                o[:], ps_v[:, 0:KD], P["bv_bc"][:], ALU.add), (S["VE"], 1))
            rec["ve_v"][st] = S["VE"].v
            W("tensor", S["VE"], rec["ve_v"][st])
            for kvh in range(KVH):
                kv_start = (st == 0 and kvh == 0)
                kv_stop = (st == NSUB - 1 and kvh == KVH - 1)
                OP("tensor", lambda e, kvh=kvh, a=sk_t[psl], b=v_t[psl],
                   st_=kv_start, sp_=kv_stop: e.matmul(
                    ps_kv[:, kvh * 128:(kvh + 1) * 128],
                    a[:, kvh * 128:(kvh + 1) * 128],
                    b[:, kvh * 128:(kvh + 1) * 128],
                    start=st_, stop=sp_, skip_group_check=True))
                OP("tensor", lambda e, kvh=kvh, a=sk_t[psl],
                   st_=kv_start, sp_=kv_stop: e.matmul(
                    ps_ks[:, kvh:kvh + 1],
                    a[:, kvh * 128:(kvh + 1) * 128], P["ones_c"][:],
                    start=st_, stop=sp_, skip_group_check=True),
                    *(((S["PE"], 1),) if kvh == KVH - 1 else ()))
            rec["pe_kv"][st] = S["PE"].v

        emit_ln(0)
        emit_ln(1)
        emit_ln(2)
        body_tr(0)
        for st in range(NSUB):
            if st + 1 < NSUB:
                body_tr(st + 1)
            body_kv(st)
            if st + 3 < NSUB:
                emit_ln(st + 3)
            do_prep(1)
        do_prep(len(prep_items))
        W("vector", S["PE"], rec["pe_kv"][NSUB - 1])
        OP("vector", lambda e: e.tensor_copy(P["kvb"][:], ps_kv[:, :]), (S["VE"], 1))
        OP("vector", lambda e: e.tensor_copy(P["ksb"][:], ps_ks[:, :]), (S["VE"], 1))
        for i in range(4):
            W("sync", S[f"XW{i}"], S[f"XW{i}"].v)
        OP("sync", lambda e: e.dma_start(
            P["xn_t"][:].rearrange("p (s d) -> p s d", s=4),
            D["xnS"][0, :, :, :].rearrange("s p d -> p s d")), (S["XN0"], 16))
        W("sync", S["SW0"], S["SW0"].v)
        W("sync", S["SW1"], S["SW1"].v)
        OP("sync", lambda e: e.dma_start(P["wq_sl"][0][:], D["wqS"][0, :, :]),
           (S["WQ0"], 16))
        OP("sync", lambda e: e.dma_start(P["wq_sl"][1][:], D["wqS"][1, :, :]),
           (S["WQ1"], 16))
        self.c_pre = dict(xn0=S["XN0"].v, wq0=S["WQ0"].v, wq1=S["WQ1"].v)

    def phase_c(self, scope):
        nc, D, P, S = self.nc, self.D, self.P, self.Sm
        W, OP = self.W, self.OP
        DC, KD, DM, H, HC = self.DC, self.KD, self.DM, self.H, self.HC
        NTB, KVH, G, NTC, NHREG = self.NTB, self.KVH, self.G, self.NTC, self.NHREG
        Ec = scope.enter_context
        attn_t = Ec(nc.sbuf_tensor("attn_t", [128, H * 512], BF16))
        hidden = Ec(nc.sbuf_tensor("hidden", [128, HC * 512], BF16))
        xn_t = P["xn_t"]
        wq_sl = P["wq_sl"]
        wo_sl = [Ec(nc.sbuf_tensor(f"wo_sl{i}", [128, H * 256], BF16)) for i in range(2)]
        up_sl = [Ec(nc.sbuf_tensor(f"up_sl{i}", [128, DM], BF16)) for i in range(2)]
        dn_sl = [Ec(nc.sbuf_tensor(f"dn_sl{i}", [128, DM], BF16)) for i in range(2)]
        t1_all = Ec(nc.sbuf_tensor("t1_all", [128, 8 * 512], F32))
        pi_view = t1_all[:].bitcast(BF16)
        pi_b = [pi_view[:, i * DM:(i + 1) * DM] for i in range(4)]
        up_pi = pi_b
        dn_pi = pi_b
        sq_sl = [Ec(nc.sbuf_tensor(f"sq_sl{i}", [128, 512], BF16)) for i in range(2)]
        s_tile = Ec(nc.sbuf_tensor("s_tile", [128, 4 * DM], BF16))
        qb = Ec(nc.sbuf_tensor("qb", [128, 512], F32))
        negq = Ec(nc.sbuf_tensor("negq", [128, 512], F32))
        e_q = Ec(nc.sbuf_tensor("e_q", [128, 512], F32))
        r_tmp = Ec(nc.sbuf_tensor("r_tmp", [1, 512], F32))
        r_row = Ec(nc.sbuf_tensor("r_row", [1, 512], F32))
        rbc = Ec(nc.sbuf_tensor("rbc", [128, 512], F32))
        sg = [Ec(nc.sbuf_tensor(f"sg_{i}", [128, 512], F32)) for i in range(2)]
        xr = [Ec(nc.sbuf_tensor(f"xr_{i}", [128, 512], F32)) for i in range(3)]
        yt = [Ec(nc.sbuf_tensor(f"yt_{i}", [128, 512], F32)) for i in range(2)]
        PS = [Ec(nc.psum_tensor(f"psc{i}", [128, 512], F32)) for i in range(8)]
        self.C = dict(PS=PS, t1_all=t1_all, sg=sg, xr=xr, yt=yt, s_tile=s_tile)

        a = self.a_end
        for eng in Prog.ENGS:
            W(eng, S["PE"], a["PE"])
            W(eng, S["VE"], a["VE"])
            W(eng, S["SE"], a["SE"])
        W("sync", S["SW0"], a["SW0"])
        W("sync", S["SW1"], a["SW1"])
        for i in range(4):
            W("sync", S[f"XW{i}"], a["XW"][i])
        W("gpsimd", S["SW0"], a["SW0"])
        W("gpsimd", S["SW1"], a["SW1"])
        W("scalar", S["SW0"], a["SW0"])
        W("scalar", S["SW1"], a["SW1"])

        xn_re = xn_t[:].rearrange("p (s dc t) -> p s dc t", s=4, dc=DC)

        xn_cnt = [0] * NTC
        wq_cnt = [0, 0]
        wo_cnt = [0, 0]
        up_cnt = [0, 0]
        dn_cnt = [0, 0]
        self.q_pe = {}
        mix_pe = {}
        attn_ve = {}
        sq_ve = {}
        qevac_se = {}
        up_pe = {}
        up_sgn = {}
        dn_sgn = {}
        upw_done = {}
        dnw_done = {}
        s_pe = {}
        s_evac_se = {}
        up_evac_se = {}
        dn_jj_pe = {}
        self.dn_evac_ve = [0] * 8
        self.t1_free_se = [0] * 8
        self.sg_free = [0, 0]
        self.xr_free = [0] * 3
        self.yw_done = [0] * 2
        self.last_ph2_ve = 0
        self.last_stile_ve = 0
        self.pending_ph2 = None
        last_dn_pe = 0
        prev_lnr_se = [0]
        prev_rbc_se = [0]
        prev_bc_pe = [0]
        up_store_total = None
        dn_store_total = None
        q_pe = self.q_pe

        def emit_xn_load(tau):
            if tau >= 1:
                W("sync", S["PE"], up_pe[(tau - 1, HC - 1)])
            OP("sync", lambda e, t=tau: e.dma_start(
                xn_t[:].rearrange("p (s d) -> p s d", s=4),
                D["xnS"][t, :, :, :].rearrange("s p d -> p s d")), (S["XN0"], 16))
            xn_cnt[tau] = S["XN0"].v

        def emit_wq_load(tau, f):
            fsl = f % 2
            wq = S[f"WQ{fsl}"]
            prev = q_pe.get((tau, f - 2)) if f >= 2 else (
                q_pe.get((tau - 1, H - 2 + f)) if tau >= 1 else None)
            if prev:
                W("sync", S["PE"], prev)
            OP("sync", lambda e, d=wq_sl[fsl], f=f: e.dma_start(
                d[:], D["wqS"][f, :, :]), (wq, 16))
            wq_cnt[fsl] = wq.v

        for tau in range(NTC):
            if tau == 0:
                xn_cnt[0] = self.c_pre["xn0"]
                wq_cnt[0] = self.c_pre["wq0"]
                wq_cnt[1] = self.c_pre["wq1"]

            def emit_q(f):
                fsl = f % 2
                pq = PS[f % 2]
                W("tensor", S[f"WQ{fsl}"], wq_cnt[fsl])
                if f == 0:
                    W("tensor", S["XN0"], xn_cnt[tau])
                    if tau >= 1:
                        W("tensor", S["VE"], self.dn_evac_ve[0])
                if f == 1 and tau >= 1:
                    W("tensor", S["VE"], self.dn_evac_ve[1])
                pe_war = qevac_se.get((tau, f - 2)) if f >= 2 else (
                    qevac_se.get((tau - 1, H - 2 + f)) if tau >= 1 else None)
                if pe_war:
                    W("tensor", S["SE"], pe_war)
                for dc in range(DC):
                    OP("tensor", lambda e, dc=dc, w=wq_sl[fsl], p=pq: e.matmul(
                        p[:, :], w[:, dc * 128:(dc + 1) * 128],
                        xn_re[:, :, dc, :], start=(dc == 0), stop=(dc == DC - 1)),
                        *(((S["PE"], 1),) if dc == DC - 1 else ()))
                q_pe[(tau, f)] = S["PE"].v
                W("scalar", S["PE"], q_pe[(tau, f)])
                prevsq = sq_ve.get((tau, f - 1)) if f >= 1 else (
                    sq_ve.get((tau - 1, H - 1)) if tau >= 1 else None)
                if prevsq:
                    W("scalar", S["VE"], prevsq)
                OP("scalar", lambda e, p=pq, f=f: e.activation(
                    qb[:], p[:, :], AF.Identity, bias=P["bq_t"][:, f:f + 1]),
                    (S["SE"], 1))
                qevac_se[(tau, f)] = S["SE"].v
                W("vector", S["SE"], S["SE"].v)
                OP("vector", lambda e: e.tensor_scalar_min(negq[:], qb[:], 0.0),
                   (S["VE"], 1))
                W("scalar", S["VE"], S["VE"].v)
                OP("scalar", lambda e: e.activation(e_q[:], negq[:], AF.Exp),
                   (S["SE"], 1))
                W("vector", S["SE"], S["SE"].v)
                prevmix = mix_pe.get((tau, f - 2)) if f >= 2 else (
                    mix_pe.get((tau - 1, H - 2 + f)) if tau >= 1 else None)
                if prevmix:
                    W("vector", S["PE"], prevmix)
                OP("vector", lambda e, o=sq_sl[fsl]: e.scalar_tensor_tensor(
                    o[:], qb[:], 0.0, e_q[:], ALU.max, ALU.add), (S["VE"], 1))
                sq_ve[(tau, f)] = S["VE"].v

            def emit_attn_tail(f):
                fsl = f % 2
                kvh = f // G
                pm = PS[2 + (f % 2)]
                W("tensor", S["VE"], sq_ve[(tau, f)])
                if f < 2 and tau >= 1:
                    W("tensor", S["VE"], self.dn_evac_ve[2 + (f % 2)])
                prevattn = attn_ve.get((tau, f - 2)) if f >= 2 else (
                    attn_ve.get((tau - 1, H - 2 + f)) if tau >= 1 else None)
                if prevattn:
                    W("tensor", S["VE"], prevattn)
                OP("tensor", lambda e, kvh=kvh, q=sq_sl[fsl], p=pm: e.matmul(
                    p[:, :], P["kvb"][:, kvh * 128:(kvh + 1) * 128], q[:],
                    start=True, stop=True))
                if prev_lnr_se[0]:
                    W("tensor", S["SE"], prev_lnr_se[0])
                if f == 0 and tau >= 1:
                    W("tensor", S["VE"], self.dn_evac_ve[4])
                OP("tensor", lambda e, kvh=kvh, q=sq_sl[fsl]: e.matmul(
                    PS[4][0:1, :], P["ksb"][:, kvh:kvh + 1], q[:],
                    start=True, stop=True), (S["PE"], 1))
                mix_pe[(tau, f)] = S["PE"].v
                W("scalar", S["PE"], mix_pe[(tau, f)])
                if prev_bc_pe[0]:
                    W("scalar", S["PE"], prev_bc_pe[0])
                OP("scalar", lambda e: e.activation(r_tmp[:], PS[4][0:1, :],
                                                    AF.Ln, bias=P["ceps6"][:]),
                   (S["SE"], 1))
                W("scalar", S["SE"], S["SE"].v)
                OP("scalar", lambda e: e.activation(r_row[:], r_tmp[:], AF.Exp,
                                                    scale=-1.0), (S["SE"], 1))
                prev_lnr_se[0] = S["SE"].v
                W("tensor", S["SE"], prev_lnr_se[0])
                if prev_rbc_se[0]:
                    W("tensor", S["SE"], prev_rbc_se[0])
                if f == 0 and tau >= 1:
                    W("tensor", S["VE"], self.dn_evac_ve[5])
                OP("tensor", lambda e: e.matmul(PS[5][:, :], P["ones_r"][:],
                                                r_row[:], start=True, stop=True),
                   (S["PE"], 1))
                prev_bc_pe[0] = S["PE"].v
                W("scalar", S["PE"], prev_bc_pe[0])
                prevav = attn_ve.get((tau, f - 1)) if f >= 1 else (
                    attn_ve.get((tau - 1, H - 1)) if tau >= 1 else None)
                if prevav:
                    W("scalar", S["VE"], prevav)
                OP("scalar", lambda e: e.copy(rbc[:], PS[5][:, :]), (S["SE"], 1))
                prev_rbc_se[0] = S["SE"].v
                W("vector", S["SE"], prev_rbc_se[0])
                if f == 0 and tau >= 1:
                    W("vector", S["PE"], s_pe[(tau - 1, "end")])
                OP("vector", lambda e, f=f, p=pm: e.tensor_tensor(
                    attn_t[:, f * 512:(f + 1) * 512], p[:, :], rbc[:], ALU.mult),
                    (S["VE"], 1))
                attn_ve[(tau, f)] = S["VE"].v

            for f in range(H):
                if f == 2 and self.pending_ph2 is not None:
                    self.pending_ph2[0]()
                    self.pending_ph2b = self.pending_ph2[1]
                    self.pending_ph2 = None
                emit_q(f)
                if f + 2 < H:
                    emit_wq_load(tau, f + 2)
                if f >= 1:
                    emit_attn_tail(f - 1)
            emit_attn_tail(H - 1)
            attn_done_ve = S["VE"].v

            wo_done = {}

            def emit_wo_load(ntb):
                osl = ntb % 2
                wo = S[f"WO{osl}"]
                prevs = s_pe.get((tau, ntb - 2, 3)) if ntb >= 2 else (
                    s_pe.get((tau - 1, NTB - 2 + ntb, 3)) if tau >= 1 else None)
                if prevs:
                    W("gpsimd", S["PE"], prevs)
                OP("gpsimd", lambda e, d=wo_sl[osl], n=ntb: e.dma_start(
                    d[:], D["woS"][n, :, :]), (wo, 16))
                wo_done[ntb] = wo.v

            emit_wo_load(0)
            emit_wo_load(1)
            if getattr(self, "pending_ph2b", None) is not None:
                self.pending_ph2b()
                self.pending_ph2b = None
            for ntb in range(NTB):
                osl = ntb % 2
                W("tensor", S[f"WO{osl}"], wo_done[ntb])
                if ntb == 0:
                    W("tensor", S["VE"], attn_done_ve)
                    if tau >= 1:
                        W("tensor", S["VE"], self.dn_evac_ve[6])
                        W("tensor", S["VE"], self.dn_evac_ve[7])
                for sub in range(4):
                    gidx = ntb * 4 + sub
                    psl = PS[6 + (gidx % 2)]
                    war = s_evac_se.get((tau, gidx - 2)) if gidx >= 2 else (
                        s_evac_se.get((tau - 1, NTB * 4 - 2 + gidx)) if tau >= 1
                        else None)
                    if war:
                        W("tensor", S["SE"], war)
                    for h in range(H):
                        OP("tensor", lambda e, h=h, sub=sub, p=psl, w=wo_sl[osl]:
                           e.matmul(p[:, 0:256],
                                    attn_t[:, h * 512 + sub * 128:
                                           h * 512 + (sub + 1) * 128],
                                    w[:, h * 256:(h + 1) * 256],
                                    start=(h == 0), stop=(h == H - 1)),
                            *(((S["PE"], 1),) if h == H - 1 else ()))
                    s_pe[(tau, ntb, sub)] = S["PE"].v
                    W("scalar", S["PE"], S["PE"].v)
                    if gidx == 0 and tau >= 1:
                        W("scalar", S["VE"], self.last_stile_ve)
                    OP("scalar", lambda e, sub=sub, n=ntb, p=psl: e.copy(
                        s_tile[:, sub * DM + n * 256: sub * DM + (n + 1) * 256],
                        p[:, 0:256]), (S["SE"], 1))
                    s_evac_se[(tau, gidx)] = S["SE"].v
                if ntb + 2 < NTB:
                    emit_wo_load(ntb + 2)
            s_pe[(tau, "end")] = S["PE"].v

            if tau == 1:
                W("gpsimd", S["SW0"], up_store_total[0])
                W("gpsimd", S["SW1"], up_store_total[1])
            for j in range(HC):
                usl = j % 2
                up = S[f"UP{usl}"]
                pu = PS[j % 2]
                prev = up_pe.get((tau, j - 2)) if j >= 2 else (
                    up_pe.get((tau - 1, HC - 2 + j)) if tau >= 1 else None)
                if tau == 0:
                    p4 = j % 4
                    up = S[f"UP{p4}"]
                    if j >= 4:
                        sem, v = up_sgn[j - 4]
                        W("gpsimd", sem, v)
                    OP("gpsimd", lambda e, d=up_pi[p4], j=j: e.dma_start(
                        d[:], D["up_pre"][j, :, :]), (up, 16))
                    ld_done = up.v
                    eng = "scalar" if (j % 2 == 0) else "vector"
                    W(eng, up, ld_done)
                    if prev:
                        W(eng, S["PE"], prev)
                    if j >= 2:
                        W(eng, S[f"SW{usl}"], upw_done[j - 2])
                    if eng == "scalar":
                        OP("scalar", lambda e, ap=up_sl[usl], b=up_pi[p4]:
                           e.sign(ap[:], b[:]), (S["SE"], 1))
                        up_sgn[j] = (S["SE"], S["SE"].v)
                    else:
                        OP("vector", lambda e, ap=up_sl[usl], b=up_pi[p4]:
                           e.tensor_scalar(ap[:], b[:], 0.0, None, ALU.is_gt),
                           (S["VE"], 1))
                        W("vector", S["VE"], S["VE"].v)
                        OP("vector", lambda e, ap=up_sl[usl]:
                           e.tensor_scalar(ap[:], ap[:], 2.0, -1.0, ALU.mult,
                                           ALU.add), (S["VE"], 1))
                        up_sgn[j] = (S["VE"], S["VE"].v)
                    sem, v = up_sgn[j]
                    W("scalar", sem, v)
                    OP("scalar", lambda e, j=j, s=up_sl[usl]: e.dma_start(
                        D["upS"][j, :, :], s[:]), (S[f"SW{usl}"], 16))
                    upw_done[j] = S[f"SW{usl}"].v
                    W("tensor", sem, v)
                else:
                    if prev:
                        W("gpsimd", S["PE"], prev)
                    OP("gpsimd", lambda e, d=up_sl[usl], j=j: e.dma_start(
                        d[:], D["upS"][j, :, :]), (up, 16))
                    up_cnt[usl] = up.v
                    W("tensor", up, up_cnt[usl])
                if j < 2:
                    W("tensor", S["SE"], qevac_se[(tau, H - 2 + j)])
                    if tau >= 1:
                        W("tensor", S["PE"], last_dn_pe)
                else:
                    W("tensor", S["SE"], up_evac_se[(tau, j - 2)])
                for dc in range(DC):
                    OP("tensor", lambda e, dc=dc, w=up_sl[usl], p=pu: e.matmul(
                        p[:, :], w[:, dc * 128:(dc + 1) * 128],
                        xn_re[:, :, dc, :], start=(dc == 0), stop=(dc == DC - 1)),
                        *(((S["PE"], 1),) if dc == DC - 1 else ()))
                up_pe[(tau, j)] = S["PE"].v
                W("scalar", S["PE"], S["PE"].v)
                OP("scalar", lambda e, j=j, p=pu: e.activation(
                    hidden[:, j * 512:(j + 1) * 512], p[:, :], AF.Relu,
                    scale=P["usc"][:, j:j + 1]), (S["SE"], 1))
                up_evac_se[(tau, j)] = S["SE"].v
            up_evac_last = S["SE"].v
            if tau == 0:
                up_store_total = (S["SW0"].v, S["SW1"].v)

            if tau == 0:
                for j in range(HC):
                    dsl = j % 2
                    p4 = j % 4
                    dnm = S[f"DN{p4}"]
                    if j >= 4:
                        sem, v = dn_sgn[j - 4]
                        W("gpsimd", sem, v)
                    else:
                        sem, v = up_sgn[HC - 4 + j]
                        W("gpsimd", sem, v)
                    OP("gpsimd", lambda e, d=dn_pi[p4], j=j: e.dma_start(
                        d[:], D["dn_pre"][j, :, :]), (dnm, 16))
                    ld_done = dnm.v
                    eng = "scalar" if (j % 2 == 0) else "vector"
                    W(eng, dnm, ld_done)
                    if j >= 2:
                        W(eng, S["PE"], dn_jj_pe[(0, 0, j - 2)])
                        W(eng, S[f"SW{dsl}"], dnw_done[j - 2])
                    if eng == "scalar":
                        OP("scalar", lambda e, ap=dn_sl[dsl], b=dn_pi[p4]:
                           e.sign(ap[:], b[:]), (S["SE"], 1))
                        dn_sgn[j] = (S["SE"], S["SE"].v)
                    else:
                        OP("vector", lambda e, ap=dn_sl[dsl], b=dn_pi[p4]:
                           e.tensor_scalar(ap[:], b[:], 0.0, None, ALU.is_gt),
                           (S["VE"], 1))
                        W("vector", S["VE"], S["VE"].v)
                        OP("vector", lambda e, ap=dn_sl[dsl]:
                           e.tensor_scalar(ap[:], ap[:], 2.0, -1.0, ALU.mult,
                                           ALU.add), (S["VE"], 1))
                        dn_sgn[j] = (S["VE"], S["VE"].v)
                    sem, v = dn_sgn[j]
                    W("scalar", sem, v)
                    OP("scalar", lambda e, j=j, s=dn_sl[dsl]: e.dma_start(
                        D["dnS"][j, :, :], s[:]), (S[f"SW{dsl}"], 16))
                    dnw_done[j] = S[f"SW{dsl}"].v
                    W("tensor", sem, v)
                    W("tensor", S["SE"], up_evac_se[(0, j)])
                    for sub in range(4):
                        for ntd in range(NHREG):
                            bank = sub * NHREG + ntd
                            islast = (sub == 3 and ntd == NHREG - 1)
                            OP("tensor", lambda e, j=j, sub=sub, ntd=ntd, b=bank,
                               d=dn_sl[dsl]: e.matmul(
                                PS[b][:, :],
                                hidden[:, j * 512 + sub * 128:
                                       j * 512 + (sub + 1) * 128],
                                d[:, ntd * 512:(ntd + 1) * 512],
                                start=(j == 0), stop=(j == HC - 1)),
                                *(((S["PE"], 1),) if islast else ()))
                    dn_jj_pe[(0, 0, j)] = S["PE"].v
                dn_done = S["PE"].v
                last_dn_pe = dn_done
                dn_store_total = (S["SW0"].v, S["SW1"].v)
                emit_xn_load(1)
                emit_wq_load(1, 0)
                emit_wq_load(1, 1)
                pha, phb = self._dn_evac(0, 0, dn_done)
                pha()
                phb()
                for jj in range(HC // 2):
                    dsl = jj % 2
                    dnm = S[f"DNP{dsl}"]
                    if jj == 0:
                        W("sync", S["SW0"], dn_store_total[0])
                        W("sync", S["SW1"], dn_store_total[1])
                    war = (dn_jj_pe.get((0, 1, jj - 2)) if jj >= 2
                           else dn_jj_pe[(0, 0, HC - 2 + jj)])
                    if war:
                        W("sync", S["PE"], war)
                    OP("sync", lambda e, d=dn_sl[dsl], jj=jj: e.dma_start(
                        d[:].rearrange("p (j c) -> p j c", j=2),
                        D["dnS"][2 * jj:2 * jj + 2, :, DM // 2:DM].rearrange(
                            "j p c -> p j c")), (dnm, 16))
                    dn_cnt[dsl] = dnm.v
                    W("tensor", dnm, dn_cnt[dsl])
                    if jj == 0:
                        for b in range(8):
                            W("tensor", S["VE"], self.dn_evac_ve[b])
                    for j2 in range(2):
                        j = 2 * jj + j2
                        for sub in range(4):
                            for ntd in range(NHREG):
                                bank = sub * NHREG + ntd
                                islast = (j2 == 1 and sub == 3
                                          and ntd == NHREG - 1)
                                OP("tensor", lambda e, j=j, j2=j2, sub=sub,
                                   ntd=ntd, b=bank, d=dn_sl[dsl]: e.matmul(
                                    PS[b][:, :],
                                    hidden[:, j * 512 + sub * 128:
                                           j * 512 + (sub + 1) * 128],
                                    d[:, j2 * 1024 + ntd * 512:
                                      j2 * 1024 + (ntd + 1) * 512],
                                    start=(j == 0), stop=(j == HC - 1)),
                                    *(((S["PE"], 1),) if islast else ()))
                    dn_jj_pe[(0, 1, jj)] = S["PE"].v
                dn_done = S["PE"].v
                last_dn_pe = dn_done
                self.pending_ph2 = self._dn_evac(0, 1, dn_done)
            else:
                for dhalf in range(2):
                    for jj in range(HC // 2):
                        dsl = jj % 2
                        dnm = S[f"DNP{dsl}"]
                        war = dn_jj_pe.get((tau, dhalf, jj - 2)) if jj >= 2 else (
                            dn_jj_pe.get((tau, 0, HC // 2 - 2 + jj))
                            if dhalf == 1 else
                            (dn_jj_pe.get((tau - 1, 1, HC // 2 - 2 + jj))
                             if tau >= 2 else
                             dn_jj_pe.get((0, 1, HC // 2 - 2 + jj))))
                        if war:
                            W("sync", S["PE"], war)
                        OP("sync", lambda e, d=dn_sl[dsl], dh=dhalf, jj=jj:
                           e.dma_start(
                            d[:].rearrange("p (j c) -> p j c", j=2),
                            D["dnS"][2 * jj:2 * jj + 2, :,
                                     dh * (DM // 2):(dh + 1) * (DM // 2)]
                            .rearrange("j p c -> p j c")), (dnm, 16))
                        dn_cnt[dsl] = dnm.v
                        W("tensor", dnm, dn_cnt[dsl])
                        if jj == 0:
                            W("tensor", S["SE"], up_evac_last)
                            for b in range(8):
                                if self.dn_evac_ve[b]:
                                    W("tensor", S["VE"], self.dn_evac_ve[b])
                        for j2 in range(2):
                            j = 2 * jj + j2
                            for sub in range(4):
                                for ntd in range(NHREG):
                                    bank = sub * NHREG + ntd
                                    islast = (j2 == 1 and sub == 3
                                              and ntd == NHREG - 1)
                                    OP("tensor", lambda e, j=j, j2=j2, sub=sub,
                                       ntd=ntd, b=bank, d=dn_sl[dsl]: e.matmul(
                                        PS[b][:, :],
                                        hidden[:, j * 512 + sub * 128:
                                               j * 512 + (sub + 1) * 128],
                                        d[:, j2 * 1024 + ntd * 512:
                                          j2 * 1024 + (ntd + 1) * 512],
                                        start=(j == 0), stop=(j == HC - 1)),
                                        *(((S["PE"], 1),) if islast else ()))
                        dn_jj_pe[(tau, dhalf, jj)] = S["PE"].v
                    dn_done = S["PE"].v
                    last_dn_pe = dn_done
                    if dhalf == 0 and tau + 1 < NTC:
                        emit_xn_load(tau + 1)
                        emit_wq_load(tau + 1, 0)
                        emit_wq_load(tau + 1, 1)
                    pha, phb = self._dn_evac(tau, dhalf, dn_done)
                    if dhalf == 0:
                        pha()
                        phb()
                    else:
                        self.pending_ph2 = (pha, phb)
        if self.pending_ph2 is not None:
            self.pending_ph2[0]()
            self.pending_ph2[1]()
            self.pending_ph2 = None
        W("sync", S["YW0"], S["YW0"].v)
        W("sync", S["YW1"], S["YW1"].v)
        W("gpsimd", S["YW0"], S["YW0"].v)
        W("gpsimd", S["YW1"], S["YW1"].v)

    def _dn_evac(self, tau, dhalf, dn_done):
        S, P, D = self.Sm, self.P, self.D
        W, OP = self.W, self.OP
        DM, NHREG = self.DM, self.NHREG
        C = self.C
        PS, t1_all, sg, xr, yt, s_tile = (C["PS"], C["t1_all"], C["sg"],
                                          C["xr"], C["yt"], C["s_tile"])
        ev_order = [0, 2, 4, 5, 1, 3, 6, 7]
        NXR = len(xr)
        inst = []
        for i, bank in enumerate(ev_order):
            sub, ntd = bank // NHREG, bank % NHREG
            inst.append(dict(bank=bank, sub=sub, slot=i % NXR,
                             dcol=dhalf * (DM // 2) + ntd * 512,
                             row0=tau * 512 + sub * 128, xr_done=0))

        def emit_xr_load(i):
            r = inst[i]
            if self.xr_free[r["slot"]]:
                W("scalar", S["VE"], self.xr_free[r["slot"]])
            OP("scalar", lambda e, r0=r["row0"], c=r["dcol"], sl=r["slot"]:
               e.dma_start(xr[sl][:], D["x_own"][r0:r0 + 128, c:c + 512]),
               (S[f"XR{r['slot']}"], 16))
            r["xr_done"] = S[f"XR{r['slot']}"].v

        for i in range(NXR):
            emit_xr_load(i)
        W("vector", S["PE"], dn_done)
        if self.last_ph2_ve:
            W("vector", S["VE"], self.last_ph2_ve)
        for r in inst:
            b = r["bank"]
            if self.t1_free_se[b]:
                W("vector", S["SE"], self.t1_free_se[b])
            OP("vector", lambda e, b=b, dcol=r["dcol"]:
               e.tensor_tensor(t1_all[:, b * 512:(b + 1) * 512], PS[b][:, :],
                               P["dsb"][:, dcol:dcol + 512], ALU.mult),
               (S["VE"], 1))
            self.dn_evac_ve[b] = S["VE"].v

        def phase2a():
            for i, r in enumerate(inst):
                b, par = r["bank"], i % 2
                W("scalar", S["VE"], self.dn_evac_ve[b])
                if self.sg_free[par]:
                    W("scalar", S["VE"], self.sg_free[par])
                OP("scalar", lambda e, b=b, par=par: e.activation(
                    sg[par][:], t1_all[:, b * 512:(b + 1) * 512], AF.Silu),
                    (S["SE"], 1))
                self.t1_free_se[b] = S["SE"].v
                W("vector", S["SE"], S["SE"].v)
                OP("vector", lambda e, b=b, par=par, sub=r["sub"], dcol=r["dcol"]:
                   e.tensor_tensor(t1_all[:, b * 512:(b + 1) * 512], sg[par][:],
                                   s_tile[:, sub * DM + dcol:
                                          sub * DM + dcol + 512], ALU.mult),
                   (S["VE"], 1))
                self.sg_free[par] = S["VE"].v
                self.last_stile_ve = S["VE"].v

        def phase2b():
            for i, r in enumerate(inst):
                b, p4 = r["bank"], i % 2
                W("vector", S[f"XR{r['slot']}"], r["xr_done"])
                if self.yw_done[p4]:
                    W("vector", S[f"YW{p4}"], self.yw_done[p4])
                OP("vector", lambda e, b=b, p4=p4, sl=r["slot"]: e.tensor_tensor(
                    yt[p4][:], t1_all[:, b * 512:(b + 1) * 512], xr[sl][:],
                    ALU.add), (S["VE"], 1))
                self.xr_free[r["slot"]] = S["VE"].v
                self.last_ph2_ve = S["VE"].v
                W("gpsimd", S["VE"], S["VE"].v)
                OP("gpsimd", lambda e, r0=r["row0"], c=r["dcol"], p4=p4:
                   e.dma_start(D["y"][r0:r0 + 128, c:c + 512], yt[p4][:]),
                   (S[f"YW{p4}"], 16))
                self.yw_done[p4] = S[f"YW{p4}"].v
                if i + NXR < 8:
                    emit_xr_load(i + NXR)
        return phase2a, phase2b


def _host_layout(inputs, cfg):
    B, Sq, DM, H, KVH, DH = (cfg["B"], cfg["S"], cfg["DM"], cfg["H"],
                             cfg["KVH"], cfg["DH"])
    KD = KVH * 128
    HC = DH // 128
    DC = DM // 128
    NTB = DM // 256
    x = np.asarray(inputs["L0_attn_hidden_states"])
    Wq = np.asarray(inputs["L0_attn_Wq"])
    Wo = np.asarray(inputs["L0_attn_Wo"])
    up = np.asarray(inputs["L0_ffn_up_weight"])
    dn = np.asarray(inputs["L0_ffn_down_weight"])
    shared = dict(
        wq_pre=np.ascontiguousarray(
            Wq.reshape(H, 128, DC, 128).transpose(0, 3, 2, 1)
            .reshape(H, 128, DM)),
        wo_pre=np.ascontiguousarray(
            Wo.reshape(NTB, 256, H, 128).transpose(0, 3, 2, 1)
            .reshape(NTB, 128, H * 256)),
        up_pre=np.ascontiguousarray(
            up.reshape(HC, 128, DC, 128).transpose(0, 3, 2, 1)
            .reshape(HC, 128, DM)),
        dn_pre=np.ascontiguousarray(dn.T).reshape(HC, 128, DM),
        wk_cm=np.ascontiguousarray(
            np.asarray(inputs["L0_attn_Wk"]).T.reshape(DC, 128, KD)
            .transpose(1, 0, 2).reshape(128, DC * KD)),
        wv_cm=np.ascontiguousarray(
            np.asarray(inputs["L0_attn_Wv"]).T.reshape(DC, 128, KD)
            .transpose(1, 0, 2).reshape(128, DC * KD)),
        bq_t=np.ascontiguousarray(np.asarray(inputs["L0_attn_bq"]).reshape(H, 128).T),
        bk_r=np.ascontiguousarray(np.asarray(inputs["L0_attn_bk"]).reshape(1, KD)),
        bv_r=np.ascontiguousarray(np.asarray(inputs["L0_attn_bv"]).reshape(1, KD)),
        us_t=np.ascontiguousarray(
            np.asarray(inputs["L0_ffn_up_scale"]).reshape(HC, 128).T),
        ds_r=np.ascontiguousarray(
            np.asarray(inputs["L0_ffn_down_scale"]).reshape(1, DM)),
        os_r=np.ascontiguousarray(
            np.asarray(inputs["L0_ffn_output_scale"]).reshape(1, 1)),
    )
    T = Sq // 2
    in_maps = []
    for c in range(2 * B):
        b, h = c // 2, c % 2
        m = dict(shared)
        m["x_own"] = np.ascontiguousarray(x[b, h * T:(h + 1) * T])
        m["x_oth"] = np.ascontiguousarray(x[b, (1 - h) * T:(2 - h) * T])
        in_maps.append(m)
    return in_maps


def kernel(**inputs):
    cfg = REAL_CFG
    in_maps = _host_layout(inputs, cfg)
    bld = Builder(cfg)
    nc = bld.build()
    res = run_bass_kernel_spmd(nc, in_maps, core_ids=list(range(2 * cfg["B"])))
    B, Sq, DM = cfg["B"], cfg["S"], cfg["DM"]
    T = Sq // 2
    y = np.empty((B, Sq, DM), dtype=np.float32)
    for c in range(2 * B):
        b, h = c // 2, c % 2
        y[b, h * T:(h + 1) * T] = res.results[c]["y"]
    return y


# version 17
# speedup vs baseline: 1.0250x; 1.0112x over previous
import numpy as np
from contextlib import ExitStack

import concourse.bass as bass
import concourse.mybir as mybir
from concourse.bass_utils import run_bass_kernel_spmd

F32 = mybir.dt.float32
BF16 = mybir.dt.bfloat16
AF = mybir.ActivationFunctionType
ALU = mybir.AluOpType

REAL_CFG = dict(B=4, S=4096, DM=2048, H=16, KVH=4, DH=8192)


class Sem:
    __slots__ = ("h", "v", "name")

    def __init__(self, h, name):
        self.h = h
        self.v = 0
        self.name = name


class Prog:
    ENGS = ("tensor", "vector", "scalar", "gpsimd", "sync")

    def __init__(self):
        self.q = {e: [] for e in self.ENGS}

    def emit(self, eng, fn):
        self.q[eng].append(fn)

    def run(self, nc):
        with nc.Block() as blk:
            for e in self.ENGS:
                fns = self.q[e]
                if fns:
                    getattr(blk, e)(lambda engine, fns=fns: [f(engine) for f in fns])


class Builder:
    def __init__(self, cfg):
        self.cfg = dict(cfg)
        c = self.cfg
        self.B = c["B"]; self.S = c["S"]; self.DM = c["DM"]
        self.H = c["H"]; self.KVH = c["KVH"]; self.DH = c["DH"]
        assert self.H * 128 == self.DM
        self.DC = self.DM // 128
        self.HC = self.DH // 128
        self.G = self.H // self.KVH
        self.KD = self.KVH * 128
        assert self.KD <= 512
        self.T = self.S // 2
        assert self.T % 512 == 0 and self.S % 512 == 0
        self.NTC = self.T // 512
        self.NSUB = self.S // 128
        self.NSUB_OWN = self.T // 128
        self.NTB = self.DM // 256
        self.NG = self.DM // 512
        assert self.NG >= 1 and self.DM % 512 == 0
        self.NHREG = (self.DM // 2) // 512
        assert self.NHREG == 2
        assert self.DC % 2 == 0
        self.nc = bass.Bass("TRN2", target_bir_lowering=False)
        self.prog = None

    def W(self, eng, sem, v=None):
        if v is None:
            v = sem.v
        if v <= 0:
            return
        self.prog.emit(eng, lambda e, h=sem.h, v=v: e.wait_ge(h, v))

    def OP(self, eng, fn, *incs):
        pairs = []
        for s, n in incs:
            s.v += n
            pairs.append((s.h, n))

        def cl(e, fn=fn, pairs=pairs):
            inst = fn(e)
            for h, n in pairs:
                inst = inst.then_inc(h, n)

        self.prog.emit(eng, cl)

    def declare(self):
        nc = self.nc
        DM, KD, H, HC, DH = self.DM, self.KD, self.H, self.HC, self.DH
        T, NTC, NTB, DC = self.T, self.NTC, self.NTB, self.DC
        D = {}
        D["x_own"] = nc.dram_tensor("x_own", [T, DM], F32, kind="ExternalInput")
        D["x_oth"] = nc.dram_tensor("x_oth", [T, DM], F32, kind="ExternalInput")
        D["wq_pre"] = nc.dram_tensor("wq_pre", [H, 128, DM], F32, kind="ExternalInput")
        D["wo_pre"] = nc.dram_tensor("wo_pre", [NTB, 128, H * 256], F32,
                                     kind="ExternalInput")
        D["up_pre"] = nc.dram_tensor("up_pre", [HC, 128, DM], F32,
                                     kind="ExternalInput")
        D["dn_pre"] = nc.dram_tensor("dn_pre", [HC, 128, DM], F32,
                                     kind="ExternalInput")
        D["wk_cm"] = nc.dram_tensor("wk_cm", [128, DC * KD], F32, kind="ExternalInput")
        D["wv_cm"] = nc.dram_tensor("wv_cm", [128, DC * KD], F32, kind="ExternalInput")
        D["bq_t"] = nc.dram_tensor("bq_t", [128, H], F32, kind="ExternalInput")
        D["bk_r"] = nc.dram_tensor("bk_r", [1, KD], F32, kind="ExternalInput")
        D["bv_r"] = nc.dram_tensor("bv_r", [1, KD], F32, kind="ExternalInput")
        D["us_t"] = nc.dram_tensor("us_t", [128, HC], F32, kind="ExternalInput")
        D["ds_r"] = nc.dram_tensor("ds_r", [1, DM], F32, kind="ExternalInput")
        D["os_r"] = nc.dram_tensor("os_r", [1, 1], F32, kind="ExternalInput")
        D["y"] = nc.dram_tensor("y", [T, DM], F32, kind="ExternalOutput")
        D["upS"] = nc.dram_tensor("upS", [HC, 128, DM], BF16, kind="Internal")
        D["dnS"] = nc.dram_tensor("dnS", [HC, 128, DM], BF16, kind="Internal")
        D["wqS"] = nc.dram_tensor("wqS", [H, 128, DM], BF16, kind="Internal")
        D["woS"] = nc.dram_tensor("woS", [NTB, 128, H * 256], BF16, kind="Internal")
        D["xnS"] = nc.dram_tensor("xnS", [NTC, 4, 128, DM], BF16, kind="Internal")
        self.D = D

    def build(self):
        self.declare()
        nc = self.nc
        with ExitStack() as persist:
            E = persist.enter_context
            P = {}
            P["kvb"] = E(nc.sbuf_tensor("kvb", [128, self.KD], BF16))
            P["ksb"] = E(nc.sbuf_tensor("ksb", [128, self.KVH], BF16))
            P["ident"] = E(nc.sbuf_tensor("ident", [128, 128], BF16))
            P["ones_c"] = E(nc.sbuf_tensor("ones_c", [128, 1], BF16))
            P["ones_r"] = E(nc.sbuf_tensor("ones_r", [1, 128], F32))
            P["ceps5"] = E(nc.sbuf_tensor("ceps5", [128, 1], F32))
            P["ceps6"] = E(nc.sbuf_tensor("ceps6", [1, 1], F32))
            P["bq_t"] = E(nc.sbuf_tensor("bq_tb", [128, self.H], F32))
            P["usc"] = E(nc.sbuf_tensor("uscb", [128, self.HC], F32))
            P["dsb"] = E(nc.sbuf_tensor("dsb", [128, self.DM], F32))
            P["os_bc"] = E(nc.sbuf_tensor("os_bc", [128, 1], F32))
            P["xn_t"] = E(nc.sbuf_tensor("xn_t", [128, 4 * self.DM], BF16))
            P["wq_sl"] = [E(nc.sbuf_tensor(f"wq_sl{i}", [128, self.DM], BF16))
                          for i in range(2)]
            self.P = P
            S = {}
            for name in ("PE", "VE", "SE", "PC", "LDI",
                         "XA0", "XA1", "XA2", "XA3",
                         "PRE0", "PRE1", "SW0", "SW1",
                         "XN0", "WQ0", "WQ1", "WO0", "WO1",
                         "UP0", "UP1", "UP2", "UP3",
                         "DN0", "DN1", "DN2", "DN3", "DNP0", "DNP1",
                         "XR0", "XR1", "XR2", "YW0", "YW1",
                         "XW0", "XW1", "XW2", "XW3"):
                S[name] = Sem(E(nc.semaphore(name)), name)
            self.Sm = S

            with ExitStack() as a_scope:
                self.prog = Prog()
                self.phase_a(a_scope)
                self.prog.run(nc)
            self.a_end = dict(PE=S["PE"].v, VE=S["VE"].v, SE=S["SE"].v,
                              SW0=S["SW0"].v, SW1=S["SW1"].v,
                              XW=[S[f"XW{i}"].v for i in range(4)])

            with ExitStack() as c_scope:
                self.prog = Prog()
                self.phase_c(c_scope)
                self.prog.run(nc)
        return nc

    def phase_a(self, scope):
        nc, D, P, S = self.nc, self.D, self.P, self.Sm
        W, OP = self.W, self.OP
        DC, KD, DM, H, HC, NG = self.DC, self.KD, self.DM, self.H, self.HC, self.NG
        KVH, NSUB, NSUB_OWN, NTB = self.KVH, self.NSUB, self.NSUB_OWN, self.NTB
        Ea = scope.enter_context
        NS = 4
        x_t = [Ea(nc.sbuf_tensor(f"x_t{i}", [128, DM], F32)) for i in range(NS)]
        xn_tm = [Ea(nc.sbuf_tensor(f"xn_tm{i}", [128, DM], BF16)) for i in range(NS)]
        xn_fm = [Ea(nc.sbuf_tensor(f"xn_fm{i}", [128, DM], BF16)) for i in range(NS)]
        wkb = Ea(nc.sbuf_tensor("wkb", [128, DC * KD], BF16))
        wvb = Ea(nc.sbuf_tensor("wvb", [128, DC * KD], BF16))
        bn24 = [Ea(nc.sbuf_tensor(f"bn24_{i}", [128, 6 * NG], F32)) for i in range(NS)]
        mv = [Ea(nc.sbuf_tensor(f"mv{i}", [128, 2], F32)) for i in range(NS)]
        lnv = [Ea(nc.sbuf_tensor(f"lnv{i}", [128, 1], F32)) for i in range(NS)]
        rstd = [Ea(nc.sbuf_tensor(f"rstd{i}", [128, 1], F32)) for i in range(NS)]
        negmr = [Ea(nc.sbuf_tensor(f"negmr{i}", [128, 1], F32)) for i in range(NS)]
        kb = [Ea(nc.sbuf_tensor(f"kb{i}", [128, KD], F32)) for i in range(2)]
        neg = [Ea(nc.sbuf_tensor(f"negb{i}", [128, KD], F32)) for i in range(2)]
        e_t = [Ea(nc.sbuf_tensor(f"e_t{i}", [128, KD], F32)) for i in range(2)]
        sk_t = [Ea(nc.sbuf_tensor(f"sk_t{i}", [128, KD], BF16)) for i in range(2)]
        v_t = [Ea(nc.sbuf_tensor(f"v_t{i}", [128, KD], BF16)) for i in range(2)]
        dsrow = Ea(nc.sbuf_tensor("dsrow", [1, DM], F32))
        bkrow = Ea(nc.sbuf_tensor("bkrow", [1, KD], F32))
        bvrow = Ea(nc.sbuf_tensor("bvrow", [1, KD], F32))
        osrow = Ea(nc.sbuf_tensor("osrow", [1, 1], F32))
        prep_b = [Ea(nc.sbuf_tensor(f"prep_b{i}", [128, max(DM, H * 256)], BF16))
                  for i in range(2)]
        P["bk_bc"] = Ea(nc.sbuf_tensor("bk_bc", [128, KD], F32))
        P["bv_bc"] = Ea(nc.sbuf_tensor("bv_bc", [128, KD], F32))
        ps_kv = Ea(nc.psum_tensor("ps_kv", [128, KD], F32))
        ps_ks = Ea(nc.psum_tensor("ps_ks", [128, KVH], F32))
        ps_tr = [Ea(nc.psum_tensor(f"ps_tr{i}", [128, DM], BF16)) for i in range(2)]
        ps_k = Ea(nc.psum_tensor("ps_k", [128, 512], F32))
        ps_v = Ea(nc.psum_tensor("ps_v", [128, 512], F32))

        n_ldi = 0

        def gload(dst, src):
            nonlocal n_ldi
            OP("gpsimd", lambda e, d=dst, s=src: e.dma_start(d, s), (S["LDI"], 16))
            n_ldi += 1

        gload(P["bq_t"][:], D["bq_t"][:, :])
        gload(P["usc"][:], D["us_t"][:, :])
        gload(dsrow[:], D["ds_r"][:, :])
        gload(bkrow[:], D["bk_r"][:, :])
        gload(bvrow[:], D["bv_r"][:, :])
        gload(osrow[:], D["os_r"][:, :])
        gload(wkb[:], D["wk_cm"][:, :])
        gload(wvb[:], D["wv_cm"][:, :])
        ldi_total = 16 * n_ldi

        OP("gpsimd", lambda e: e.memset(P["ident"][:], 0.0), (S["PC"], 1))
        W("gpsimd", S["PC"])
        OP("gpsimd", lambda e: e.affine_select(
            out=P["ident"][:], in_=P["ident"][:], compare_op=ALU.not_equal,
            fill=1.0, base=0, pattern=[[-1, 128]], channel_multiplier=1),
            (S["PC"], 1))
        pc_ident = S["PC"].v

        OP("vector", lambda e: e.memset(P["ones_c"][:], 1.0), (S["VE"], 1))
        OP("vector", lambda e: e.memset(P["ones_r"][:], 1.0), (S["VE"], 1))
        OP("vector", lambda e: e.memset(P["ceps5"][:], 1e-5), (S["VE"], 1))
        OP("vector", lambda e: e.memset(P["ceps6"][:], 1e-6), (S["VE"], 1))
        ve_init = S["VE"].v

        W("tensor", S["LDI"], ldi_total)
        W("tensor", S["VE"], ve_init)
        OP("tensor", lambda e: e.matmul(ps_k[:, 0:1], P["ones_r"][:], osrow[:],
                                        start=True, stop=True), (S["PE"], 1))
        W("scalar", S["PE"], S["PE"].v)
        OP("scalar", lambda e: e.copy(P["os_bc"][:], ps_k[:, 0:1]), (S["SE"], 1))
        se_os = S["SE"].v
        W("tensor", S["SE"], se_os)
        OP("tensor", lambda e: e.matmul(ps_k[:, 0:KD], P["ones_r"][:], bkrow[:],
                                        start=True, stop=True), (S["PE"], 1))
        W("scalar", S["PE"], S["PE"].v)
        OP("scalar", lambda e: e.copy(P["bk_bc"][:], ps_k[:, 0:KD]), (S["SE"], 1))
        OP("tensor", lambda e: e.matmul(ps_v[:, 0:KD], P["ones_r"][:], bvrow[:],
                                        start=True, stop=True), (S["PE"], 1))
        W("scalar", S["PE"], S["PE"].v)
        OP("scalar", lambda e: e.copy(P["bv_bc"][:], ps_v[:, 0:KD]), (S["SE"], 1))
        for g in range(NG):
            W("tensor", S["SE"], S["SE"].v)
            OP("tensor", lambda e, g=g: e.matmul(
                ps_v[:, :], P["ones_r"][:], dsrow[:, g * 512:(g + 1) * 512],
                start=True, stop=True), (S["PE"], 1))
            W("scalar", S["PE"], S["PE"].v)
            W("scalar", S["SE"], se_os)
            OP("scalar", lambda e, g=g: e.activation(
                P["dsb"][:, g * 512:(g + 1) * 512], ps_v[:, :],
                AF.Identity, scale=P["os_bc"][:]), (S["SE"], 1))
        W("tensor", S["SE"], S["SE"].v)
        se_consts = S["SE"].v

        prep_items = [("wq", f) for f in range(H)] + [("wo", n) for n in range(NTB)]
        pstate = dict(idx=0, out_free=[0, 0])
        pre_cnt = [0, 0]
        self.prep_done = {}

        def do_prep(n):
            for _ in range(n):
                i = pstate["idx"]
                if i >= len(prep_items):
                    return
                pstate["idx"] = i + 1
                kind, j = prep_items[i]
                sl = i % 2
                pre = S[f"PRE{sl}"]
                sw = S[f"SW{sl}"]
                if pstate["out_free"][sl]:
                    W("gpsimd", sw, pstate["out_free"][sl])
                if kind == "wq":
                    src = D["wq_pre"][j, :, :]
                    dst_s = D["wqS"][j, :, :]
                    wid = self.DM
                else:
                    src = D["wo_pre"][j, :, :]
                    dst_s = D["woS"][j, :, :]
                    wid = H * 256
                OP("gpsimd", lambda e, d=prep_b[sl], s=src, w=wid:
                   e.dma_start(d[:, 0:w], s), (pre, 16))
                pre_cnt[sl] += 16
                W("scalar", pre, pre_cnt[sl])
                OP("scalar", lambda e, d=dst_s, s=prep_b[sl], w=wid:
                   e.dma_start(d, s[:, 0:w]), (sw, 16))
                pstate["out_free"][sl] = sw.v
                self.prep_done[i] = sw.v

        rec = dict(se_xn=[0] * NSUB, ve_bn=[0] * NSUB, ve_tre=[0] * NSUB,
                   pe_tr=[0] * NSUB, pe_k=[0] * NSUB, pe_v=[0] * NSUB,
                   ve_kb=[0] * NSUB, ve_sk=[0] * NSUB, ve_v=[0] * NSUB,
                   se_ln=[0] * NSUB, se_rstd=[0] * NSUB, ve_negmr=[0] * NSUB,
                   ve_aggr=[0] * NSUB, pe_kv=[0] * NSUB)
        xa_cnt = [0] * NS
        xw_done = [0] * NSUB
        nblk = DC // 2

        def emit_ln(st):
            own = st < NSUB_OWN
            sl = st % NS
            src = D["x_own"] if own else D["x_oth"]
            row0 = (st if own else st - NSUB_OWN) * 128
            xa = S[f"XA{sl}"]
            if st >= NS:
                W("sync", S["SE"], rec["se_xn"][st - NS])
                W("sync", S["VE"], rec["ve_bn"][st - NS])
            OP("sync", lambda e, d=x_t[sl], s=src, r=row0:
               e.dma_start(d[:], s[r:r + 128, :]), (xa, 16))
            xa_cnt[sl] += 16
            W("vector", xa, xa_cnt[sl])
            if st >= NS:
                W("vector", S["VE"], rec["ve_negmr"][st - NS])
                W("vector", S["SE"], rec["se_ln"][st - NS])
            for g in range(NG):
                OP("vector", lambda e, g=g, x=x_t[sl], b=bn24[sl]: e.bn_stats(
                    b[:, g * 6:(g + 1) * 6], x[:, g * 512:(g + 1) * 512]),
                    (S["VE"], 1))
            W("vector", S["VE"], S["VE"].v)
            OP("vector", lambda e, b=bn24[sl], m=mv[sl]: e.bn_aggr(m[:], b[:]),
               (S["VE"], 1))
            rec["ve_aggr"][st] = S["VE"].v
            rec["ve_bn"][st] = S["VE"].v
            W("scalar", S["VE"], rec["ve_aggr"][st])
            if st >= NS:
                W("scalar", S["VE"], rec["ve_negmr"][st - NS])
            OP("scalar", lambda e, m=mv[sl], l=lnv[sl]: e.activation(
                l[:], m[:, 1:2], AF.Ln, bias=P["ceps5"][:]), (S["SE"], 1))
            rec["se_ln"][st] = S["SE"].v
            W("scalar", S["SE"], S["SE"].v)
            OP("scalar", lambda e, l=lnv[sl], r=rstd[sl]: e.activation(
                r[:], l[:], AF.Exp, scale=-0.5), (S["SE"], 1))
            rec["se_rstd"][st] = S["SE"].v
            W("vector", S["SE"], rec["se_rstd"][st])
            if st >= NS:
                W("vector", S["SE"], rec["se_xn"][st - NS])
            OP("vector", lambda e, m=mv[sl], r=rstd[sl], n=negmr[sl]:
               e.scalar_tensor_tensor(n[:], m[:, 0:1], -1.0, r[:],
                                      ALU.mult, ALU.mult), (S["VE"], 1))
            rec["ve_negmr"][st] = S["VE"].v
            W("scalar", S["VE"], rec["ve_negmr"][st])
            if st >= NS:
                W("scalar", S["PE"], rec["pe_tr"][st - NS])
            OP("scalar", lambda e, x=x_t[sl], o=xn_tm[sl], r=rstd[sl], n=negmr[sl]:
               e.activation(o[:], x[:], AF.Identity, scale=r[:], bias=n[:]),
               (S["SE"], 1))
            rec["se_xn"][st] = S["SE"].v

        def body_tr(st):
            sl = st % NS
            W("tensor", S["SE"], rec["se_xn"][st])
            if st == 0:
                W("tensor", S["PC"], pc_ident)
                W("tensor", S["SE"], se_consts)
            if st >= 2:
                W("tensor", S["VE"], rec["ve_tre"][st - 2])
            for dc in range(DC):
                OP("tensor", lambda e, dc=dc, x=xn_tm[sl]:
                   e.transpose(ps_tr[st % 2][:, dc * 128:(dc + 1) * 128],
                               x[:, dc * 128:(dc + 1) * 128], P["ident"][:]),
                   *(((S["PE"], 1),) if dc == DC - 1 else ()))
            rec["pe_tr"][st] = S["PE"].v

        def body_kv(st):
            own = st < NSUB_OWN
            tau, sub = st // 4, st % 4
            sl = st % NS
            psl = st % 2
            W("vector", S["PE"], rec["pe_tr"][st])
            if st >= NS:
                W("vector", S["PE"], rec["pe_v"][st - NS])
                if st - NS < NSUB_OWN:
                    W("vector", S[f"XW{st % 4}"], xw_done[st - NS])
            OP("vector", lambda e, o=xn_fm[sl]:
               e.tensor_copy(o[:], ps_tr[st % 2][:, :]), (S["VE"], 1))
            rec["ve_tre"][st] = S["VE"].v
            if own:
                W("sync", S["VE"], rec["ve_tre"][st])
                OP("sync", lambda e, t=tau, s2=sub, s=xn_fm[sl]:
                   e.dma_start(D["xnS"][t, s2, :, :], s[:]),
                   (S[f"XW{st % 4}"], 16))
                xw_done[st] = S[f"XW{st % 4}"].v
            W("tensor", S["VE"], rec["ve_tre"][st])
            if st >= 1:
                W("tensor", S["VE"], rec["ve_kb"][st - 1])
            for dc in range(DC):
                OP("tensor", lambda e, dc=dc, x=xn_fm[sl]: e.matmul(
                    ps_k[:, 0:KD], x[:, dc * 128:(dc + 1) * 128],
                    wkb[:, dc * KD:(dc + 1) * KD],
                    start=(dc == 0), stop=(dc == DC - 1)),
                    *(((S["PE"], 1),) if dc == DC - 1 else ()))
            rec["pe_k"][st] = S["PE"].v
            if st >= 1:
                W("tensor", S["VE"], rec["ve_v"][st - 1])
            for dc in range(DC):
                OP("tensor", lambda e, dc=dc, x=xn_fm[sl]: e.matmul(
                    ps_v[:, 0:KD], x[:, dc * 128:(dc + 1) * 128],
                    wvb[:, dc * KD:(dc + 1) * KD],
                    start=(dc == 0), stop=(dc == DC - 1)),
                    *(((S["PE"], 1),) if dc == DC - 1 else ()))
            rec["pe_v"][st] = S["PE"].v
            W("vector", S["PE"], rec["pe_k"][st])
            if st >= 2:
                W("vector", S["VE"], rec["ve_sk"][st - 2])
            OP("vector", lambda e, o=kb[psl]: e.tensor_tensor(
                o[:], ps_k[:, 0:KD], P["bk_bc"][:], ALU.add), (S["VE"], 1))
            rec["ve_kb"][st] = S["VE"].v
            W("vector", S["VE"], S["VE"].v)
            OP("vector", lambda e, i=kb[psl], o=neg[psl]:
               e.tensor_scalar_min(o[:], i[:], 0.0), (S["VE"], 1))
            W("scalar", S["VE"], S["VE"].v)
            if st >= 2:
                W("scalar", S["VE"], rec["ve_sk"][st - 2])
            OP("scalar", lambda e, i=neg[psl], o=e_t[psl]:
               e.activation(o[:], i[:], AF.Exp), (S["SE"], 1))
            W("vector", S["SE"], S["SE"].v)
            if st >= 2:
                W("vector", S["PE"], rec["pe_kv"][st - 1])
            OP("vector", lambda e, o=sk_t[psl], i=kb[psl], x=e_t[psl]:
               e.scalar_tensor_tensor(o[:], i[:], 0.0, x[:], ALU.max, ALU.add),
               (S["VE"], 1))
            rec["ve_sk"][st] = S["VE"].v
            W("vector", S["PE"], rec["pe_v"][st])
            OP("vector", lambda e, o=v_t[psl]: e.tensor_tensor(
                o[:], ps_v[:, 0:KD], P["bv_bc"][:], ALU.add), (S["VE"], 1))
            rec["ve_v"][st] = S["VE"].v
            W("tensor", S["VE"], rec["ve_v"][st])
            for kvh in range(KVH):
                kv_start = (st == 0 and kvh == 0)
                kv_stop = (st == NSUB - 1 and kvh == KVH - 1)
                OP("tensor", lambda e, kvh=kvh, a=sk_t[psl], b=v_t[psl],
                   st_=kv_start, sp_=kv_stop: e.matmul(
                    ps_kv[:, kvh * 128:(kvh + 1) * 128],
                    a[:, kvh * 128:(kvh + 1) * 128],
                    b[:, kvh * 128:(kvh + 1) * 128],
                    start=st_, stop=sp_, skip_group_check=True))
                OP("tensor", lambda e, kvh=kvh, a=sk_t[psl],
                   st_=kv_start, sp_=kv_stop: e.matmul(
                    ps_ks[:, kvh:kvh + 1],
                    a[:, kvh * 128:(kvh + 1) * 128], P["ones_c"][:],
                    start=st_, stop=sp_, skip_group_check=True),
                    *(((S["PE"], 1),) if kvh == KVH - 1 else ()))
            rec["pe_kv"][st] = S["PE"].v

        emit_ln(0)
        emit_ln(1)
        emit_ln(2)
        body_tr(0)
        for st in range(NSUB):
            if st + 1 < NSUB:
                body_tr(st + 1)
            body_kv(st)
            if st + 3 < NSUB:
                emit_ln(st + 3)
            do_prep(1)
        do_prep(len(prep_items))
        W("vector", S["PE"], rec["pe_kv"][NSUB - 1])
        OP("vector", lambda e: e.tensor_copy(P["kvb"][:], ps_kv[:, :]), (S["VE"], 1))
        OP("vector", lambda e: e.tensor_copy(P["ksb"][:], ps_ks[:, :]), (S["VE"], 1))
        for i in range(4):
            W("sync", S[f"XW{i}"], S[f"XW{i}"].v)
        OP("sync", lambda e: e.dma_start(
            P["xn_t"][:].rearrange("p (s d) -> p s d", s=4),
            D["xnS"][0, :, :, :].rearrange("s p d -> p s d")), (S["XN0"], 16))
        W("sync", S["SW0"], S["SW0"].v)
        W("sync", S["SW1"], S["SW1"].v)
        OP("sync", lambda e: e.dma_start(P["wq_sl"][0][:], D["wqS"][0, :, :]),
           (S["WQ0"], 16))
        OP("sync", lambda e: e.dma_start(P["wq_sl"][1][:], D["wqS"][1, :, :]),
           (S["WQ1"], 16))
        self.c_pre = dict(xn0=S["XN0"].v, wq0=S["WQ0"].v, wq1=S["WQ1"].v)

    def phase_c(self, scope):
        nc, D, P, S = self.nc, self.D, self.P, self.Sm
        W, OP = self.W, self.OP
        DC, KD, DM, H, HC = self.DC, self.KD, self.DM, self.H, self.HC
        NTB, KVH, G, NTC, NHREG = self.NTB, self.KVH, self.G, self.NTC, self.NHREG
        Ec = scope.enter_context
        attn_t = Ec(nc.sbuf_tensor("attn_t", [128, H * 512], BF16))
        hidden = Ec(nc.sbuf_tensor("hidden", [128, HC * 512], BF16))
        xn_t = P["xn_t"]
        wq_sl = P["wq_sl"]
        wo_sl = [Ec(nc.sbuf_tensor(f"wo_sl{i}", [128, H * 256], BF16)) for i in range(2)]
        up_sl = [Ec(nc.sbuf_tensor(f"up_sl{i}", [128, DM], BF16)) for i in range(2)]
        dn_sl = [Ec(nc.sbuf_tensor(f"dn_sl{i}", [128, DM], BF16)) for i in range(2)]
        t1_all = Ec(nc.sbuf_tensor("t1_all", [128, 8 * 512], F32))
        pi_view = t1_all[:].bitcast(BF16)
        pi_b = [pi_view[:, i * DM:(i + 1) * DM] for i in range(4)]
        up_pi = pi_b
        dn_pi = pi_b
        sq_sl = [Ec(nc.sbuf_tensor(f"sq_sl{i}", [128, 512], BF16)) for i in range(2)]
        s_tile = Ec(nc.sbuf_tensor("s_tile", [128, 4 * DM], BF16))
        qb = Ec(nc.sbuf_tensor("qb", [128, 512], F32))
        negq = Ec(nc.sbuf_tensor("negq", [128, 512], F32))
        e_q = Ec(nc.sbuf_tensor("e_q", [128, 512], F32))
        r_tmp = Ec(nc.sbuf_tensor("r_tmp", [1, 512], F32))
        r_row = Ec(nc.sbuf_tensor("r_row", [1, 512], F32))
        rbc = Ec(nc.sbuf_tensor("rbc", [128, 512], F32))
        sg = [Ec(nc.sbuf_tensor(f"sg_{i}", [128, 512], F32)) for i in range(2)]
        xr = [Ec(nc.sbuf_tensor(f"xr_{i}", [128, 512], F32)) for i in range(3)]
        yt = [Ec(nc.sbuf_tensor(f"yt_{i}", [128, 512], F32)) for i in range(2)]
        PS = [Ec(nc.psum_tensor(f"psc{i}", [128, 512], F32)) for i in range(8)]
        self.C = dict(PS=PS, t1_all=t1_all, sg=sg, xr=xr, yt=yt, s_tile=s_tile)

        a = self.a_end
        for eng in Prog.ENGS:
            W(eng, S["PE"], a["PE"])
            W(eng, S["VE"], a["VE"])
            W(eng, S["SE"], a["SE"])
        W("sync", S["SW0"], a["SW0"])
        W("sync", S["SW1"], a["SW1"])
        for i in range(4):
            W("sync", S[f"XW{i}"], a["XW"][i])
        W("gpsimd", S["SW0"], a["SW0"])
        W("gpsimd", S["SW1"], a["SW1"])
        W("scalar", S["SW0"], a["SW0"])
        W("scalar", S["SW1"], a["SW1"])

        xn_re = xn_t[:].rearrange("p (s dc t) -> p s dc t", s=4, dc=DC)

        xn_cnt = [0] * NTC
        wq_cnt = [0, 0]
        wo_cnt = [0, 0]
        up_cnt = [0, 0]
        dn_cnt = [0, 0]
        self.q_pe = {}
        mix_pe = {}
        attn_ve = {}
        sq_ve = {}
        qevac_se = {}
        up_pe = {}
        up_sgn = {}
        dn_sgn = {}
        upw_done = {}
        dnw_done = {}
        s_pe = {}
        s_evac_se = {}
        up_evac_se = {}
        dn_jj_pe = {}
        self.dn_evac_ve = [0] * 8
        self.t1_free_se = [0] * 8
        self.sg_free = [0, 0]
        self.xr_free = [0] * 3
        self.yw_done = [0] * 2
        self.last_ph2_ve = 0
        self.last_stile_ve = 0
        self.pending_ph2 = None
        last_dn_pe = 0
        prev_lnr_se = [0]
        prev_rbc_se = [0]
        prev_bc_pe = [0]
        up_store_total = None
        dn_store_total = None
        q_pe = self.q_pe

        def emit_xn_load(tau):
            if tau >= 1:
                W("sync", S["PE"], up_pe[(tau - 1, HC - 1)])
            OP("sync", lambda e, t=tau: e.dma_start(
                xn_t[:].rearrange("p (s d) -> p s d", s=4),
                D["xnS"][t, :, :, :].rearrange("s p d -> p s d")), (S["XN0"], 16))
            xn_cnt[tau] = S["XN0"].v

        def emit_wq_load(tau, f):
            fsl = f % 2
            wq = S[f"WQ{fsl}"]
            prev = q_pe.get((tau, f - 2)) if f >= 2 else (
                q_pe.get((tau - 1, H - 2 + f)) if tau >= 1 else None)
            if prev:
                W("sync", S["PE"], prev)
            OP("sync", lambda e, d=wq_sl[fsl], f=f: e.dma_start(
                d[:], D["wqS"][f, :, :]), (wq, 16))
            wq_cnt[fsl] = wq.v

        for tau in range(NTC):
            if tau == 0:
                xn_cnt[0] = self.c_pre["xn0"]
                wq_cnt[0] = self.c_pre["wq0"]
                wq_cnt[1] = self.c_pre["wq1"]

            def emit_q(f):
                fsl = f % 2
                pq = PS[f % 2]
                W("tensor", S[f"WQ{fsl}"], wq_cnt[fsl])
                if f == 0:
                    W("tensor", S["XN0"], xn_cnt[tau])
                    if tau >= 1:
                        W("tensor", S["VE"], self.dn_evac_ve[0])
                if f == 1 and tau >= 1:
                    W("tensor", S["VE"], self.dn_evac_ve[1])
                pe_war = qevac_se.get((tau, f - 2)) if f >= 2 else (
                    qevac_se.get((tau - 1, H - 2 + f)) if tau >= 1 else None)
                if pe_war:
                    W("tensor", S["SE"], pe_war)
                for dc in range(DC):
                    OP("tensor", lambda e, dc=dc, w=wq_sl[fsl], p=pq: e.matmul(
                        p[:, :], w[:, dc * 128:(dc + 1) * 128],
                        xn_re[:, :, dc, :], start=(dc == 0), stop=(dc == DC - 1)),
                        *(((S["PE"], 1),) if dc == DC - 1 else ()))
                q_pe[(tau, f)] = S["PE"].v
                W("scalar", S["PE"], q_pe[(tau, f)])
                prevsq = sq_ve.get((tau, f - 1)) if f >= 1 else (
                    sq_ve.get((tau - 1, H - 1)) if tau >= 1 else None)
                if prevsq:
                    W("scalar", S["VE"], prevsq)
                OP("scalar", lambda e, p=pq, f=f: e.activation(
                    qb[:], p[:, :], AF.Identity, bias=P["bq_t"][:, f:f + 1]),
                    (S["SE"], 1))
                qevac_se[(tau, f)] = S["SE"].v
                W("vector", S["SE"], S["SE"].v)
                OP("vector", lambda e: e.tensor_scalar_min(negq[:], qb[:], 0.0),
                   (S["VE"], 1))
                W("scalar", S["VE"], S["VE"].v)
                OP("scalar", lambda e: e.activation(e_q[:], negq[:], AF.Exp),
                   (S["SE"], 1))
                W("vector", S["SE"], S["SE"].v)
                prevmix = mix_pe.get((tau, f - 2)) if f >= 2 else (
                    mix_pe.get((tau - 1, H - 2 + f)) if tau >= 1 else None)
                if prevmix:
                    W("vector", S["PE"], prevmix)
                OP("vector", lambda e, o=sq_sl[fsl]: e.scalar_tensor_tensor(
                    o[:], qb[:], 0.0, e_q[:], ALU.max, ALU.add), (S["VE"], 1))
                sq_ve[(tau, f)] = S["VE"].v

            def emit_attn_tail(f):
                fsl = f % 2
                kvh = f // G
                pm = PS[2 + (f % 2)]
                W("tensor", S["VE"], sq_ve[(tau, f)])
                if f < 2 and tau >= 1:
                    W("tensor", S["VE"], self.dn_evac_ve[2 + (f % 2)])
                prevattn = attn_ve.get((tau, f - 2)) if f >= 2 else (
                    attn_ve.get((tau - 1, H - 2 + f)) if tau >= 1 else None)
                if prevattn:
                    W("tensor", S["VE"], prevattn)
                OP("tensor", lambda e, kvh=kvh, q=sq_sl[fsl], p=pm: e.matmul(
                    p[:, :], P["kvb"][:, kvh * 128:(kvh + 1) * 128], q[:],
                    start=True, stop=True))
                if prev_lnr_se[0]:
                    W("tensor", S["SE"], prev_lnr_se[0])
                if f == 0 and tau >= 1:
                    W("tensor", S["VE"], self.dn_evac_ve[4])
                OP("tensor", lambda e, kvh=kvh, q=sq_sl[fsl]: e.matmul(
                    PS[4][0:1, :], P["ksb"][:, kvh:kvh + 1], q[:],
                    start=True, stop=True), (S["PE"], 1))
                mix_pe[(tau, f)] = S["PE"].v
                W("scalar", S["PE"], mix_pe[(tau, f)])
                if prev_bc_pe[0]:
                    W("scalar", S["PE"], prev_bc_pe[0])
                OP("scalar", lambda e: e.activation(r_tmp[:], PS[4][0:1, :],
                                                    AF.Ln, bias=P["ceps6"][:]),
                   (S["SE"], 1))
                W("scalar", S["SE"], S["SE"].v)
                OP("scalar", lambda e: e.activation(r_row[:], r_tmp[:], AF.Exp,
                                                    scale=-1.0), (S["SE"], 1))
                prev_lnr_se[0] = S["SE"].v
                W("tensor", S["SE"], prev_lnr_se[0])
                if prev_rbc_se[0]:
                    W("tensor", S["SE"], prev_rbc_se[0])
                if f == 0 and tau >= 1:
                    W("tensor", S["VE"], self.dn_evac_ve[5])
                OP("tensor", lambda e: e.matmul(PS[5][:, :], P["ones_r"][:],
                                                r_row[:], start=True, stop=True),
                   (S["PE"], 1))
                prev_bc_pe[0] = S["PE"].v
                W("scalar", S["PE"], prev_bc_pe[0])
                prevav = attn_ve.get((tau, f - 1)) if f >= 1 else (
                    attn_ve.get((tau - 1, H - 1)) if tau >= 1 else None)
                if prevav:
                    W("scalar", S["VE"], prevav)
                OP("scalar", lambda e: e.copy(rbc[:], PS[5][:, :]), (S["SE"], 1))
                prev_rbc_se[0] = S["SE"].v
                W("vector", S["SE"], prev_rbc_se[0])
                if f == 0 and tau >= 1:
                    W("vector", S["PE"], s_pe[(tau - 1, "end")])
                OP("vector", lambda e, f=f, p=pm: e.tensor_tensor(
                    attn_t[:, f * 512:(f + 1) * 512], p[:, :], rbc[:], ALU.mult),
                    (S["VE"], 1))
                attn_ve[(tau, f)] = S["VE"].v

            for f in range(H):
                if f == 2 and self.pending_ph2 is not None:
                    self.pending_ph2[0]()
                    self.pending_ph2b = self.pending_ph2[1]
                    self.pending_ph2 = None
                emit_q(f)
                if f + 2 < H:
                    emit_wq_load(tau, f + 2)
                if f >= 1:
                    emit_attn_tail(f - 1)
            emit_attn_tail(H - 1)
            attn_done_ve = S["VE"].v

            wo_done = {}

            def emit_wo_load(ntb):
                osl = ntb % 2
                wo = S[f"WO{osl}"]
                prevs = s_pe.get((tau, ntb - 2, 3)) if ntb >= 2 else (
                    s_pe.get((tau - 1, NTB - 2 + ntb, 3)) if tau >= 1 else None)
                if prevs:
                    W("gpsimd", S["PE"], prevs)
                OP("gpsimd", lambda e, d=wo_sl[osl], n=ntb: e.dma_start(
                    d[:], D["woS"][n, :, :]), (wo, 16))
                wo_done[ntb] = wo.v

            emit_wo_load(0)
            emit_wo_load(1)
            if getattr(self, "pending_ph2b", None) is not None:
                self.pending_ph2b()
                self.pending_ph2b = None
            for ntb in range(NTB):
                osl = ntb % 2
                W("tensor", S[f"WO{osl}"], wo_done[ntb])
                if ntb == 0:
                    W("tensor", S["VE"], attn_done_ve)
                    if tau >= 1:
                        W("tensor", S["VE"], self.dn_evac_ve[6])
                        W("tensor", S["VE"], self.dn_evac_ve[7])
                for sub in range(4):
                    gidx = ntb * 4 + sub
                    psl = PS[6 + (gidx % 2)]
                    war = s_evac_se.get((tau, gidx - 2)) if gidx >= 2 else (
                        s_evac_se.get((tau - 1, NTB * 4 - 2 + gidx)) if tau >= 1
                        else None)
                    if war:
                        W("tensor", S["SE"], war)
                    for h in range(H):
                        OP("tensor", lambda e, h=h, sub=sub, p=psl, w=wo_sl[osl]:
                           e.matmul(p[:, 0:256],
                                    attn_t[:, h * 512 + sub * 128:
                                           h * 512 + (sub + 1) * 128],
                                    w[:, h * 256:(h + 1) * 256],
                                    start=(h == 0), stop=(h == H - 1)),
                            *(((S["PE"], 1),) if h == H - 1 else ()))
                    s_pe[(tau, ntb, sub)] = S["PE"].v
                    W("scalar", S["PE"], S["PE"].v)
                    if gidx == 0 and tau >= 1:
                        W("scalar", S["VE"], self.last_stile_ve)
                    OP("scalar", lambda e, sub=sub, n=ntb, p=psl: e.copy(
                        s_tile[:, sub * DM + n * 256: sub * DM + (n + 1) * 256],
                        p[:, 0:256]), (S["SE"], 1))
                    s_evac_se[(tau, gidx)] = S["SE"].v
                if ntb + 2 < NTB:
                    emit_wo_load(ntb + 2)
            s_pe[(tau, "end")] = S["PE"].v

            if tau == 1:
                W("gpsimd", S["SW0"], S["SW0"].v)
                W("gpsimd", S["SW1"], S["SW1"].v)
            for j in range(HC):
                usl = j % 2
                up = S[f"UP{usl}"]
                pu = PS[j % 2]
                prev = up_pe.get((tau, j - 2)) if j >= 2 else (
                    up_pe.get((tau - 1, HC - 2 + j)) if tau >= 1 else None)
                if tau == 0:
                    p4 = j % 4
                    up = S[f"UP{p4}"]
                    if j >= 4:
                        sem, v = up_sgn[j - 4]
                        W("gpsimd", sem, v)
                    OP("gpsimd", lambda e, d=up_pi[p4], j=j: e.dma_start(
                        d[:], D["up_pre"][j, :, :]), (up, 16))
                    ld_done = up.v
                    eng = "scalar" if (j % 2 == 0) else "vector"
                    W(eng, up, ld_done)
                    if prev:
                        W(eng, S["PE"], prev)
                    if j >= 2:
                        W(eng, S[f"SW{usl}"], upw_done[j - 2])
                    if eng == "scalar":
                        OP("scalar", lambda e, ap=up_sl[usl], b=up_pi[p4]:
                           e.sign(ap[:], b[:]), (S["SE"], 1))
                        up_sgn[j] = (S["SE"], S["SE"].v)
                    else:
                        OP("vector", lambda e, ap=up_sl[usl], b=up_pi[p4]:
                           e.tensor_scalar(ap[:], b[:], 0.0, None, ALU.is_gt),
                           (S["VE"], 1))
                        W("vector", S["VE"], S["VE"].v)
                        OP("vector", lambda e, ap=up_sl[usl]:
                           e.tensor_scalar(ap[:], ap[:], 2.0, -1.0, ALU.mult,
                                           ALU.add), (S["VE"], 1))
                        up_sgn[j] = (S["VE"], S["VE"].v)
                    sem, v = up_sgn[j]
                    W("scalar", sem, v)
                    OP("scalar", lambda e, j=j, s=up_sl[usl]: e.dma_start(
                        D["upS"][j, :, :], s[:]), (S[f"SW{usl}"], 16))
                    upw_done[j] = S[f"SW{usl}"].v
                    W("tensor", sem, v)
                else:
                    if prev:
                        W("gpsimd", S["PE"], prev)
                    OP("gpsimd", lambda e, d=up_sl[usl], j=j: e.dma_start(
                        d[:], D["upS"][j, :, :]), (up, 16))
                    up_cnt[usl] = up.v
                    W("tensor", up, up_cnt[usl])
                if j < 2:
                    W("tensor", S["SE"], qevac_se[(tau, H - 2 + j)])
                    if tau >= 1:
                        W("tensor", S["PE"], last_dn_pe)
                else:
                    W("tensor", S["SE"], up_evac_se[(tau, j - 2)])
                for dc in range(DC):
                    OP("tensor", lambda e, dc=dc, w=up_sl[usl], p=pu: e.matmul(
                        p[:, :], w[:, dc * 128:(dc + 1) * 128],
                        xn_re[:, :, dc, :], start=(dc == 0), stop=(dc == DC - 1)),
                        *(((S["PE"], 1),) if dc == DC - 1 else ()))
                up_pe[(tau, j)] = S["PE"].v
                W("scalar", S["PE"], S["PE"].v)
                OP("scalar", lambda e, j=j, p=pu: e.activation(
                    hidden[:, j * 512:(j + 1) * 512], p[:, :], AF.Relu,
                    scale=P["usc"][:, j:j + 1]), (S["SE"], 1))
                up_evac_se[(tau, j)] = S["SE"].v
            up_evac_last = S["SE"].v
            if tau == 0:
                up_store_total = (S["SW0"].v, S["SW1"].v)

            if tau == 0:
                for j in range(HC):
                    dsl = j % 2
                    p4 = j % 4
                    dnm = S[f"DN{p4}"]
                    if j >= 4:
                        sem, v = dn_sgn[j - 4]
                        W("gpsimd", sem, v)
                    else:
                        sem, v = up_sgn[HC - 4 + j]
                        W("gpsimd", sem, v)
                    OP("gpsimd", lambda e, d=dn_pi[p4], j=j: e.dma_start(
                        d[:], D["dn_pre"][j, :, :]), (dnm, 16))
                    ld_done = dnm.v
                    eng = "scalar" if (j % 2 == 0) else "vector"
                    W(eng, dnm, ld_done)
                    if j >= 2:
                        W(eng, S["PE"], dn_jj_pe[(0, 0, j - 2)])
                        W(eng, S[f"SW{dsl}"], dnw_done[j - 2])
                    if eng == "scalar":
                        OP("scalar", lambda e, ap=dn_sl[dsl], b=dn_pi[p4]:
                           e.sign(ap[:], b[:]), (S["SE"], 1))
                        dn_sgn[j] = (S["SE"], S["SE"].v)
                    else:
                        OP("vector", lambda e, ap=dn_sl[dsl], b=dn_pi[p4]:
                           e.tensor_scalar(ap[:], b[:], 0.0, None, ALU.is_gt),
                           (S["VE"], 1))
                        W("vector", S["VE"], S["VE"].v)
                        OP("vector", lambda e, ap=dn_sl[dsl]:
                           e.tensor_scalar(ap[:], ap[:], 2.0, -1.0, ALU.mult,
                                           ALU.add), (S["VE"], 1))
                        dn_sgn[j] = (S["VE"], S["VE"].v)
                    sem, v = dn_sgn[j]
                    W("scalar", sem, v)
                    OP("scalar", lambda e, j=j, s=dn_sl[dsl]: e.dma_start(
                        D["dnS"][j, :, :], s[:]), (S[f"SW{dsl}"], 16))
                    dnw_done[j] = S[f"SW{dsl}"].v
                    W("tensor", sem, v)
                    W("tensor", S["SE"], up_evac_se[(0, j)])
                    for sub in range(4):
                        for ntd in range(NHREG):
                            bank = sub * NHREG + ntd
                            islast = (sub == 3 and ntd == NHREG - 1)
                            OP("tensor", lambda e, j=j, sub=sub, ntd=ntd, b=bank,
                               d=dn_sl[dsl]: e.matmul(
                                PS[b][:, :],
                                hidden[:, j * 512 + sub * 128:
                                       j * 512 + (sub + 1) * 128],
                                d[:, ntd * 512:(ntd + 1) * 512],
                                start=(j == 0), stop=(j == HC - 1)),
                                *(((S["PE"], 1),) if islast else ()))
                    dn_jj_pe[(0, 0, j)] = S["PE"].v
                dn_done = S["PE"].v
                last_dn_pe = dn_done
                dn_store_total = (S["SW0"].v, S["SW1"].v)
                emit_xn_load(1)
                emit_wq_load(1, 0)
                emit_wq_load(1, 1)
                pha, phb = self._dn_evac(0, 0, dn_done)
                pha()
                phb()
                for jj in range(HC // 2):
                    dsl = jj % 2
                    dnm = S[f"DNP{dsl}"]
                    if jj == 0:
                        W("sync", S["SW0"], dn_store_total[0])
                        W("sync", S["SW1"], dn_store_total[1])
                    war = (dn_jj_pe.get((0, 1, jj - 2)) if jj >= 2
                           else dn_jj_pe[(0, 0, HC - 2 + jj)])
                    if war:
                        W("sync", S["PE"], war)
                    OP("sync", lambda e, d=dn_sl[dsl], jj=jj: e.dma_start(
                        d[:].rearrange("p (j c) -> p j c", j=2),
                        D["dnS"][2 * jj:2 * jj + 2, :, DM // 2:DM].rearrange(
                            "j p c -> p j c")), (dnm, 16))
                    dn_cnt[dsl] = dnm.v
                    W("tensor", dnm, dn_cnt[dsl])
                    if jj == 0:
                        for b in range(8):
                            W("tensor", S["VE"], self.dn_evac_ve[b])
                    for j2 in range(2):
                        j = 2 * jj + j2
                        for sub in range(4):
                            for ntd in range(NHREG):
                                bank = sub * NHREG + ntd
                                islast = (j2 == 1 and sub == 3
                                          and ntd == NHREG - 1)
                                OP("tensor", lambda e, j=j, j2=j2, sub=sub,
                                   ntd=ntd, b=bank, d=dn_sl[dsl]: e.matmul(
                                    PS[b][:, :],
                                    hidden[:, j * 512 + sub * 128:
                                           j * 512 + (sub + 1) * 128],
                                    d[:, j2 * 1024 + ntd * 512:
                                      j2 * 1024 + (ntd + 1) * 512],
                                    start=(j == 0), stop=(j == HC - 1)),
                                    *(((S["PE"], 1),) if islast else ()))
                    dn_jj_pe[(0, 1, jj)] = S["PE"].v
                dn_done = S["PE"].v
                last_dn_pe = dn_done
                self.pending_ph2 = self._dn_evac(0, 1, dn_done)
            else:
                for dhalf in range(2):
                    for jj in range(HC // 2):
                        dsl = jj % 2
                        dnm = S[f"DNP{dsl}"]
                        war = dn_jj_pe.get((tau, dhalf, jj - 2)) if jj >= 2 else (
                            dn_jj_pe.get((tau, 0, HC // 2 - 2 + jj))
                            if dhalf == 1 else
                            (dn_jj_pe.get((tau - 1, 1, HC // 2 - 2 + jj))
                             if tau >= 2 else
                             dn_jj_pe.get((0, 1, HC // 2 - 2 + jj))))
                        if war:
                            W("sync", S["PE"], war)
                        OP("sync", lambda e, d=dn_sl[dsl], dh=dhalf, jj=jj:
                           e.dma_start(
                            d[:].rearrange("p (j c) -> p j c", j=2),
                            D["dnS"][2 * jj:2 * jj + 2, :,
                                     dh * (DM // 2):(dh + 1) * (DM // 2)]
                            .rearrange("j p c -> p j c")), (dnm, 16))
                        dn_cnt[dsl] = dnm.v
                        W("tensor", dnm, dn_cnt[dsl])
                        if jj == 0:
                            W("tensor", S["SE"], up_evac_last)
                            for b in range(8):
                                if self.dn_evac_ve[b]:
                                    W("tensor", S["VE"], self.dn_evac_ve[b])
                        for j2 in range(2):
                            j = 2 * jj + j2
                            for sub in range(4):
                                for ntd in range(NHREG):
                                    bank = sub * NHREG + ntd
                                    islast = (j2 == 1 and sub == 3
                                              and ntd == NHREG - 1)
                                    OP("tensor", lambda e, j=j, j2=j2, sub=sub,
                                       ntd=ntd, b=bank, d=dn_sl[dsl]: e.matmul(
                                        PS[b][:, :],
                                        hidden[:, j * 512 + sub * 128:
                                               j * 512 + (sub + 1) * 128],
                                        d[:, j2 * 1024 + ntd * 512:
                                          j2 * 1024 + (ntd + 1) * 512],
                                        start=(j == 0), stop=(j == HC - 1)),
                                        *(((S["PE"], 1),) if islast else ()))
                        dn_jj_pe[(tau, dhalf, jj)] = S["PE"].v
                    dn_done = S["PE"].v
                    last_dn_pe = dn_done
                    if dhalf == 0 and tau + 1 < NTC:
                        emit_xn_load(tau + 1)
                        emit_wq_load(tau + 1, 0)
                        emit_wq_load(tau + 1, 1)
                    pha, phb = self._dn_evac(tau, dhalf, dn_done)
                    if dhalf == 0:
                        pha()
                        phb()
                    else:
                        self.pending_ph2 = (pha, phb)
        if self.pending_ph2 is not None:
            self.pending_ph2[0]()
            self.pending_ph2[1]()
            self.pending_ph2 = None
        W("sync", S["YW0"], S["YW0"].v)
        W("sync", S["YW1"], S["YW1"].v)
        W("gpsimd", S["YW0"], S["YW0"].v)
        W("gpsimd", S["YW1"], S["YW1"].v)

    def _dn_evac(self, tau, dhalf, dn_done):
        S, P, D = self.Sm, self.P, self.D
        W, OP = self.W, self.OP
        DM, NHREG = self.DM, self.NHREG
        C = self.C
        PS, t1_all, sg, xr, yt, s_tile = (C["PS"], C["t1_all"], C["sg"],
                                          C["xr"], C["yt"], C["s_tile"])
        ev_order = [0, 2, 4, 5, 1, 3, 6, 7]
        NXR = len(xr)
        inst = []
        for i, bank in enumerate(ev_order):
            sub, ntd = bank // NHREG, bank % NHREG
            inst.append(dict(bank=bank, sub=sub, slot=i % NXR,
                             dcol=dhalf * (DM // 2) + ntd * 512,
                             row0=tau * 512 + sub * 128, xr_done=0))

        def emit_xr_load(i):
            r = inst[i]
            if self.xr_free[r["slot"]]:
                W("scalar", S["VE"], self.xr_free[r["slot"]])
            OP("scalar", lambda e, r0=r["row0"], c=r["dcol"], sl=r["slot"]:
               e.dma_start(xr[sl][:], D["x_own"][r0:r0 + 128, c:c + 512]),
               (S[f"XR{r['slot']}"], 16))
            r["xr_done"] = S[f"XR{r['slot']}"].v

        for i in range(NXR):
            emit_xr_load(i)
        W("vector", S["PE"], dn_done)
        if self.last_ph2_ve:
            W("vector", S["VE"], self.last_ph2_ve)
        for r in inst:
            b = r["bank"]
            if self.t1_free_se[b]:
                W("vector", S["SE"], self.t1_free_se[b])
            OP("vector", lambda e, b=b, dcol=r["dcol"]:
               e.tensor_tensor(t1_all[:, b * 512:(b + 1) * 512], PS[b][:, :],
                               P["dsb"][:, dcol:dcol + 512], ALU.mult),
               (S["VE"], 1))
            self.dn_evac_ve[b] = S["VE"].v

        def phase2a():
            for i, r in enumerate(inst):
                b, par = r["bank"], i % 2
                W("scalar", S["VE"], self.dn_evac_ve[b])
                if self.sg_free[par]:
                    W("scalar", S["VE"], self.sg_free[par])
                OP("scalar", lambda e, b=b, par=par: e.activation(
                    sg[par][:], t1_all[:, b * 512:(b + 1) * 512], AF.Silu),
                    (S["SE"], 1))
                self.t1_free_se[b] = S["SE"].v
                W("vector", S["SE"], S["SE"].v)
                OP("vector", lambda e, b=b, par=par, sub=r["sub"], dcol=r["dcol"]:
                   e.tensor_tensor(t1_all[:, b * 512:(b + 1) * 512], sg[par][:],
                                   s_tile[:, sub * DM + dcol:
                                          sub * DM + dcol + 512], ALU.mult),
                   (S["VE"], 1))
                self.sg_free[par] = S["VE"].v
                self.last_stile_ve = S["VE"].v

        def phase2b():
            for i, r in enumerate(inst):
                b, p4 = r["bank"], i % 2
                W("vector", S[f"XR{r['slot']}"], r["xr_done"])
                if self.yw_done[p4]:
                    W("vector", S[f"YW{p4}"], self.yw_done[p4])
                OP("vector", lambda e, b=b, p4=p4, sl=r["slot"]: e.tensor_tensor(
                    yt[p4][:], t1_all[:, b * 512:(b + 1) * 512], xr[sl][:],
                    ALU.add), (S["VE"], 1))
                self.xr_free[r["slot"]] = S["VE"].v
                self.last_ph2_ve = S["VE"].v
                W("gpsimd", S["VE"], S["VE"].v)
                OP("gpsimd", lambda e, r0=r["row0"], c=r["dcol"], p4=p4:
                   e.dma_start(D["y"][r0:r0 + 128, c:c + 512], yt[p4][:]),
                   (S[f"YW{p4}"], 16))
                self.yw_done[p4] = S[f"YW{p4}"].v
                if i + NXR < 8:
                    emit_xr_load(i + NXR)
        return phase2a, phase2b


def _host_layout(inputs, cfg):
    B, Sq, DM, H, KVH, DH = (cfg["B"], cfg["S"], cfg["DM"], cfg["H"],
                             cfg["KVH"], cfg["DH"])
    KD = KVH * 128
    HC = DH // 128
    DC = DM // 128
    NTB = DM // 256
    x = np.asarray(inputs["L0_attn_hidden_states"])
    Wq = np.asarray(inputs["L0_attn_Wq"])
    Wo = np.asarray(inputs["L0_attn_Wo"])
    up = np.asarray(inputs["L0_ffn_up_weight"])
    dn = np.asarray(inputs["L0_ffn_down_weight"])
    shared = dict(
        wq_pre=np.ascontiguousarray(
            Wq.reshape(H, 128, DC, 128).transpose(0, 3, 2, 1)
            .reshape(H, 128, DM)),
        wo_pre=np.ascontiguousarray(
            Wo.reshape(NTB, 256, H, 128).transpose(0, 3, 2, 1)
            .reshape(NTB, 128, H * 256)),
        up_pre=np.ascontiguousarray(
            up.reshape(HC, 128, DC, 128).transpose(0, 3, 2, 1)
            .reshape(HC, 128, DM)),
        dn_pre=np.ascontiguousarray(dn.T).reshape(HC, 128, DM),
        wk_cm=np.ascontiguousarray(
            np.asarray(inputs["L0_attn_Wk"]).T.reshape(DC, 128, KD)
            .transpose(1, 0, 2).reshape(128, DC * KD)),
        wv_cm=np.ascontiguousarray(
            np.asarray(inputs["L0_attn_Wv"]).T.reshape(DC, 128, KD)
            .transpose(1, 0, 2).reshape(128, DC * KD)),
        bq_t=np.ascontiguousarray(np.asarray(inputs["L0_attn_bq"]).reshape(H, 128).T),
        bk_r=np.ascontiguousarray(np.asarray(inputs["L0_attn_bk"]).reshape(1, KD)),
        bv_r=np.ascontiguousarray(np.asarray(inputs["L0_attn_bv"]).reshape(1, KD)),
        us_t=np.ascontiguousarray(
            np.asarray(inputs["L0_ffn_up_scale"]).reshape(HC, 128).T),
        ds_r=np.ascontiguousarray(
            np.asarray(inputs["L0_ffn_down_scale"]).reshape(1, DM)),
        os_r=np.ascontiguousarray(
            np.asarray(inputs["L0_ffn_output_scale"]).reshape(1, 1)),
    )
    T = Sq // 2
    in_maps = []
    for c in range(2 * B):
        b, h = c // 2, c % 2
        m = dict(shared)
        m["x_own"] = np.ascontiguousarray(x[b, h * T:(h + 1) * T])
        m["x_oth"] = np.ascontiguousarray(x[b, (1 - h) * T:(2 - h) * T])
        in_maps.append(m)
    return in_maps


def kernel(**inputs):
    cfg = REAL_CFG
    in_maps = _host_layout(inputs, cfg)
    bld = Builder(cfg)
    nc = bld.build()
    res = run_bass_kernel_spmd(nc, in_maps, core_ids=list(range(2 * cfg["B"])))
    B, Sq, DM = cfg["B"], cfg["S"], cfg["DM"]
    T = Sq // 2
    y = np.empty((B, Sq, DM), dtype=np.float32)
    for c in range(2 * B):
        b, h = c // 2, c % 2
        y[b, h * T:(h + 1) * T] = res.results[c]["y"]
    return y


# version 18 (final)
# speedup vs baseline: 1.0570x; 1.0313x over previous
import numpy as np
from contextlib import ExitStack

import concourse.bass as bass
import concourse.mybir as mybir
from concourse.bass_utils import run_bass_kernel_spmd

F32 = mybir.dt.float32
BF16 = mybir.dt.bfloat16
AF = mybir.ActivationFunctionType
ALU = mybir.AluOpType

REAL_CFG = dict(B=4, S=4096, DM=2048, H=16, KVH=4, DH=8192)


class Sem:
    __slots__ = ("h", "v", "name")

    def __init__(self, h, name):
        self.h = h
        self.v = 0
        self.name = name


class Prog:
    ENGS = ("tensor", "vector", "scalar", "gpsimd", "sync")

    def __init__(self):
        self.q = {e: [] for e in self.ENGS}

    def emit(self, eng, fn):
        self.q[eng].append(fn)

    def run(self, nc):
        with nc.Block() as blk:
            for e in self.ENGS:
                fns = self.q[e]
                if fns:
                    getattr(blk, e)(lambda engine, fns=fns: [f(engine) for f in fns])


class Builder:
    def __init__(self, cfg):
        self.cfg = dict(cfg)
        c = self.cfg
        self.B = c["B"]; self.S = c["S"]; self.DM = c["DM"]
        self.H = c["H"]; self.KVH = c["KVH"]; self.DH = c["DH"]
        assert self.H * 128 == self.DM
        self.DC = self.DM // 128
        self.HC = self.DH // 128
        self.G = self.H // self.KVH
        self.KD = self.KVH * 128
        assert self.KD <= 512
        self.T = self.S // 2
        assert self.T % 512 == 0 and self.S % 512 == 0
        self.NTC = self.T // 512
        self.NSUB = self.S // 128
        self.NSUB_OWN = self.T // 128
        self.NTB = self.DM // 256
        self.NG = self.DM // 512
        assert self.NG >= 1 and self.DM % 512 == 0
        self.NHREG = (self.DM // 2) // 512
        assert self.NHREG == 2
        assert self.DC % 2 == 0
        self.nc = bass.Bass("TRN2", target_bir_lowering=False)
        self.prog = None

    def W(self, eng, sem, v=None):
        if v is None:
            v = sem.v
        if v <= 0:
            return
        self.prog.emit(eng, lambda e, h=sem.h, v=v: e.wait_ge(h, v))

    def OP(self, eng, fn, *incs):
        pairs = []
        for s, n in incs:
            s.v += n
            pairs.append((s.h, n))

        def cl(e, fn=fn, pairs=pairs):
            inst = fn(e)
            for h, n in pairs:
                inst = inst.then_inc(h, n)

        self.prog.emit(eng, cl)

    def declare(self):
        nc = self.nc
        DM, KD, H, HC, DH = self.DM, self.KD, self.H, self.HC, self.DH
        T, NTC, NTB, DC = self.T, self.NTC, self.NTB, self.DC
        D = {}
        D["x_own"] = nc.dram_tensor("x_own", [T, DM], F32, kind="ExternalInput")
        D["x_oth"] = nc.dram_tensor("x_oth", [T, DM], F32, kind="ExternalInput")
        D["wq_pre"] = nc.dram_tensor("wq_pre", [H, 128, DM], F32, kind="ExternalInput")
        D["wo_pre"] = nc.dram_tensor("wo_pre", [NTB, 128, H * 256], F32,
                                     kind="ExternalInput")
        D["up_pre"] = nc.dram_tensor("up_pre", [HC, 128, DM], F32,
                                     kind="ExternalInput")
        D["dn_pre"] = nc.dram_tensor("dn_pre", [HC, 128, DM], F32,
                                     kind="ExternalInput")
        D["wk_cm"] = nc.dram_tensor("wk_cm", [128, DC * KD], F32, kind="ExternalInput")
        D["wv_cm"] = nc.dram_tensor("wv_cm", [128, DC * KD], F32, kind="ExternalInput")
        D["bq_t"] = nc.dram_tensor("bq_t", [128, H], F32, kind="ExternalInput")
        D["bk_r"] = nc.dram_tensor("bk_r", [1, KD], F32, kind="ExternalInput")
        D["bv_r"] = nc.dram_tensor("bv_r", [1, KD], F32, kind="ExternalInput")
        D["us_t"] = nc.dram_tensor("us_t", [128, HC], F32, kind="ExternalInput")
        D["ds_r"] = nc.dram_tensor("ds_r", [1, DM], F32, kind="ExternalInput")
        D["os_r"] = nc.dram_tensor("os_r", [1, 1], F32, kind="ExternalInput")
        D["y"] = nc.dram_tensor("y", [T, DM], F32, kind="ExternalOutput")
        D["upS"] = nc.dram_tensor("upS", [HC, 128, DM], BF16, kind="Internal")
        D["dnS"] = nc.dram_tensor("dnS", [HC, 128, DM], BF16, kind="Internal")
        D["wqS"] = nc.dram_tensor("wqS", [H, 128, DM], BF16, kind="Internal")
        D["woS"] = nc.dram_tensor("woS", [NTB, 128, H * 256], BF16, kind="Internal")
        D["xnS"] = nc.dram_tensor("xnS", [NTC, 4, 128, DM], BF16, kind="Internal")
        self.D = D

    def build(self):
        self.declare()
        nc = self.nc
        with ExitStack() as persist:
            E = persist.enter_context
            P = {}
            P["kvb"] = E(nc.sbuf_tensor("kvb", [128, self.KD], BF16))
            P["ksb"] = E(nc.sbuf_tensor("ksb", [128, self.KVH], BF16))
            P["ident"] = E(nc.sbuf_tensor("ident", [128, 128], BF16))
            P["ones_c"] = E(nc.sbuf_tensor("ones_c", [128, 1], BF16))
            P["ones_r"] = E(nc.sbuf_tensor("ones_r", [1, 128], F32))
            P["ceps5"] = E(nc.sbuf_tensor("ceps5", [128, 1], F32))
            P["ceps6"] = E(nc.sbuf_tensor("ceps6", [1, 1], F32))
            P["bq_t"] = E(nc.sbuf_tensor("bq_tb", [128, self.H], F32))
            P["usc"] = E(nc.sbuf_tensor("uscb", [128, self.HC], F32))
            P["dsb"] = E(nc.sbuf_tensor("dsb", [128, self.DM], F32))
            P["os_bc"] = E(nc.sbuf_tensor("os_bc", [128, 1], F32))
            P["xn_t"] = E(nc.sbuf_tensor("xn_t", [128, 4 * self.DM], BF16))
            P["wq_sl"] = [E(nc.sbuf_tensor(f"wq_sl{i}", [128, self.DM], BF16))
                          for i in range(2)]
            self.P = P
            S = {}
            for name in ("PE", "VE", "SE", "PC", "LDI",
                         "XA0", "XA1", "XA2", "XA3",
                         "PRE0", "PRE1", "SW0", "SW1",
                         "XN0", "WQ0", "WQ1", "WO0", "WO1",
                         "UP0", "UP1", "UP2", "UP3",
                         "DN0", "DN1", "DN2", "DN3", "DNP0", "DNP1",
                         "XR0", "XR1", "XR2", "YW0", "YW1",
                         "XW0", "XW1", "XW2", "XW3"):
                S[name] = Sem(E(nc.semaphore(name)), name)
            self.Sm = S

            with ExitStack() as a_scope:
                self.prog = Prog()
                self.phase_a(a_scope)
                self.prog.run(nc)
            self.a_end = dict(PE=S["PE"].v, VE=S["VE"].v, SE=S["SE"].v,
                              SW0=S["SW0"].v, SW1=S["SW1"].v,
                              XW=[S[f"XW{i}"].v for i in range(4)])

            with ExitStack() as c_scope:
                self.prog = Prog()
                self.phase_c(c_scope)
                self.prog.run(nc)
        return nc

    def phase_a(self, scope):
        nc, D, P, S = self.nc, self.D, self.P, self.Sm
        W, OP = self.W, self.OP
        DC, KD, DM, H, HC, NG = self.DC, self.KD, self.DM, self.H, self.HC, self.NG
        KVH, NSUB, NSUB_OWN, NTB = self.KVH, self.NSUB, self.NSUB_OWN, self.NTB
        Ea = scope.enter_context
        NS = 4
        x_t = [Ea(nc.sbuf_tensor(f"x_t{i}", [128, DM], F32)) for i in range(NS)]
        xn_tm = [Ea(nc.sbuf_tensor(f"xn_tm{i}", [128, DM], BF16)) for i in range(NS)]
        xn_fm = [Ea(nc.sbuf_tensor(f"xn_fm{i}", [128, DM], BF16)) for i in range(NS)]
        wkb = Ea(nc.sbuf_tensor("wkb", [128, DC * KD], BF16))
        wvb = Ea(nc.sbuf_tensor("wvb", [128, DC * KD], BF16))
        bn24 = [Ea(nc.sbuf_tensor(f"bn24_{i}", [128, 6 * NG], F32)) for i in range(NS)]
        mv = [Ea(nc.sbuf_tensor(f"mv{i}", [128, 2], F32)) for i in range(NS)]
        lnv = [Ea(nc.sbuf_tensor(f"lnv{i}", [128, 1], F32)) for i in range(NS)]
        rstd = [Ea(nc.sbuf_tensor(f"rstd{i}", [128, 1], F32)) for i in range(NS)]
        negmr = [Ea(nc.sbuf_tensor(f"negmr{i}", [128, 1], F32)) for i in range(NS)]
        kb = [Ea(nc.sbuf_tensor(f"kb{i}", [128, KD], F32)) for i in range(2)]
        neg = [Ea(nc.sbuf_tensor(f"negb{i}", [128, KD], F32)) for i in range(2)]
        e_t = [Ea(nc.sbuf_tensor(f"e_t{i}", [128, KD], F32)) for i in range(2)]
        sk_t = [Ea(nc.sbuf_tensor(f"sk_t{i}", [128, KD], BF16)) for i in range(2)]
        v_t = [Ea(nc.sbuf_tensor(f"v_t{i}", [128, KD], BF16)) for i in range(2)]
        dsrow = Ea(nc.sbuf_tensor("dsrow", [1, DM], F32))
        bkrow = Ea(nc.sbuf_tensor("bkrow", [1, KD], F32))
        bvrow = Ea(nc.sbuf_tensor("bvrow", [1, KD], F32))
        osrow = Ea(nc.sbuf_tensor("osrow", [1, 1], F32))
        prep_b = [Ea(nc.sbuf_tensor(f"prep_b{i}", [128, max(DM, H * 256)], BF16))
                  for i in range(2)]
        P["bk_bc"] = Ea(nc.sbuf_tensor("bk_bc", [128, KD], F32))
        P["bv_bc"] = Ea(nc.sbuf_tensor("bv_bc", [128, KD], F32))
        ps_kv = Ea(nc.psum_tensor("ps_kv", [128, KD], F32))
        ps_ks = Ea(nc.psum_tensor("ps_ks", [128, KVH], F32))
        ps_tr = [Ea(nc.psum_tensor(f"ps_tr{i}", [128, DM], BF16)) for i in range(2)]
        ps_k = Ea(nc.psum_tensor("ps_k", [128, 512], F32))
        ps_v = Ea(nc.psum_tensor("ps_v", [128, 512], F32))

        n_ldi = 0

        def gload(dst, src):
            nonlocal n_ldi
            OP("gpsimd", lambda e, d=dst, s=src: e.dma_start(d, s), (S["LDI"], 16))
            n_ldi += 1

        gload(P["bq_t"][:], D["bq_t"][:, :])
        gload(P["usc"][:], D["us_t"][:, :])
        gload(dsrow[:], D["ds_r"][:, :])
        gload(bkrow[:], D["bk_r"][:, :])
        gload(bvrow[:], D["bv_r"][:, :])
        gload(osrow[:], D["os_r"][:, :])
        gload(wkb[:], D["wk_cm"][:, :])
        gload(wvb[:], D["wv_cm"][:, :])
        ldi_total = 16 * n_ldi

        OP("gpsimd", lambda e: e.memset(P["ident"][:], 0.0), (S["PC"], 1))
        W("gpsimd", S["PC"])
        OP("gpsimd", lambda e: e.affine_select(
            out=P["ident"][:], in_=P["ident"][:], compare_op=ALU.not_equal,
            fill=1.0, base=0, pattern=[[-1, 128]], channel_multiplier=1),
            (S["PC"], 1))
        pc_ident = S["PC"].v

        OP("vector", lambda e: e.memset(P["ones_c"][:], 1.0), (S["VE"], 1))
        OP("vector", lambda e: e.memset(P["ones_r"][:], 1.0), (S["VE"], 1))
        OP("vector", lambda e: e.memset(P["ceps5"][:], 1e-5), (S["VE"], 1))
        OP("vector", lambda e: e.memset(P["ceps6"][:], 1e-6), (S["VE"], 1))
        ve_init = S["VE"].v

        W("tensor", S["LDI"], ldi_total)
        W("tensor", S["VE"], ve_init)
        OP("tensor", lambda e: e.matmul(ps_k[:, 0:1], P["ones_r"][:], osrow[:],
                                        start=True, stop=True), (S["PE"], 1))
        W("scalar", S["PE"], S["PE"].v)
        OP("scalar", lambda e: e.copy(P["os_bc"][:], ps_k[:, 0:1]), (S["SE"], 1))
        se_os = S["SE"].v
        W("tensor", S["SE"], se_os)
        OP("tensor", lambda e: e.matmul(ps_k[:, 0:KD], P["ones_r"][:], bkrow[:],
                                        start=True, stop=True), (S["PE"], 1))
        W("scalar", S["PE"], S["PE"].v)
        OP("scalar", lambda e: e.copy(P["bk_bc"][:], ps_k[:, 0:KD]), (S["SE"], 1))
        OP("tensor", lambda e: e.matmul(ps_v[:, 0:KD], P["ones_r"][:], bvrow[:],
                                        start=True, stop=True), (S["PE"], 1))
        W("scalar", S["PE"], S["PE"].v)
        OP("scalar", lambda e: e.copy(P["bv_bc"][:], ps_v[:, 0:KD]), (S["SE"], 1))
        for g in range(NG):
            W("tensor", S["SE"], S["SE"].v)
            OP("tensor", lambda e, g=g: e.matmul(
                ps_v[:, :], P["ones_r"][:], dsrow[:, g * 512:(g + 1) * 512],
                start=True, stop=True), (S["PE"], 1))
            W("scalar", S["PE"], S["PE"].v)
            W("scalar", S["SE"], se_os)
            OP("scalar", lambda e, g=g: e.activation(
                P["dsb"][:, g * 512:(g + 1) * 512], ps_v[:, :],
                AF.Identity, scale=P["os_bc"][:]), (S["SE"], 1))
        W("tensor", S["SE"], S["SE"].v)
        se_consts = S["SE"].v

        prep_items = [("wq", f) for f in range(H)] + [("wo", n) for n in range(NTB)]
        pstate = dict(idx=0, out_free=[0, 0])
        pre_cnt = [0, 0]
        self.prep_done = {}

        def do_prep(n):
            for _ in range(n):
                i = pstate["idx"]
                if i >= len(prep_items):
                    return
                pstate["idx"] = i + 1
                kind, j = prep_items[i]
                sl = i % 2
                pre = S[f"PRE{sl}"]
                sw = S[f"SW{sl}"]
                if pstate["out_free"][sl]:
                    W("gpsimd", sw, pstate["out_free"][sl])
                if kind == "wq":
                    src = D["wq_pre"][j, :, :]
                    dst_s = D["wqS"][j, :, :]
                    wid = self.DM
                else:
                    src = D["wo_pre"][j, :, :]
                    dst_s = D["woS"][j, :, :]
                    wid = H * 256
                OP("gpsimd", lambda e, d=prep_b[sl], s=src, w=wid:
                   e.dma_start(d[:, 0:w], s), (pre, 16))
                pre_cnt[sl] += 16
                W("scalar", pre, pre_cnt[sl])
                OP("scalar", lambda e, d=dst_s, s=prep_b[sl], w=wid:
                   e.dma_start(d, s[:, 0:w]), (sw, 16))
                pstate["out_free"][sl] = sw.v
                self.prep_done[i] = sw.v

        rec = dict(se_xn=[0] * NSUB, ve_bn=[0] * NSUB, ve_tre=[0] * NSUB,
                   pe_tr=[0] * NSUB, pe_k=[0] * NSUB, pe_v=[0] * NSUB,
                   ve_kb=[0] * NSUB, ve_sk=[0] * NSUB, ve_v=[0] * NSUB,
                   se_ln=[0] * NSUB, se_rstd=[0] * NSUB, ve_negmr=[0] * NSUB,
                   ve_aggr=[0] * NSUB, pe_kv=[0] * NSUB)
        xa_cnt = [0] * NS
        xw_done = [0] * NSUB
        nblk = DC // 2

        def emit_ln(st):
            own = st < NSUB_OWN
            sl = st % NS
            src = D["x_own"] if own else D["x_oth"]
            row0 = (st if own else st - NSUB_OWN) * 128
            xa = S[f"XA{sl}"]
            if st >= NS:
                W("sync", S["SE"], rec["se_xn"][st - NS])
                W("sync", S["VE"], rec["ve_bn"][st - NS])
            OP("sync", lambda e, d=x_t[sl], s=src, r=row0:
               e.dma_start(d[:], s[r:r + 128, :]), (xa, 16))
            xa_cnt[sl] += 16
            W("vector", xa, xa_cnt[sl])
            if st >= NS:
                W("vector", S["VE"], rec["ve_negmr"][st - NS])
                W("vector", S["SE"], rec["se_ln"][st - NS])
            for g in range(NG):
                OP("vector", lambda e, g=g, x=x_t[sl], b=bn24[sl]: e.bn_stats(
                    b[:, g * 6:(g + 1) * 6], x[:, g * 512:(g + 1) * 512]),
                    (S["VE"], 1))
            W("vector", S["VE"], S["VE"].v)
            OP("vector", lambda e, b=bn24[sl], m=mv[sl]: e.bn_aggr(m[:], b[:]),
               (S["VE"], 1))
            rec["ve_aggr"][st] = S["VE"].v
            rec["ve_bn"][st] = S["VE"].v
            W("scalar", S["VE"], rec["ve_aggr"][st])
            if st >= NS:
                W("scalar", S["VE"], rec["ve_negmr"][st - NS])
            OP("scalar", lambda e, m=mv[sl], l=lnv[sl]: e.activation(
                l[:], m[:, 1:2], AF.Ln, bias=P["ceps5"][:]), (S["SE"], 1))
            rec["se_ln"][st] = S["SE"].v
            W("scalar", S["SE"], S["SE"].v)
            OP("scalar", lambda e, l=lnv[sl], r=rstd[sl]: e.activation(
                r[:], l[:], AF.Exp, scale=-0.5), (S["SE"], 1))
            rec["se_rstd"][st] = S["SE"].v
            W("vector", S["SE"], rec["se_rstd"][st])
            if st >= NS:
                W("vector", S["SE"], rec["se_xn"][st - NS])
            OP("vector", lambda e, m=mv[sl], r=rstd[sl], n=negmr[sl]:
               e.scalar_tensor_tensor(n[:], m[:, 0:1], -1.0, r[:],
                                      ALU.mult, ALU.mult), (S["VE"], 1))
            rec["ve_negmr"][st] = S["VE"].v
            W("scalar", S["VE"], rec["ve_negmr"][st])
            if st >= NS:
                W("scalar", S["PE"], rec["pe_tr"][st - NS])
            OP("scalar", lambda e, x=x_t[sl], o=xn_tm[sl], r=rstd[sl], n=negmr[sl]:
               e.activation(o[:], x[:], AF.Identity, scale=r[:], bias=n[:]),
               (S["SE"], 1))
            rec["se_xn"][st] = S["SE"].v

        def body_tr(st):
            sl = st % NS
            W("tensor", S["SE"], rec["se_xn"][st])
            if st == 0:
                W("tensor", S["PC"], pc_ident)
                W("tensor", S["SE"], se_consts)
            if st >= 2:
                W("tensor", S["VE"], rec["ve_tre"][st - 2])
            for dc in range(DC):
                OP("tensor", lambda e, dc=dc, x=xn_tm[sl]:
                   e.transpose(ps_tr[st % 2][:, dc * 128:(dc + 1) * 128],
                               x[:, dc * 128:(dc + 1) * 128], P["ident"][:]),
                   *(((S["PE"], 1),) if dc == DC - 1 else ()))
            rec["pe_tr"][st] = S["PE"].v

        def emit_evac(st):
            own = st < NSUB_OWN
            tau, sub = st // 4, st % 4
            sl = st % NS
            W("vector", S["PE"], rec["pe_tr"][st])
            if st >= NS:
                W("vector", S["PE"], rec["pe_v"][st - NS])
                if st - NS < NSUB_OWN:
                    W("vector", S[f"XW{st % 4}"], xw_done[st - NS])
            OP("vector", lambda e, o=xn_fm[sl]:
               e.tensor_copy(o[:], ps_tr[st % 2][:, :]), (S["VE"], 1))
            rec["ve_tre"][st] = S["VE"].v
            if own:
                W("sync", S["VE"], rec["ve_tre"][st])
                OP("sync", lambda e, t=tau, s2=sub, s=xn_fm[sl]:
                   e.dma_start(D["xnS"][t, s2, :, :], s[:]),
                   (S[f"XW{st % 4}"], 16))
                xw_done[st] = S[f"XW{st % 4}"].v

        def body_kv(st):
            own = st < NSUB_OWN
            tau, sub = st // 4, st % 4
            sl = st % NS
            psl = st % 2
            W("tensor", S["VE"], rec["ve_tre"][st])
            if st >= 1:
                W("tensor", S["VE"], rec["ve_kb"][st - 1])
            for dc in range(DC):
                OP("tensor", lambda e, dc=dc, x=xn_fm[sl]: e.matmul(
                    ps_k[:, 0:KD], x[:, dc * 128:(dc + 1) * 128],
                    wkb[:, dc * KD:(dc + 1) * KD],
                    start=(dc == 0), stop=(dc == DC - 1)),
                    *(((S["PE"], 1),) if dc == DC - 1 else ()))
            rec["pe_k"][st] = S["PE"].v
            if st >= 1:
                W("tensor", S["VE"], rec["ve_v"][st - 1])
            for dc in range(DC):
                OP("tensor", lambda e, dc=dc, x=xn_fm[sl]: e.matmul(
                    ps_v[:, 0:KD], x[:, dc * 128:(dc + 1) * 128],
                    wvb[:, dc * KD:(dc + 1) * KD],
                    start=(dc == 0), stop=(dc == DC - 1)),
                    *(((S["PE"], 1),) if dc == DC - 1 else ()))
            rec["pe_v"][st] = S["PE"].v
            W("vector", S["PE"], rec["pe_k"][st])
            if st >= 2:
                W("vector", S["VE"], rec["ve_sk"][st - 2])
            OP("vector", lambda e, o=kb[psl]: e.tensor_tensor(
                o[:], ps_k[:, 0:KD], P["bk_bc"][:], ALU.add), (S["VE"], 1))
            rec["ve_kb"][st] = S["VE"].v
            W("vector", S["VE"], S["VE"].v)
            OP("vector", lambda e, i=kb[psl], o=neg[psl]:
               e.tensor_scalar_min(o[:], i[:], 0.0), (S["VE"], 1))
            W("scalar", S["VE"], S["VE"].v)
            if st >= 2:
                W("scalar", S["VE"], rec["ve_sk"][st - 2])
            OP("scalar", lambda e, i=neg[psl], o=e_t[psl]:
               e.activation(o[:], i[:], AF.Exp), (S["SE"], 1))
            W("vector", S["SE"], S["SE"].v)
            if st >= 2:
                W("vector", S["PE"], rec["pe_kv"][st - 1])
            OP("vector", lambda e, o=sk_t[psl], i=kb[psl], x=e_t[psl]:
               e.scalar_tensor_tensor(o[:], i[:], 0.0, x[:], ALU.max, ALU.add),
               (S["VE"], 1))
            rec["ve_sk"][st] = S["VE"].v
            W("vector", S["PE"], rec["pe_v"][st])
            OP("vector", lambda e, o=v_t[psl]: e.tensor_tensor(
                o[:], ps_v[:, 0:KD], P["bv_bc"][:], ALU.add), (S["VE"], 1))
            rec["ve_v"][st] = S["VE"].v
            W("tensor", S["VE"], rec["ve_v"][st])
            for kvh in range(KVH):
                kv_start = (st == 0 and kvh == 0)
                kv_stop = (st == NSUB - 1 and kvh == KVH - 1)
                OP("tensor", lambda e, kvh=kvh, a=sk_t[psl], b=v_t[psl],
                   st_=kv_start, sp_=kv_stop: e.matmul(
                    ps_kv[:, kvh * 128:(kvh + 1) * 128],
                    a[:, kvh * 128:(kvh + 1) * 128],
                    b[:, kvh * 128:(kvh + 1) * 128],
                    start=st_, stop=sp_, skip_group_check=True))
                OP("tensor", lambda e, kvh=kvh, a=sk_t[psl],
                   st_=kv_start, sp_=kv_stop: e.matmul(
                    ps_ks[:, kvh:kvh + 1],
                    a[:, kvh * 128:(kvh + 1) * 128], P["ones_c"][:],
                    start=st_, stop=sp_, skip_group_check=True),
                    *(((S["PE"], 1),) if kvh == KVH - 1 else ()))
            rec["pe_kv"][st] = S["PE"].v

        emit_ln(0)
        emit_ln(1)
        emit_ln(2)
        body_tr(0)
        emit_evac(0)
        for st in range(NSUB):
            if st + 1 < NSUB:
                body_tr(st + 1)
                emit_evac(st + 1)
            body_kv(st)
            if st + 3 < NSUB:
                emit_ln(st + 3)
            do_prep(1)
        do_prep(len(prep_items))
        W("vector", S["PE"], rec["pe_kv"][NSUB - 1])
        OP("vector", lambda e: e.tensor_copy(P["kvb"][:], ps_kv[:, :]), (S["VE"], 1))
        OP("vector", lambda e: e.tensor_copy(P["ksb"][:], ps_ks[:, :]), (S["VE"], 1))
        for i in range(4):
            W("sync", S[f"XW{i}"], S[f"XW{i}"].v)
        OP("sync", lambda e: e.dma_start(
            P["xn_t"][:].rearrange("p (s d) -> p s d", s=4),
            D["xnS"][0, :, :, :].rearrange("s p d -> p s d")), (S["XN0"], 16))
        W("sync", S["SW0"], S["SW0"].v)
        W("sync", S["SW1"], S["SW1"].v)
        OP("sync", lambda e: e.dma_start(P["wq_sl"][0][:], D["wqS"][0, :, :]),
           (S["WQ0"], 16))
        OP("sync", lambda e: e.dma_start(P["wq_sl"][1][:], D["wqS"][1, :, :]),
           (S["WQ1"], 16))
        self.c_pre = dict(xn0=S["XN0"].v, wq0=S["WQ0"].v, wq1=S["WQ1"].v)

    def phase_c(self, scope):
        nc, D, P, S = self.nc, self.D, self.P, self.Sm
        W, OP = self.W, self.OP
        DC, KD, DM, H, HC = self.DC, self.KD, self.DM, self.H, self.HC
        NTB, KVH, G, NTC, NHREG = self.NTB, self.KVH, self.G, self.NTC, self.NHREG
        Ec = scope.enter_context
        attn_t = Ec(nc.sbuf_tensor("attn_t", [128, H * 512], BF16))
        hidden = Ec(nc.sbuf_tensor("hidden", [128, HC * 512], BF16))
        xn_t = P["xn_t"]
        wq_sl = P["wq_sl"]
        wo_sl = [Ec(nc.sbuf_tensor(f"wo_sl{i}", [128, H * 256], BF16)) for i in range(2)]
        up_sl = [Ec(nc.sbuf_tensor(f"up_sl{i}", [128, DM], BF16)) for i in range(2)]
        dn_sl = [Ec(nc.sbuf_tensor(f"dn_sl{i}", [128, DM], BF16)) for i in range(2)]
        t1_all = Ec(nc.sbuf_tensor("t1_all", [128, 8 * 512], F32))
        pi_view = t1_all[:].bitcast(BF16)
        pi_b = [pi_view[:, i * DM:(i + 1) * DM] for i in range(4)]
        up_pi = pi_b
        dn_pi = pi_b
        sq_sl = [Ec(nc.sbuf_tensor(f"sq_sl{i}", [128, 512], BF16)) for i in range(2)]
        s_tile = Ec(nc.sbuf_tensor("s_tile", [128, 4 * DM], BF16))
        qb = Ec(nc.sbuf_tensor("qb", [128, 512], F32))
        negq = Ec(nc.sbuf_tensor("negq", [128, 512], F32))
        e_q = Ec(nc.sbuf_tensor("e_q", [128, 512], F32))
        r_tmp = Ec(nc.sbuf_tensor("r_tmp", [1, 512], F32))
        r_row = Ec(nc.sbuf_tensor("r_row", [1, 512], F32))
        rbc = Ec(nc.sbuf_tensor("rbc", [128, 512], F32))
        sg = [Ec(nc.sbuf_tensor(f"sg_{i}", [128, 512], F32)) for i in range(2)]
        xr = [Ec(nc.sbuf_tensor(f"xr_{i}", [128, 512], F32)) for i in range(3)]
        yt = [Ec(nc.sbuf_tensor(f"yt_{i}", [128, 512], F32)) for i in range(2)]
        PS = [Ec(nc.psum_tensor(f"psc{i}", [128, 512], F32)) for i in range(8)]
        self.C = dict(PS=PS, t1_all=t1_all, sg=sg, xr=xr, yt=yt, s_tile=s_tile)

        a = self.a_end
        for eng in Prog.ENGS:
            W(eng, S["PE"], a["PE"])
            W(eng, S["VE"], a["VE"])
            W(eng, S["SE"], a["SE"])
        W("sync", S["SW0"], a["SW0"])
        W("sync", S["SW1"], a["SW1"])
        for i in range(4):
            W("sync", S[f"XW{i}"], a["XW"][i])
        W("gpsimd", S["SW0"], a["SW0"])
        W("gpsimd", S["SW1"], a["SW1"])
        W("scalar", S["SW0"], a["SW0"])
        W("scalar", S["SW1"], a["SW1"])

        xn_re = xn_t[:].rearrange("p (s dc t) -> p s dc t", s=4, dc=DC)

        xn_cnt = [0] * NTC
        wq_cnt = [0, 0]
        wo_cnt = [0, 0]
        up_cnt = [0, 0]
        dn_cnt = [0, 0]
        self.q_pe = {}
        mix_pe = {}
        attn_ve = {}
        sq_ve = {}
        qevac_se = {}
        up_pe = {}
        up_sgn = {}
        dn_sgn = {}
        upw_done = {}
        dnw_done = {}
        s_pe = {}
        s_evac_se = {}
        up_evac_se = {}
        dn_jj_pe = {}
        self.dn_evac_ve = [0] * 8
        self.t1_free_se = [0] * 8
        self.sg_free = [0, 0]
        self.xr_free = [0] * 3
        self.yw_done = [0] * 2
        self.last_ph2_ve = 0
        self.last_stile_ve = 0
        self.pending_ph2 = None
        last_dn_pe = 0
        prev_lnr_se = [0]
        prev_rbc_se = [0]
        prev_bc_pe = [0]
        up_store_total = None
        dn_store_total = None
        q_pe = self.q_pe

        def emit_xn_load(tau):
            if tau >= 1:
                W("sync", S["PE"], up_pe[(tau - 1, HC - 1)])
            OP("sync", lambda e, t=tau: e.dma_start(
                xn_t[:].rearrange("p (s d) -> p s d", s=4),
                D["xnS"][t, :, :, :].rearrange("s p d -> p s d")), (S["XN0"], 16))
            xn_cnt[tau] = S["XN0"].v

        def emit_wq_load(tau, f):
            fsl = f % 2
            wq = S[f"WQ{fsl}"]
            prev = q_pe.get((tau, f - 2)) if f >= 2 else (
                q_pe.get((tau - 1, H - 2 + f)) if tau >= 1 else None)
            if prev:
                W("sync", S["PE"], prev)
            OP("sync", lambda e, d=wq_sl[fsl], f=f: e.dma_start(
                d[:], D["wqS"][f, :, :]), (wq, 16))
            wq_cnt[fsl] = wq.v

        for tau in range(NTC):
            if tau == 0:
                xn_cnt[0] = self.c_pre["xn0"]
                wq_cnt[0] = self.c_pre["wq0"]
                wq_cnt[1] = self.c_pre["wq1"]

            def emit_q(f):
                fsl = f % 2
                pq = PS[f % 2]
                W("tensor", S[f"WQ{fsl}"], wq_cnt[fsl])
                if f == 0:
                    W("tensor", S["XN0"], xn_cnt[tau])
                    if tau >= 1:
                        W("tensor", S["VE"], self.dn_evac_ve[0])
                if f == 1 and tau >= 1:
                    W("tensor", S["VE"], self.dn_evac_ve[1])
                pe_war = qevac_se.get((tau, f - 2)) if f >= 2 else (
                    qevac_se.get((tau - 1, H - 2 + f)) if tau >= 1 else None)
                if pe_war:
                    W("tensor", S["SE"], pe_war)
                for dc in range(DC):
                    OP("tensor", lambda e, dc=dc, w=wq_sl[fsl], p=pq: e.matmul(
                        p[:, :], w[:, dc * 128:(dc + 1) * 128],
                        xn_re[:, :, dc, :], start=(dc == 0), stop=(dc == DC - 1)),
                        *(((S["PE"], 1),) if dc == DC - 1 else ()))
                q_pe[(tau, f)] = S["PE"].v
                W("scalar", S["PE"], q_pe[(tau, f)])
                prevsq = sq_ve.get((tau, f - 1)) if f >= 1 else (
                    sq_ve.get((tau - 1, H - 1)) if tau >= 1 else None)
                if prevsq:
                    W("scalar", S["VE"], prevsq)
                OP("scalar", lambda e, p=pq, f=f: e.activation(
                    qb[:], p[:, :], AF.Identity, bias=P["bq_t"][:, f:f + 1]),
                    (S["SE"], 1))
                qevac_se[(tau, f)] = S["SE"].v
                W("vector", S["SE"], S["SE"].v)
                OP("vector", lambda e: e.tensor_scalar_min(negq[:], qb[:], 0.0),
                   (S["VE"], 1))
                W("scalar", S["VE"], S["VE"].v)
                OP("scalar", lambda e: e.activation(e_q[:], negq[:], AF.Exp),
                   (S["SE"], 1))
                W("vector", S["SE"], S["SE"].v)
                prevmix = mix_pe.get((tau, f - 2)) if f >= 2 else (
                    mix_pe.get((tau - 1, H - 2 + f)) if tau >= 1 else None)
                if prevmix:
                    W("vector", S["PE"], prevmix)
                OP("vector", lambda e, o=sq_sl[fsl]: e.scalar_tensor_tensor(
                    o[:], qb[:], 0.0, e_q[:], ALU.max, ALU.add), (S["VE"], 1))
                sq_ve[(tau, f)] = S["VE"].v

            def emit_attn_tail(f):
                fsl = f % 2
                kvh = f // G
                pm = PS[2 + (f % 2)]
                W("tensor", S["VE"], sq_ve[(tau, f)])
                if f < 2 and tau >= 1:
                    W("tensor", S["VE"], self.dn_evac_ve[2 + (f % 2)])
                prevattn = attn_ve.get((tau, f - 2)) if f >= 2 else (
                    attn_ve.get((tau - 1, H - 2 + f)) if tau >= 1 else None)
                if prevattn:
                    W("tensor", S["VE"], prevattn)
                OP("tensor", lambda e, kvh=kvh, q=sq_sl[fsl], p=pm: e.matmul(
                    p[:, :], P["kvb"][:, kvh * 128:(kvh + 1) * 128], q[:],
                    start=True, stop=True))
                if prev_lnr_se[0]:
                    W("tensor", S["SE"], prev_lnr_se[0])
                if f == 0 and tau >= 1:
                    W("tensor", S["VE"], self.dn_evac_ve[4])
                OP("tensor", lambda e, kvh=kvh, q=sq_sl[fsl]: e.matmul(
                    PS[4][0:1, :], P["ksb"][:, kvh:kvh + 1], q[:],
                    start=True, stop=True), (S["PE"], 1))
                mix_pe[(tau, f)] = S["PE"].v
                W("scalar", S["PE"], mix_pe[(tau, f)])
                if prev_bc_pe[0]:
                    W("scalar", S["PE"], prev_bc_pe[0])
                OP("scalar", lambda e: e.activation(r_tmp[:], PS[4][0:1, :],
                                                    AF.Ln, bias=P["ceps6"][:]),
                   (S["SE"], 1))
                W("scalar", S["SE"], S["SE"].v)
                OP("scalar", lambda e: e.activation(r_row[:], r_tmp[:], AF.Exp,
                                                    scale=-1.0), (S["SE"], 1))
                prev_lnr_se[0] = S["SE"].v
                W("tensor", S["SE"], prev_lnr_se[0])
                if prev_rbc_se[0]:
                    W("tensor", S["SE"], prev_rbc_se[0])
                if f == 0 and tau >= 1:
                    W("tensor", S["VE"], self.dn_evac_ve[5])
                OP("tensor", lambda e: e.matmul(PS[5][:, :], P["ones_r"][:],
                                                r_row[:], start=True, stop=True),
                   (S["PE"], 1))
                prev_bc_pe[0] = S["PE"].v
                W("scalar", S["PE"], prev_bc_pe[0])
                prevav = attn_ve.get((tau, f - 1)) if f >= 1 else (
                    attn_ve.get((tau - 1, H - 1)) if tau >= 1 else None)
                if prevav:
                    W("scalar", S["VE"], prevav)
                OP("scalar", lambda e: e.copy(rbc[:], PS[5][:, :]), (S["SE"], 1))
                prev_rbc_se[0] = S["SE"].v
                W("vector", S["SE"], prev_rbc_se[0])
                if f == 0 and tau >= 1:
                    W("vector", S["PE"], s_pe[(tau - 1, "end")])
                OP("vector", lambda e, f=f, p=pm: e.tensor_tensor(
                    attn_t[:, f * 512:(f + 1) * 512], p[:, :], rbc[:], ALU.mult),
                    (S["VE"], 1))
                attn_ve[(tau, f)] = S["VE"].v

            for f in range(H):
                if f == 2 and self.pending_ph2 is not None:
                    self.pending_ph2[0]()
                    self.pending_ph2b = self.pending_ph2[1]
                    self.pending_ph2 = None
                emit_q(f)
                if f + 2 < H:
                    emit_wq_load(tau, f + 2)
                if f >= 1:
                    emit_attn_tail(f - 1)
            emit_attn_tail(H - 1)
            attn_done_ve = S["VE"].v

            wo_done = {}

            def emit_wo_load(ntb):
                osl = ntb % 2
                wo = S[f"WO{osl}"]
                prevs = s_pe.get((tau, ntb - 2, 3)) if ntb >= 2 else (
                    s_pe.get((tau - 1, NTB - 2 + ntb, 3)) if tau >= 1 else None)
                if prevs:
                    W("gpsimd", S["PE"], prevs)
                OP("gpsimd", lambda e, d=wo_sl[osl], n=ntb: e.dma_start(
                    d[:], D["woS"][n, :, :]), (wo, 16))
                wo_done[ntb] = wo.v

            emit_wo_load(0)
            emit_wo_load(1)
            if getattr(self, "pending_ph2b", None) is not None:
                self.pending_ph2b()
                self.pending_ph2b = None
            for ntb in range(NTB):
                osl = ntb % 2
                W("tensor", S[f"WO{osl}"], wo_done[ntb])
                if ntb == 0:
                    W("tensor", S["VE"], attn_done_ve)
                    if tau >= 1:
                        W("tensor", S["VE"], self.dn_evac_ve[6])
                        W("tensor", S["VE"], self.dn_evac_ve[7])
                for sub in range(4):
                    gidx = ntb * 4 + sub
                    psl = PS[6 + (gidx % 2)]
                    war = s_evac_se.get((tau, gidx - 2)) if gidx >= 2 else (
                        s_evac_se.get((tau - 1, NTB * 4 - 2 + gidx)) if tau >= 1
                        else None)
                    if war:
                        W("tensor", S["SE"], war)
                    for h in range(H):
                        OP("tensor", lambda e, h=h, sub=sub, p=psl, w=wo_sl[osl]:
                           e.matmul(p[:, 0:256],
                                    attn_t[:, h * 512 + sub * 128:
                                           h * 512 + (sub + 1) * 128],
                                    w[:, h * 256:(h + 1) * 256],
                                    start=(h == 0), stop=(h == H - 1)),
                            *(((S["PE"], 1),) if h == H - 1 else ()))
                    s_pe[(tau, ntb, sub)] = S["PE"].v
                    W("scalar", S["PE"], S["PE"].v)
                    if gidx == 0 and tau >= 1:
                        W("scalar", S["VE"], self.last_stile_ve)
                    OP("scalar", lambda e, sub=sub, n=ntb, p=psl: e.copy(
                        s_tile[:, sub * DM + n * 256: sub * DM + (n + 1) * 256],
                        p[:, 0:256]), (S["SE"], 1))
                    s_evac_se[(tau, gidx)] = S["SE"].v
                if ntb + 2 < NTB:
                    emit_wo_load(ntb + 2)
            s_pe[(tau, "end")] = S["PE"].v

            if tau == 1:
                W("gpsimd", S["SW0"], S["SW0"].v)
                W("gpsimd", S["SW1"], S["SW1"].v)
            for j in range(HC):
                usl = j % 2
                up = S[f"UP{usl}"]
                pu = PS[j % 2]
                prev = up_pe.get((tau, j - 2)) if j >= 2 else (
                    up_pe.get((tau - 1, HC - 2 + j)) if tau >= 1 else None)
                if tau == 0:
                    p4 = j % 4
                    up = S[f"UP{p4}"]
                    if j >= 4:
                        sem, v = up_sgn[j - 4]
                        W("gpsimd", sem, v)
                    OP("gpsimd", lambda e, d=up_pi[p4], j=j: e.dma_start(
                        d[:], D["up_pre"][j, :, :]), (up, 16))
                    ld_done = up.v
                    eng = "scalar" if (j % 2 == 0) else "vector"
                    W(eng, up, ld_done)
                    if prev:
                        W(eng, S["PE"], prev)
                    if j >= 2:
                        W(eng, S[f"SW{usl}"], upw_done[j - 2])
                    if eng == "scalar":
                        OP("scalar", lambda e, ap=up_sl[usl], b=up_pi[p4]:
                           e.sign(ap[:], b[:]), (S["SE"], 1))
                        up_sgn[j] = (S["SE"], S["SE"].v)
                    else:
                        OP("vector", lambda e, ap=up_sl[usl], b=up_pi[p4]:
                           e.tensor_scalar(ap[:], b[:], 0.0, None, ALU.is_gt),
                           (S["VE"], 1))
                        W("vector", S["VE"], S["VE"].v)
                        OP("vector", lambda e, ap=up_sl[usl]:
                           e.tensor_scalar(ap[:], ap[:], 2.0, -1.0, ALU.mult,
                                           ALU.add), (S["VE"], 1))
                        up_sgn[j] = (S["VE"], S["VE"].v)
                    sem, v = up_sgn[j]
                    W("scalar", sem, v)
                    OP("scalar", lambda e, j=j, s=up_sl[usl]: e.dma_start(
                        D["upS"][j, :, :], s[:]), (S[f"SW{usl}"], 16))
                    upw_done[j] = S[f"SW{usl}"].v
                    W("tensor", sem, v)
                else:
                    if prev:
                        W("gpsimd", S["PE"], prev)
                    OP("gpsimd", lambda e, d=up_sl[usl], j=j: e.dma_start(
                        d[:], D["upS"][j, :, :]), (up, 16))
                    up_cnt[usl] = up.v
                    W("tensor", up, up_cnt[usl])
                if j < 2:
                    W("tensor", S["SE"], qevac_se[(tau, H - 2 + j)])
                    if tau >= 1:
                        W("tensor", S["PE"], last_dn_pe)
                else:
                    W("tensor", S["SE"], up_evac_se[(tau, j - 2)])
                for dc in range(DC):
                    OP("tensor", lambda e, dc=dc, w=up_sl[usl], p=pu: e.matmul(
                        p[:, :], w[:, dc * 128:(dc + 1) * 128],
                        xn_re[:, :, dc, :], start=(dc == 0), stop=(dc == DC - 1)),
                        *(((S["PE"], 1),) if dc == DC - 1 else ()))
                up_pe[(tau, j)] = S["PE"].v
                W("scalar", S["PE"], S["PE"].v)
                OP("scalar", lambda e, j=j, p=pu: e.activation(
                    hidden[:, j * 512:(j + 1) * 512], p[:, :], AF.Relu,
                    scale=P["usc"][:, j:j + 1]), (S["SE"], 1))
                up_evac_se[(tau, j)] = S["SE"].v
            up_evac_last = S["SE"].v
            if tau == 0:
                up_store_total = (S["SW0"].v, S["SW1"].v)

            if tau == 0:
                for j in range(HC):
                    dsl = j % 2
                    p4 = j % 4
                    dnm = S[f"DN{p4}"]
                    if j >= 4:
                        sem, v = dn_sgn[j - 4]
                        W("gpsimd", sem, v)
                    else:
                        sem, v = up_sgn[HC - 4 + j]
                        W("gpsimd", sem, v)
                    OP("gpsimd", lambda e, d=dn_pi[p4], j=j: e.dma_start(
                        d[:], D["dn_pre"][j, :, :]), (dnm, 16))
                    ld_done = dnm.v
                    eng = "scalar" if (j % 2 == 0) else "vector"
                    W(eng, dnm, ld_done)
                    if j >= 2:
                        W(eng, S["PE"], dn_jj_pe[(0, 0, j - 2)])
                        W(eng, S[f"SW{dsl}"], dnw_done[j - 2])
                    if eng == "scalar":
                        OP("scalar", lambda e, ap=dn_sl[dsl], b=dn_pi[p4]:
                           e.sign(ap[:], b[:]), (S["SE"], 1))
                        dn_sgn[j] = (S["SE"], S["SE"].v)
                    else:
                        OP("vector", lambda e, ap=dn_sl[dsl], b=dn_pi[p4]:
                           e.tensor_scalar(ap[:], b[:], 0.0, None, ALU.is_gt),
                           (S["VE"], 1))
                        W("vector", S["VE"], S["VE"].v)
                        OP("vector", lambda e, ap=dn_sl[dsl]:
                           e.tensor_scalar(ap[:], ap[:], 2.0, -1.0, ALU.mult,
                                           ALU.add), (S["VE"], 1))
                        dn_sgn[j] = (S["VE"], S["VE"].v)
                    sem, v = dn_sgn[j]
                    W("scalar", sem, v)
                    OP("scalar", lambda e, j=j, s=dn_sl[dsl]: e.dma_start(
                        D["dnS"][j, :, :], s[:]), (S[f"SW{dsl}"], 16))
                    dnw_done[j] = S[f"SW{dsl}"].v
                    W("tensor", sem, v)
                    W("tensor", S["SE"], up_evac_se[(0, j)])
                    for sub in range(4):
                        for ntd in range(NHREG):
                            bank = sub * NHREG + ntd
                            islast = (sub == 3 and ntd == NHREG - 1)
                            OP("tensor", lambda e, j=j, sub=sub, ntd=ntd, b=bank,
                               d=dn_sl[dsl]: e.matmul(
                                PS[b][:, :],
                                hidden[:, j * 512 + sub * 128:
                                       j * 512 + (sub + 1) * 128],
                                d[:, ntd * 512:(ntd + 1) * 512],
                                start=(j == 0), stop=(j == HC - 1)),
                                *(((S["PE"], 1),) if islast else ()))
                    dn_jj_pe[(0, 0, j)] = S["PE"].v
                dn_done = S["PE"].v
                last_dn_pe = dn_done
                dn_store_total = (S["SW0"].v, S["SW1"].v)
                emit_xn_load(1)
                emit_wq_load(1, 0)
                emit_wq_load(1, 1)
                pha, phb = self._dn_evac(0, 0, dn_done)
                pha()
                phb()
                for jj in range(HC // 2):
                    dsl = jj % 2
                    dnm = S[f"DNP{dsl}"]
                    if jj == 0:
                        W("sync", S["SW0"], dn_store_total[0])
                        W("sync", S["SW1"], dn_store_total[1])
                    war = (dn_jj_pe.get((0, 1, jj - 2)) if jj >= 2
                           else dn_jj_pe[(0, 0, HC - 2 + jj)])
                    if war:
                        W("sync", S["PE"], war)
                    OP("sync", lambda e, d=dn_sl[dsl], jj=jj: e.dma_start(
                        d[:].rearrange("p (j c) -> p j c", j=2),
                        D["dnS"][2 * jj:2 * jj + 2, :, DM // 2:DM].rearrange(
                            "j p c -> p j c")), (dnm, 16))
                    dn_cnt[dsl] = dnm.v
                    W("tensor", dnm, dn_cnt[dsl])
                    if jj == 0:
                        for b in range(8):
                            W("tensor", S["VE"], self.dn_evac_ve[b])
                    for j2 in range(2):
                        j = 2 * jj + j2
                        for sub in range(4):
                            for ntd in range(NHREG):
                                bank = sub * NHREG + ntd
                                islast = (j2 == 1 and sub == 3
                                          and ntd == NHREG - 1)
                                OP("tensor", lambda e, j=j, j2=j2, sub=sub,
                                   ntd=ntd, b=bank, d=dn_sl[dsl]: e.matmul(
                                    PS[b][:, :],
                                    hidden[:, j * 512 + sub * 128:
                                           j * 512 + (sub + 1) * 128],
                                    d[:, j2 * 1024 + ntd * 512:
                                      j2 * 1024 + (ntd + 1) * 512],
                                    start=(j == 0), stop=(j == HC - 1)),
                                    *(((S["PE"], 1),) if islast else ()))
                    dn_jj_pe[(0, 1, jj)] = S["PE"].v
                dn_done = S["PE"].v
                last_dn_pe = dn_done
                self.pending_ph2 = self._dn_evac(0, 1, dn_done)
            else:
                for dhalf in range(2):
                    for jj in range(HC // 2):
                        dsl = jj % 2
                        dnm = S[f"DNP{dsl}"]
                        war = dn_jj_pe.get((tau, dhalf, jj - 2)) if jj >= 2 else (
                            dn_jj_pe.get((tau, 0, HC // 2 - 2 + jj))
                            if dhalf == 1 else
                            (dn_jj_pe.get((tau - 1, 1, HC // 2 - 2 + jj))
                             if tau >= 2 else
                             dn_jj_pe.get((0, 1, HC // 2 - 2 + jj))))
                        if war:
                            W("sync", S["PE"], war)
                        OP("sync", lambda e, d=dn_sl[dsl], dh=dhalf, jj=jj:
                           e.dma_start(
                            d[:].rearrange("p (j c) -> p j c", j=2),
                            D["dnS"][2 * jj:2 * jj + 2, :,
                                     dh * (DM // 2):(dh + 1) * (DM // 2)]
                            .rearrange("j p c -> p j c")), (dnm, 16))
                        dn_cnt[dsl] = dnm.v
                        W("tensor", dnm, dn_cnt[dsl])
                        if jj == 0:
                            W("tensor", S["SE"], up_evac_last)
                            for b in range(8):
                                if self.dn_evac_ve[b]:
                                    W("tensor", S["VE"], self.dn_evac_ve[b])
                        for j2 in range(2):
                            j = 2 * jj + j2
                            for sub in range(4):
                                for ntd in range(NHREG):
                                    bank = sub * NHREG + ntd
                                    islast = (j2 == 1 and sub == 3
                                              and ntd == NHREG - 1)
                                    OP("tensor", lambda e, j=j, j2=j2, sub=sub,
                                       ntd=ntd, b=bank, d=dn_sl[dsl]: e.matmul(
                                        PS[b][:, :],
                                        hidden[:, j * 512 + sub * 128:
                                               j * 512 + (sub + 1) * 128],
                                        d[:, j2 * 1024 + ntd * 512:
                                          j2 * 1024 + (ntd + 1) * 512],
                                        start=(j == 0), stop=(j == HC - 1)),
                                        *(((S["PE"], 1),) if islast else ()))
                        dn_jj_pe[(tau, dhalf, jj)] = S["PE"].v
                    dn_done = S["PE"].v
                    last_dn_pe = dn_done
                    if dhalf == 0 and tau + 1 < NTC:
                        emit_xn_load(tau + 1)
                        emit_wq_load(tau + 1, 0)
                        emit_wq_load(tau + 1, 1)
                    pha, phb = self._dn_evac(tau, dhalf, dn_done)
                    if dhalf == 0:
                        pha()
                        phb()
                    else:
                        self.pending_ph2 = (pha, phb)
        if self.pending_ph2 is not None:
            self.pending_ph2[0]()
            self.pending_ph2[1]()
            self.pending_ph2 = None
        W("sync", S["YW0"], S["YW0"].v)
        W("sync", S["YW1"], S["YW1"].v)
        W("gpsimd", S["YW0"], S["YW0"].v)
        W("gpsimd", S["YW1"], S["YW1"].v)

    def _dn_evac(self, tau, dhalf, dn_done):
        S, P, D = self.Sm, self.P, self.D
        W, OP = self.W, self.OP
        DM, NHREG = self.DM, self.NHREG
        C = self.C
        PS, t1_all, sg, xr, yt, s_tile = (C["PS"], C["t1_all"], C["sg"],
                                          C["xr"], C["yt"], C["s_tile"])
        ev_order = [0, 2, 4, 5, 1, 3, 6, 7]
        NXR = len(xr)
        inst = []
        for i, bank in enumerate(ev_order):
            sub, ntd = bank // NHREG, bank % NHREG
            inst.append(dict(bank=bank, sub=sub, slot=i % NXR,
                             dcol=dhalf * (DM // 2) + ntd * 512,
                             row0=tau * 512 + sub * 128, xr_done=0))

        def emit_xr_load(i):
            r = inst[i]
            if self.xr_free[r["slot"]]:
                W("scalar", S["VE"], self.xr_free[r["slot"]])
            OP("scalar", lambda e, r0=r["row0"], c=r["dcol"], sl=r["slot"]:
               e.dma_start(xr[sl][:], D["x_own"][r0:r0 + 128, c:c + 512]),
               (S[f"XR{r['slot']}"], 16))
            r["xr_done"] = S[f"XR{r['slot']}"].v

        for i in range(NXR):
            emit_xr_load(i)
        W("vector", S["PE"], dn_done)
        if self.last_ph2_ve:
            W("vector", S["VE"], self.last_ph2_ve)
        for r in inst:
            b = r["bank"]
            if self.t1_free_se[b]:
                W("vector", S["SE"], self.t1_free_se[b])
            OP("vector", lambda e, b=b, dcol=r["dcol"]:
               e.tensor_tensor(t1_all[:, b * 512:(b + 1) * 512], PS[b][:, :],
                               P["dsb"][:, dcol:dcol + 512], ALU.mult),
               (S["VE"], 1))
            self.dn_evac_ve[b] = S["VE"].v

        def phase2a():
            for i, r in enumerate(inst):
                b, par = r["bank"], i % 2
                W("scalar", S["VE"], self.dn_evac_ve[b])
                if self.sg_free[par]:
                    W("scalar", S["VE"], self.sg_free[par])
                OP("scalar", lambda e, b=b, par=par: e.activation(
                    sg[par][:], t1_all[:, b * 512:(b + 1) * 512], AF.Silu),
                    (S["SE"], 1))
                self.t1_free_se[b] = S["SE"].v
                W("vector", S["SE"], S["SE"].v)
                OP("vector", lambda e, b=b, par=par, sub=r["sub"], dcol=r["dcol"]:
                   e.tensor_tensor(t1_all[:, b * 512:(b + 1) * 512], sg[par][:],
                                   s_tile[:, sub * DM + dcol:
                                          sub * DM + dcol + 512], ALU.mult),
                   (S["VE"], 1))
                self.sg_free[par] = S["VE"].v
                self.last_stile_ve = S["VE"].v

        def phase2b():
            for i, r in enumerate(inst):
                b, p4 = r["bank"], i % 2
                W("vector", S[f"XR{r['slot']}"], r["xr_done"])
                if self.yw_done[p4]:
                    W("vector", S[f"YW{p4}"], self.yw_done[p4])
                OP("vector", lambda e, b=b, p4=p4, sl=r["slot"]: e.tensor_tensor(
                    yt[p4][:], t1_all[:, b * 512:(b + 1) * 512], xr[sl][:],
                    ALU.add), (S["VE"], 1))
                self.xr_free[r["slot"]] = S["VE"].v
                self.last_ph2_ve = S["VE"].v
                W("gpsimd", S["VE"], S["VE"].v)
                OP("gpsimd", lambda e, r0=r["row0"], c=r["dcol"], p4=p4:
                   e.dma_start(D["y"][r0:r0 + 128, c:c + 512], yt[p4][:]),
                   (S[f"YW{p4}"], 16))
                self.yw_done[p4] = S[f"YW{p4}"].v
                if i + NXR < 8:
                    emit_xr_load(i + NXR)
        return phase2a, phase2b


def _host_layout(inputs, cfg):
    B, Sq, DM, H, KVH, DH = (cfg["B"], cfg["S"], cfg["DM"], cfg["H"],
                             cfg["KVH"], cfg["DH"])
    KD = KVH * 128
    HC = DH // 128
    DC = DM // 128
    NTB = DM // 256
    x = np.asarray(inputs["L0_attn_hidden_states"])
    Wq = np.asarray(inputs["L0_attn_Wq"])
    Wo = np.asarray(inputs["L0_attn_Wo"])
    up = np.asarray(inputs["L0_ffn_up_weight"])
    dn = np.asarray(inputs["L0_ffn_down_weight"])
    shared = dict(
        wq_pre=np.ascontiguousarray(
            Wq.reshape(H, 128, DC, 128).transpose(0, 3, 2, 1)
            .reshape(H, 128, DM)),
        wo_pre=np.ascontiguousarray(
            Wo.reshape(NTB, 256, H, 128).transpose(0, 3, 2, 1)
            .reshape(NTB, 128, H * 256)),
        up_pre=np.ascontiguousarray(
            up.reshape(HC, 128, DC, 128).transpose(0, 3, 2, 1)
            .reshape(HC, 128, DM)),
        dn_pre=np.ascontiguousarray(dn.T).reshape(HC, 128, DM),
        wk_cm=np.ascontiguousarray(
            np.asarray(inputs["L0_attn_Wk"]).T.reshape(DC, 128, KD)
            .transpose(1, 0, 2).reshape(128, DC * KD)),
        wv_cm=np.ascontiguousarray(
            np.asarray(inputs["L0_attn_Wv"]).T.reshape(DC, 128, KD)
            .transpose(1, 0, 2).reshape(128, DC * KD)),
        bq_t=np.ascontiguousarray(np.asarray(inputs["L0_attn_bq"]).reshape(H, 128).T),
        bk_r=np.ascontiguousarray(np.asarray(inputs["L0_attn_bk"]).reshape(1, KD)),
        bv_r=np.ascontiguousarray(np.asarray(inputs["L0_attn_bv"]).reshape(1, KD)),
        us_t=np.ascontiguousarray(
            np.asarray(inputs["L0_ffn_up_scale"]).reshape(HC, 128).T),
        ds_r=np.ascontiguousarray(
            np.asarray(inputs["L0_ffn_down_scale"]).reshape(1, DM)),
        os_r=np.ascontiguousarray(
            np.asarray(inputs["L0_ffn_output_scale"]).reshape(1, 1)),
    )
    T = Sq // 2
    in_maps = []
    for c in range(2 * B):
        b, h = c // 2, c % 2
        m = dict(shared)
        m["x_own"] = np.ascontiguousarray(x[b, h * T:(h + 1) * T])
        m["x_oth"] = np.ascontiguousarray(x[b, (1 - h) * T:(2 - h) * T])
        in_maps.append(m)
    return in_maps


def kernel(**inputs):
    cfg = REAL_CFG
    in_maps = _host_layout(inputs, cfg)
    bld = Builder(cfg)
    nc = bld.build()
    res = run_bass_kernel_spmd(nc, in_maps, core_ids=list(range(2 * cfg["B"])))
    B, Sq, DM = cfg["B"], cfg["S"], cfg["DM"]
    T = Sq // 2
    y = np.empty((B, Sq, DM), dtype=np.float32)
    for c in range(2 * B):
        b, h = c // 2, c % 2
        y[b, h * T:(h + 1) * T] = res.results[c]["y"]
    return y
